# version 01
# baseline (speedup 1.0000x reference)
import numpy as np
import ml_dtypes
import concourse.bass as bass
import concourse.mybir as mybir
from concourse.bass_utils import run_bass_kernel_spmd
from concourse.alu_op_type import AluOpType as AO
from contextlib import ExitStack

F32 = mybir.dt.float32
BF16 = mybir.dt.bfloat16
I8 = mybir.dt.int8
AF = mybir.ActivationFunctionType
AX = mybir.AxisListType.X

T, D, H, HD, D3, INNER = 2048, 1024, 16, 64, 3072, 4096
NT = T // 128
NKK = D // 128
NEG = -1e9
ENGINES = ("sync", "vector", "scalar", "tensor", "gpsimd")


class Chan:
    __slots__ = ("sem", "unit", "n")

    def __init__(self, sem, unit):
        self.sem, self.unit, self.n = sem, unit, 0

    def at(self, i):
        assert 0 <= i < self.n, f"wait on future push ({i} >= {self.n})"
        return (self.sem, (i + 1) * self.unit)

    def now(self):
        return (self.sem, self.n * self.unit) if self.n else None

    def last(self):
        return self.at(self.n - 1)


class Prog:

    CHAINED = ("vector", "scalar", "gpsimd")

    def __init__(self, nc, es, serial=False):
        self.nc, self.es = nc, es
        self.ops = {e: [] for e in ENGINES}
        self.serial = serial
        self.prev = None
        self.eprev = {e: None for e in ENGINES}
        self.fill = {}
        for e in self.CHAINED:
            sem = self.es.enter_context(nc.semaphore(name=f"fill_{e}"))
            self.fill[e] = Chan(sem, 1)
        if serial:
            self.sfill = Chan(self.es.enter_context(nc.semaphore(name="sfill")), 1)

    def chan(self, unit=1):
        sem = self.es.enter_context(self.nc.semaphore())
        return Chan(sem, unit)

    def op(self, eng, fn, waits=(), inc=None):
        if inc is None and (self.serial or eng in self.CHAINED):
            inc = self.sfill if self.serial else self.fill[eng]
        idx = None
        tgt = None
        if inc is not None:
            idx = inc.n
            inc.n += 1
            tgt = (inc.sem, inc.unit)
        ws = tuple(w for w in waits if w is not None)
        if self.serial:
            if self.prev is not None:
                ws = ws + (self.prev,)
            self.prev = (inc.sem, inc.n * inc.unit)
        elif eng in self.CHAINED:
            if self.eprev[eng] is not None:
                ws = ws + (self.eprev[eng],)
            self.eprev[eng] = (inc.sem, inc.n * inc.unit)
        self.ops[eng].append((fn, ws, tgt))
        return idx

    def realize(self, block, final_waits):
        def mk(name):
            ops = self.ops[name]

            def body(eng):
                guaranteed = {}
                for fn, waits, tgt in ops:
                    for (sem, val) in waits:
                        k = id(sem)
                        if guaranteed.get(k, -1) >= val:
                            continue
                        eng.wait_ge(sem, val)
                        guaranteed[k] = val
                    ins = fn(eng)
                    if tgt is not None:
                        ins.then_inc(tgt[0], tgt[1])
                if name == "sync":
                    for w in final_waits:
                        if w is not None:
                            eng.wait_ge(w[0], w[1])
            return body

        block.sync(mk("sync"))
        block.vector(mk("vector"))
        block.scalar(mk("scalar"))
        block.tensor(mk("tensor"))
        block.gpsimd(mk("gpsimd"))


class PBank:
    def __init__(self, ap):
        self.ap = ap
        self.last = []

    def acquire(self):
        w = self.last
        self.last = []
        return list(w)

    def consumed(self, chan, idx):
        self.last.append(chan.at(idx))


def host_consts():
    j = np.arange(0, HD, 2, dtype=np.float64)
    inv_freq = 1.0 / (10000.0 ** (j / HD))
    c = np.arange(32)
    gidx = (2 * c) % 32
    pos = np.arange(T, dtype=np.float64)
    theta = pos[:, None] * inv_freq[gidx][None, :]
    cos, sin = np.cos(theta), np.sin(theta)
    Cfull = np.zeros((T, D), dtype=np.float64)
    S2full = np.zeros((T, D), dtype=np.float64)
    for h in range(H):
        b = h * HD
        Cfull[:, b + 0:b + HD:2] = cos
        Cfull[:, b + 1:b + HD:2] = cos
        S2full[:, b + 0:b + HD:2] = -sin
        S2full[:, b + 1:b + HD:2] = sin
    ctab = np.concatenate([Cfull, S2full], axis=1).astype(np.float32)

    kp = np.arange(128)[:, None]
    qf = np.arange(128)[None, :]
    m = np.zeros((128, 384), dtype=np.float32)
    m[:, 0:128] = (kp <= qf)
    m[:, 128:256] = 1.0
    m[:, 256:384] = (qf <= kp)
    mask_o = m.astype(ml_dtypes.bfloat16)
    me = m.copy()
    me[0, :] = 0.0
    mask_e = me.astype(ml_dtypes.bfloat16)
    identm8 = np.eye(128).astype(np.float32)
    return ctab, mask_e, mask_o, identm8


def build(upto="full", debug=False, serial=False):
    nc = bass.Bass("TRN2", target_bir_lowering=False)
    es = ExitStack()
    stage = {"p1": 1, "p2": 2, "p3": 3, "p4": 4, "full": 6}[upto]

    dn = lambda n, s, dt, k: nc.dram_tensor(n, s, dt, kind=k).ap()
    x_h = dn("x", [T, D], F32, "ExternalInput")
    w_in = dn("w_in", [D3, D], F32, "ExternalInput")
    b_in = dn("b_in", [1, D3], F32, "ExternalInput")
    w_o = dn("w_o", [D, D], F32, "ExternalInput")
    b_o = dn("b_o", [1, D], F32, "ExternalInput")
    w_g = dn("w_g", [INNER, D], F32, "ExternalInput")
    w_u = dn("w_u", [INNER, D], F32, "ExternalInput")
    w_d = dn("w_d", [D, INNER], F32, "ExternalInput")
    ctab_h = dn("ctab", [T, 2 * D], F32, "ExternalInput")
    mask_e_h = dn("mask_e", [128, 384], BF16, "ExternalInput")
    mask_o_h = dn("mask_o", [128, 384], BF16, "ExternalInput")
    ident_h = dn("identm8", [128, 128], F32, "ExternalInput")
    y_h = dn("y", [T, D], F32, "ExternalOutput")

    sk = "ExternalOutput" if debug else "Internal"
    s_win = dn("s_win", [D3, D], BF16, sk)
    s_wo = dn("s_wo", [D, D], BF16, sk)
    s_wg = dn("s_wg", [INNER, D], BF16, sk)
    s_wu = dn("s_wu", [INNER, D], BF16, sk)
    s_wd = dn("s_wd", [D, INNER], BF16, sk)
    s_xr = dn("s_xr", [T, D], BF16, sk)
    s_xnq = dn("s_xnq", [T, D], BF16, sk)
    s_m = dn("s_m", [H, T], F32, sk)
    s_gu = dn("s_gu", [T, INNER], BF16, sk)
    s_guq = dn("s_guq", [T, INNER], BF16, sk)
    s_acc = dn("s_acc", [T, D], F32, sk)
    d_qfm = dn("d_qfm", [128, 16384], BF16, sk) if debug else None
    d_kfm = dn("d_kfm", [128, 16384], BF16, sk) if debug else None
    d_vtm = dn("d_vtm", [128, 16640], BF16, sk) if debug else None
    d_ofm = dn("d_ofm", [128, 16384], BF16, sk) if debug else None
    d_tc = dn("d_tc", [128, 96], F32, sk) if debug else None
    d_xn = dn("d_xn", [T, D], F32, sk) if debug else None
    d_pg = dn("d_pg", [8, 2048], BF16, sk) if debug else None
    d_vg = dn("d_vg", [8, 1040], BF16, sk) if debug else None
    d_m8 = dn("d_m8", [1, 2048], BF16, sk) if debug else None
    d_g = dn("d_g", [T, INNER], F32, sk) if debug else None
    d_u = dn("d_u", [T, INNER], F32, sk) if debug else None
    d_ws = dn("d_ws", [128, 12288], BF16, sk) if debug else None
    d_sc = dn("d_sc", [128, 32], F32, sk) if debug else None
    d_scall = dn("d_scall", [16, 16], F32, sk) if debug else None

    def sbuf(name, shape, dt):
        return es.enter_context(nc.sbuf_tensor(name, shape, dt)).ap()

    x_t = [sbuf(f"xt{i}", [128, 1024], F32) for i in range(2)]
    xn_t = [sbuf(f"xnt{i}", [128, 1024], F32) for i in range(2)]
    ctab_t = sbuf("ctab_t", [128, 1024], F32)
    d_t = sbuf("d_t", [128, 1024], F32)
    c1_t = sbuf("c1_t", [128, 2048], BF16)
    m2_t = sbuf("m2_t", [128, 2048], BF16)
    wprep_t = sbuf("wprep_t", [128, 1024], F32)
    gub2_t = sbuf("gub2_t", [128, 1024], F32)
    wq_t = sbuf("wq_t", [128, 1024], BF16)
    b_bias = [sbuf(f"bb{i}", [1, 512], F32) for i in range(2)]
    ident_s = sbuf("ident_s", [128, 128], F32)
    ones_f = sbuf("ones_f", [128, 128], F32)
    ones_b = sbuf("ones_b", [128, 128], BF16)
    tcol = sbuf("tcol", [128, 96], F32)
    scols = [sbuf(f"scol{i}", [128, 16], F32) for i in range(2)]
    scal = tcol[0:1, 70:79]
    thr_col = sbuf("thr_col", [128, 12], F32)
    b_cols = sbuf("b_cols", [128, 16], F32)
    mcols = [b_cols, scols[1]]
    y8_t = sbuf("y8_t", [128, 1024], I8)
    q_fm = sbuf("q_fm", [128, 16384], BF16)
    k_fm = sbuf("k_fm", [128, 16384], BF16)
    v_tm = sbuf("v_tm", [128, 16640], BF16)
    big2 = sbuf("big2", [128, 16384], BF16)
    wstr = sbuf("wstr", [128, 12288], BF16)
    b7 = sbuf("b7", [128, 3072], BF16)
    v_g = wstr[0:8, 0:1040]
    mask_os = b7[:, 0:384]
    mask_es = mask_os
    pt_sb = [wq_t[:, 0:384], wq_t[:, 384:768]]
    guev = [ctab_t[:, 0:512], ctab_t[:, 512:1024], d_t[:, 0:512], d_t[:, 512:1024]]
    mtmp = d_t
    l_rows = x_t
    gu_big = [wprep_t, gub2_t]

    pb = [es.enter_context(nc.psum_tensor(f"pb{i}", [128, 512], F32)).ap() for i in range(8)]

    prog = Prog(nc, es, serial=serial)
    op = prog.op
    banks = [PBank(p) for p in pb]

    def MM(o, l, r, s, p):
        return lambda e, o=o, l=l, r=r, s=s, p=p: e.matmul(
            o, l, r, start=s, stop=p, skip_group_check=True)

    def mm_group(bi, mms, waits, inc):
        bw = banks[bi].acquire()
        n = len(mms)
        idx = None
        for i2, (o, l, r) in enumerate(mms):
            idx = op("tensor", MM(o, l, r, i2 == 0, i2 == n - 1),
                     waits=(list(waits) + bw) if i2 == 0 else (),
                     inc=inc if i2 == n - 1 else None)
        return idx

    ch = lambda u=1: prog.chan(u)
    c_init, c_mset, c_mskld = ch(16), ch(), ch(16)
    c_bias, c_bgrp, c_bcol = ch(16), ch(), ch()
    c_x, c_ct = ch(16), ch(16)
    c_v1, c_xsq, c_ssq, c_var, c_rstd, c_xn, c_sq2 = ch(), ch(), ch(), ch(), ch(), ch(), ch()
    c_sx, c_deqx, c_y8, c_dq, c_c1, c_m2, c_xnq = ch(), ch(), ch(), ch(), ch(), ch(), ch()
    c_xnqw, c_t1, c_t2, c_xr, c_xrw, c_xrfm = ch(16), ch(), ch(), ch(), ch(16), ch(16)
    c_wld, c_wcast, c_wabs, c_wsum, c_wmm, c_thr = ch(16), ch(16), ch(), ch(), ch(), ch()
    c_wld2, c_wc1, c_wq, c_wqw = ch(16), ch(), ch(), ch(16)
    c_wqk, c_mm2a, c_qke = ch(16), ch(), ch()
    c_wv, c_mm2b, c_ve, c_mmvg, c_vg = ch(16), ch(), ch(), ch(), ch()
    c_mmax, c_mred, c_trm, c_mtr, c_mwr, c_m8 = ch(), ch(), ch(), ch(), ch(16), ch(16)
    c_mmg, c_pg, c_mmS, c_ms, c_expS, c_mask = ch(), ch(), ch(), ch(), ch(), ch()
    c_pv, c_oevac, c_lcopy, c_recip, c_bcmm, c_bce, c_onorm = ch(), ch(), ch(), ch(), ch(), ch(), ch()
    c_wot, c_x2, c_mm4, c_acce, c_accw = ch(16), ch(16), ch(), ch(), ch(16)
    c_xnqfm, c_wgu, c_wguu, c_mmg5, c_ge, c_mmu5, c_ue = ch(16), ch(16), ch(16), ch(), ch(), ch(), ch()
    c_gum, c_rmax, c_guw = ch(), ch(), ch(16)
    c_gurd, c_sgu, c_qt, c_y8b, c_d2, c_c1b, c_m2b, c_guq, c_guqw = \
        ch(16), ch(), ch(), ch(), ch(), ch(), ch(), ch(), ch(16)
    c_guqfm, c_wdt, c_dnk, c_dnmm, c_acr, c_fin, c_out = ch(16), ch(16), ch(), ch(), ch(16), ch(), ch(16)
    c_deqg = ch()
    c_dbg = ch(16)

    op("sync", lambda e: e.dma_start(out=ident_s, in_=ident_h), inc=c_init)
    op("vector", lambda e: e.memset(ones_f, 1.0), inc=c_mset)
    op("vector", lambda e: e.memset(ones_b, 1.0), inc=c_mset)
    op("vector", lambda e: e.memset(ones_b[32:33, :], -8.0))
    op("vector", lambda e: e.memset(ones_b[64:65, :], -8.0))
    op("vector", lambda e: e.memset(
        v_tm.rearrange("p (t c) -> p t c", c=65)[:, :, 64:65], 1.0), inc=c_mset)
    op("vector", lambda e: e.memset(tcol[:, 69:70], 1e-5), inc=c_mset)

    bw6 = banks[6].acquire()
    for q4 in range(4):
        bb = b_bias[q4 % 2]
        lw = [c_bgrp.at(q4 - 2)] if q4 >= 2 else []
        op("sync", lambda e, b2=bb, qq=q4: e.dma_start(
            out=b2, in_=b_in[:, qq * 512:(qq + 1) * 512]), waits=lw, inc=c_bias)
        for m in range(4):
            op("tensor", MM(pb[6][:, q4 * 4 + m:q4 * 4 + m + 1],
                            bb[0:1, m * 128:(m + 1) * 128],
                            ones_f[0:1, 0:1], q4 == 0 and m == 0, False),
               waits=[c_bias.at(q4), c_mset.at(0)] + (bw6 if (q4 == 0 and m == 0) else []),
               inc=(c_bgrp if m == 3 else None))
    i = op("scalar", lambda e: e.copy(b_cols, pb[6][:, 0:16]), waits=[c_bgrp.at(3)], inc=c_bcol)
    banks[6].consumed(c_bcol, i)

    xr_slot = b7[:, 0:1024]
    xnq_slot = b7[:, 1024:2048]
    t2_slot = b7[:, 2048:3072]
    for i in range(NT):
        b = i % 2
        sc = scols[b]
        xw = [c_xn.at(i - 2), c_ssq.at(i - 2)] if i >= 2 else []
        op("sync", lambda e, ii=i, bb=b: e.dma_start(
            out=x_t[bb], in_=x_h[ii * 128:(ii + 1) * 128, :]), waits=xw, inc=c_x)
        cw = [c_t2.at(i - 1)] if i >= 1 else []
        op("sync", lambda e, ii=i: e.dma_start(
            out=ctab_t, in_=ctab_h[ii * 128:(ii + 1) * 128, 0:1024]), waits=cw, inc=c_ct)
        sqw = [c_xr.at(i - 1)] if i >= 1 else []
        vw = [c_y8.at(i - 2)] if i >= 2 else []
        op("scalar", lambda e, bb=b: e.activation(d_t, x_t[bb], AF.Square),
           waits=[c_x.at(i)] + sqw + vw, inc=c_sq2)
        op("vector", lambda e, bb=b, s=sc: e.reduce_sum(s[:, 0:1], x_t[bb], axis=AX),
           waits=[c_x.at(i)] + vw, inc=c_v1)
        op("vector", lambda e, s=sc: e.reduce_sum(s[:, 1:2], d_t, axis=AX),
           waits=[c_sq2.at(i)], inc=c_ssq)
        op("vector", lambda e, s=sc: e.tensor_scalar(s[:, 2:3], s[:, 0:1], 1.0 / D, None, AO.mult),
           waits=[c_v1.at(i)])
        op("vector", lambda e, s=sc: e.tensor_tensor(s[:, 3:4], s[:, 2:3], s[:, 2:3], AO.mult))
        op("vector", lambda e, s=sc: e.scalar_tensor_tensor(
            s[:, 4:5], s[:, 1:2], 1.0 / D, s[:, 3:4], AO.mult, AO.subtract),
           waits=[c_ssq.at(i)], inc=c_var)
        op("scalar", lambda e, s=sc: e.activation(s[:, 5:6], s[:, 4:5], AF.Ln, bias=tcol[:, 69:70]),
           waits=[c_var.at(i), c_mset.at(3)])
        op("scalar", lambda e, s=sc: e.activation(s[:, 6:7], s[:, 5:6], AF.Exp, scale=-0.5),
           inc=c_rstd)
        xnw = [c_y8.at(i - 2), c_t1.at(i - 2), c_t2.at(i - 2)] if i >= 2 else []
        op("vector", lambda e, bb=b, s=sc: e.tensor_scalar(
            xn_t[bb], x_t[bb], s[:, 2:3], s[:, 6:7], AO.subtract, AO.mult),
           waits=[c_rstd.at(i)] + xnw, inc=c_xn)
        if debug:
            op("scalar", lambda e, ii=i, bb=b: e.dma_start(
                out=d_xn[ii * 128:(ii + 1) * 128, :], in_=xn_t[bb]),
               waits=[c_xn.at(i)], inc=c_dbg)
            op("scalar", lambda e, ii=i, s=sc: e.dma_start(
                out=d_scall[ii:ii + 1, :], in_=s[0:1, 0:16]),
               waits=[c_xn.at(i)], inc=c_dbg)
        op("vector", lambda e, bb=b, s=sc: e.reduce_max(
            s[:, 7:8], xn_t[bb], axis=AX, apply_absolute_value=True), waits=[c_xn.at(i)])
        op("vector", lambda e, s=sc: e.tensor_scalar(s[:, 8:9], s[:, 7:8], 1e-5, None, AO.max))
        op("vector", lambda e, s=sc: e.reciprocal(s[:, 9:10], s[:, 8:9]))
        op("vector", lambda e, s=sc: e.tensor_scalar(s[:, 10:11], s[:, 9:10], 127.0, None, AO.mult),
           inc=c_sx)
        op("vector", lambda e, ii=i, s=sc: e.tensor_scalar(
            tcol[:, ii:ii + 1], s[:, 8:9], 1.0 / 127.0, None, AO.mult), inc=c_deqx)
        y8w = [c_xnq.at(i - 1)] if i >= 1 else []
        op("scalar", lambda e, bb=b, s=sc: e.activation(
            y8_t, xn_t[bb], AF.Copy, scale=s[:, 10:11]),
           waits=[c_sx.at(i)] + y8w, inc=c_y8)
        op("vector", lambda e, bb=b, s=sc: e.scalar_tensor_tensor(
            d_t, xn_t[bb], s[:, 10:11], y8_t, AO.mult, AO.subtract),
           waits=[c_y8.at(i), c_ssq.at(i)], inc=c_dq)
        cg = [c_m2.at(i - 1)] if i >= 1 else []
        op("gpsimd", lambda e: e.tensor_scalar(c1_t[:, 0:1024], d_t, 0.5, None, AO.is_ge),
           waits=[c_dq.at(i)] + cg, inc=c_c1)
        op("vector", lambda e: e.scalar_tensor_tensor(
            m2_t[:, 0:1024], d_t, -0.5, c1_t[:, 0:1024], AO.is_le, AO.subtract),
           waits=[c_c1.at(i)], inc=c_m2)
        xqw = [c_xnqw.at(i - 1)] if i >= 1 else []
        op("vector", lambda e: e.scalar_tensor_tensor(
            xnq_slot, y8_t, 0.0, m2_t[:, 0:1024], AO.add, AO.subtract),
           waits=[c_m2.at(i)] + xqw, inc=c_xnq)
        op("scalar", lambda e, ii=i: e.dma_start(
            out=s_xnq[ii * 128:(ii + 1) * 128, :], in_=xnq_slot),
           waits=[c_xnq.at(i)], inc=c_xnqw)
        op("vector", lambda e, bb=b: e.tensor_tensor(d_t, xn_t[bb], ctab_t, AO.mult),
           waits=[c_ct.at(2 * i), c_m2.at(i)], inc=c_t1)
        op("sync", lambda e, ii=i: e.dma_start(
            out=ctab_t, in_=ctab_h[ii * 128:(ii + 1) * 128, 1024:2048]),
           waits=[c_t1.at(i)], inc=c_ct)
        t2w = [c_xr.at(i - 1)] if i >= 1 else []
        op("gpsimd", lambda e, bb=b: e.tensor_tensor(
            t2_slot.rearrange("p (c two) -> p c two", two=2),
            xn_t[bb].rearrange("p (c two) -> p c two", two=2)[:, :, ::-1],
            ctab_t.rearrange("p (c two) -> p c two", two=2), AO.mult),
           waits=[c_ct.at(2 * i + 1)] + t2w, inc=c_t2)
        xrw = [c_xrw.at(i - 1)] if i >= 1 else []
        op("vector", lambda e: e.tensor_tensor(xr_slot, d_t, t2_slot, AO.add),
           waits=[c_t1.at(i), c_t2.at(i)] + xrw, inc=c_xr)
        op("scalar", lambda e, ii=i: e.dma_start(
            out=s_xr[ii * 128:(ii + 1) * 128, :], in_=xr_slot),
           waits=[c_xr.at(i)], inc=c_xrw)

    wbuf = [wprep_t, gub2_t]
    wbuf_last = [[], []]
    wj = 0
    for j in range(32):
        src_ = w_in[j * 128:(j + 1) * 128, :] if j < 24 else w_o[(j - 24) * 128:(j - 23) * 128, :]
        dst_ = s_win[j * 128:(j + 1) * 128, :] if j < 24 else s_wo[(j - 24) * 128:(j - 23) * 128, :]
        sl = wj % 2
        i_ld = op("sync", lambda e, s2=src_, w=wbuf[sl]: e.dma_start(out=w, in_=s2),
                  waits=wbuf_last[sl], inc=c_wld)
        i_c = op("gpsimd", lambda e, d2=dst_, w=wbuf[sl]: e.dma_start(out=d2, in_=w),
                 waits=[c_wld.at(i_ld)], inc=c_wcast)
        wbuf_last[sl] = [c_wcast.at(i_c)]
        wj += 1

    tern = [(w_g, s_wg, INNER * D), (w_u, s_wu, INNER * D), (w_d, s_wd, D * INNER)]
    wqw_end = []
    for wi, (wsrc, wdst, nelem) in enumerate(tern):
        nchunk = nelem // (128 * 1024)
        ncb = wsrc.shape[1] // 1024
        chunks = [wsrc[(j // ncb) * 128:(j // ncb + 1) * 128,
                       (j % ncb) * 1024:(j % ncb + 1) * 1024] for j in range(nchunk)]
        dchunks = [wdst[(j // ncb) * 128:(j // ncb + 1) * 128,
                        (j % ncb) * 1024:(j % ncb + 1) * 1024] for j in range(nchunk)]
        op("vector", lambda e: e.memset(tcol[:, 64:65], 0.0),
           waits=[c_wsum.now(), c_wmm.now()], inc=c_wsum)
        for j in range(nchunk):
            sl = wj % 2
            i_ld = op("sync", lambda e, f=chunks[j], w=wbuf[sl]: e.dma_start(
                out=w, in_=f), waits=wbuf_last[sl], inc=c_wld)
            i_ab = op("vector", lambda e, w=wbuf[sl]: e.reduce_sum(
                tcol[:, 65:66], w, axis=AX, apply_absolute_value=True),
                waits=[c_wld.at(i_ld), c_wsum.last()], inc=c_wabs)
            wbuf_last[sl] = [c_wabs.at(i_ab)]
            op("vector", lambda e: e.tensor_tensor(
                tcol[:, 64:65], tcol[:, 64:65], tcol[:, 65:66], AO.add),
               waits=[c_wabs.at(i_ab)], inc=c_wsum)
            wj += 1
        bw = banks[7].acquire()
        i_mm = op("tensor", MM(pb[7][0:1, 0:1], tcol[:, 64:65], ones_f[:, 0:1], True, True),
                  waits=[c_wsum.last()] + bw, inc=c_wmm)
        i_t = op("scalar", lambda e, w2=wi: e.copy(scal[0:1, w2:w2 + 1], pb[7][0:1, 0:1]),
                 waits=[c_wmm.at(i_mm)], inc=c_thr)
        banks[7].consumed(c_thr, i_t)
        op("vector", lambda e, w2=wi: e.tensor_scalar(
            scal[0:1, 3 + w2:4 + w2], scal[0:1, w2:w2 + 1], 1.0 / nelem, 1e-5, AO.mult, AO.max),
           waits=[c_thr.at(i_t)], inc=c_thr)
        op("vector", lambda e, w2=wi: e.tensor_scalar(
            scal[0:1, 6 + w2:7 + w2], scal[0:1, 3 + w2:4 + w2], 0.5, None, AO.mult), inc=c_thr)
        bw = banks[7].acquire()
        op("tensor", MM(pb[7][:, 0:1], ones_f[0:1, :], scal[0:1, 6 + wi:7 + wi], True, False),
           waits=[c_thr.last()] + bw)
        i_mm = op("tensor", MM(pb[7][:, 1:2], ones_f[0:1, :], scal[0:1, 3 + wi:4 + wi], False, True),
                  inc=c_wmm)
        i_t = op("scalar", lambda e, w2=wi: e.copy(thr_col[:, 4 * w2:4 * w2 + 2], pb[7][:, 0:2]),
                 waits=[c_wmm.at(i_mm)], inc=c_thr)
        banks[7].consumed(c_thr, i_t)
        op("vector", lambda e, w2=wi: e.tensor_scalar(
            thr_col[:, 4 * w2 + 2:4 * w2 + 3], thr_col[:, 4 * w2:4 * w2 + 1], -1.0, None, AO.mult),
           waits=[c_thr.at(i_t)], inc=c_thr)
        op("vector", lambda e, w2=wi: e.tensor_scalar(
            thr_col[:, 4 * w2 + 3:4 * w2 + 4], thr_col[:, 4 * w2 + 1:4 * w2 + 2], -1.0, None, AO.mult),
           inc=c_thr)
        thr_rdy = c_thr.last()
        for j in range(nchunk):
            sl = wj % 2
            i_ld = op("sync", lambda e, f=chunks[j], w=wbuf[sl]: e.dma_start(
                out=w, in_=f),
                waits=wbuf_last[sl] + [thr_rdy], inc=c_wld2)
            cg = [c_wq.last()] if c_wq.n else []
            i_c1 = op("gpsimd", lambda e, w=wbuf[sl], w2=wi: e.tensor_scalar(
                c1_t[:, 1024:2048], w, thr_col[:, 4 * w2:4 * w2 + 1], None, AO.is_ge),
                waits=[c_wld2.at(i_ld), thr_rdy] + cg +
                      ([c_c1.at(NT - 1)] if (wi == 0 and j == 0) else []),
                inc=c_wc1)
            qw = [c_wqw.at(c_wqw.n - 1)] if c_wqw.n >= 1 else []
            i_wq = op("vector", lambda e, w=wbuf[sl], w2=wi: e.scalar_tensor_tensor(
                wq_t, w, thr_col[:, 4 * w2 + 2:4 * w2 + 3], c1_t[:, 1024:2048],
                AO.is_le, AO.subtract),
                waits=[c_wc1.at(i_c1)] + qw, inc=c_wq)
            wbuf_last[sl] = [c_wq.at(i_wq)]
            op("scalar", lambda e, fd=dchunks[j]: e.dma_start(out=fd, in_=wq_t),
               waits=[c_wq.at(i_wq)], inc=c_wqw)
            wj += 1
        wqw_end.append(c_wqw.n - 1)

    if debug:
        op("scalar", lambda e: e.dma_start(out=d_sc[:, 0:16], in_=scols[0]),
           waits=[c_deqx.at(NT - 2)], inc=c_dbg)
        op("scalar", lambda e: e.dma_start(out=d_sc[:, 16:32], in_=scols[1]),
           waits=[c_deqx.at(NT - 1)], inc=c_dbg)
    op("sync", lambda e: e.dma_start(out=mask_os, in_=mask_o_h),
       waits=[c_xrw.at(NT - 1), c_xnqw.at(NT - 1), c_xr.at(NT - 1)], inc=c_mskld)

    for tokB in range(4):
        for kk in range(NKK):
            qd = "sync"
            op(qd, lambda e, tb=tokB, k2=kk: e.dma_start(
                out=big2[:, k2 * 2048 + tb * 512: k2 * 2048 + (tb + 1) * 512],
                in_=s_xr[tb * 512:(tb + 1) * 512, k2 * 128:(k2 + 1) * 128], transpose=True),
               waits=[c_xrw.at(tokB * 4 + 3)], inc=c_xrfm)

    if stage <= 1:
        with nc.Block() as block:
            prog.realize(block, [c_xnqw.now(), c_xrfm.now(), c_wcast.now(), c_wqw.now(),
                                 c_dbg.now()])
        es.close()
        return nc

    nqk = 0
    for tokB in range(4):
        for mpair in range(8):
            sl = (mpair % 2) * 2048
            grp = tokB * 8 + mpair
            for kk in range(NKK):
                qd = "sync" if kk % 2 == 0 else "scalar"
                jw = [c_wcast.at(11), c_xrfm.at(tokB * 8 + 7)]
                if grp >= 2:
                    jw.append(c_mm2a.at((grp - 2) * 2 + 1))
                op(qd, lambda e, mp=mpair, k2=kk, s2=sl: e.dma_start(
                    out=wstr[:, s2 + k2 * 256: s2 + k2 * 256 + 256],
                    in_=s_win[mp * 256:(mp + 1) * 256, k2 * 128:(k2 + 1) * 128], transpose=True),
                   waits=jw, inc=c_wqk)
            for m2 in range(2):
                m = mpair * 2 + m2
                bi = nqk % 6
                mms = [(pb[bi][0:128, 0:512],
                        wstr[:, sl + kk * 256 + m2 * 128: sl + kk * 256 + (m2 + 1) * 128],
                        big2[:, kk * 2048 + tokB * 512: kk * 2048 + (tokB + 1) * 512])
                       for kk in range(NKK)]
                i_mm = mm_group(bi, mms,
                                waits=[c_wqk.at(grp * 8 + 7), c_xrfm.at(tokB * 8 + 7)],
                                inc=c_mm2a)
                dst = q_fm if m < 8 else k_fm
                coff = (m % 8) * 2048 + tokB * 512
                i_ev = op("vector", lambda e, b2=bi, d2=dst, c2=coff, mm=m: e.tensor_scalar(
                    d2[:, c2:c2 + 512], pb[b2][0:128, 0:512], b_cols[:, mm:mm + 1], None, AO.add),
                    waits=[c_mm2a.at(i_mm), c_bcol.at(0)], inc=c_qke)
                banks[bi].consumed(c_qke, i_ev)
                nqk += 1

    for kk in range(NKK):
        qd = "sync"
        op(qd, lambda e, k2=kk: e.dma_start(
            out=wstr[:, 4096 + k2 * 1024: 4096 + (k2 + 1) * 1024],
            in_=s_win[2048:3072, k2 * 128:(k2 + 1) * 128], transpose=True),
           waits=[c_wcast.at(23), c_mm2a.now()], inc=c_wv)
    for half in range(2):
        op("sync", lambda e, hh=half: e.dma_start(
            out=b_bias[hh], in_=b_in[:, 2048 + hh * 512:2048 + (hh + 1) * 512]),
           waits=[c_bgrp.at(3), c_mm2a.now()], inc=c_bias)
    nv = 0
    vv = v_tm.rearrange("p (t c) -> p t c", c=65)
    vg = v_g.rearrange("p (t c) -> p t c", c=65)
    for mt in range(NT):
        for half in range(2):
            bi = nv % 6
            mms = [(pb[bi][0:128, 0:512],
                    big2[:, kk * 2048 + mt * 128: kk * 2048 + (mt + 1) * 128],
                    wstr[:, 4096 + kk * 1024 + half * 512: 4096 + kk * 1024 + (half + 1) * 512])
                   for kk in range(NKK)]
            mms.append((pb[bi][0:128, 0:512], ones_f[0:1, 0:128], b_bias[half][0:1, 0:512]))
            i_mm = mm_group(bi, mms, waits=[c_wv.at(7), c_bias.at(4 + half), c_xrfm.at(31)], inc=c_mm2b)
            i_ev = op("vector", lambda e, b2=bi, m3=mt, h2=half: e.tensor_copy(
                vv[:, m3 * 16 + h2 * 8:m3 * 16 + (h2 + 1) * 8, 0:64],
                pb[b2][0:128, 0:512].rearrange("p (a b) -> p a b", b=64)),
                waits=[c_mm2b.at(i_mm), c_mset.at(2)], inc=c_ve)
            banks[bi].consumed(c_ve, i_ev)
            nv += 1
    for half in range(2):
        mms = [(pb[6][0:8, 0:512],
                big2[:, kk * 2048: kk * 2048 + 2048:256],
                wstr[:, 4096 + kk * 1024 + half * 512: 4096 + kk * 1024 + (half + 1) * 512])
               for kk in range(NKK)]
        mms.append((pb[6][0:8, 0:512], ones_f[0:1, 0:8], b_bias[half][0:1, 0:512]))
        i_mm = mm_group(6, mms, waits=[c_wv.at(7), c_bias.at(4 + half), c_xrfm.at(31)], inc=c_mmvg)
        i_ev = op("vector", lambda e, h2=half: e.tensor_copy(
            vg[:, h2 * 8:(h2 + 1) * 8, 0:64],
            pb[6][0:8, 0:512].rearrange("p (a b) -> p a b", b=64)),
            waits=[c_mmvg.at(i_mm)], inc=c_vg)
        banks[6].consumed(c_vg, i_ev)
    op("vector", lambda e: e.memset(vg[:, :, 64:65], 1.0),
       waits=[c_wqk.last(), c_vg.at(c_vg.n - 1)], inc=c_vg)

    if debug:
        op("sync", lambda e: e.dma_start(out=d_qfm, in_=q_fm), waits=[c_qke.now()], inc=c_out)
        op("sync", lambda e: e.dma_start(out=d_kfm, in_=k_fm), waits=[c_qke.now()], inc=c_out)
        op("sync", lambda e: e.dma_start(out=d_vtm, in_=v_tm), waits=[c_ve.now(), c_vg.now()], inc=c_out)
    if stage <= 2:
        with nc.Block() as block:
            prog.realize(block, [c_qke.now(), c_ve.now(), c_vg.now(), c_out.now(),
                                 c_xnqw.now(), c_wqw.now()])
        es.close()
        return nc

    nmx = 0
    for h in range(H):
        chk, off = h // 2, 64 * (h % 2)
        mc = mcols[h % 2]
        for qi in range(NT):
            bi = nmx % 2
            qlo, qhi = max(qi - 1, 0), min(qi + 1, NT - 1)
            nq = (qhi - qlo + 1) * 128
            bw = banks[bi].acquire()
            mw = [c_qke.now()] + bw
            if qi == 0 and h >= 2:
                mw.append(c_trm.at(h - 2))
            op("tensor", MM(pb[bi][0:128, 0:nq],
                            q_fm[off:off + 64, chk * 2048 + qi * 128: chk * 2048 + (qi + 1) * 128],
                            k_fm[off:off + 64, chk * 2048 + qlo * 128: chk * 2048 + qlo * 128 + nq],
                            True, False), waits=mw)
            i_mm = op("tensor", MM(pb[bi][0:128, nq:nq + 8],
                                   q_fm[off:off + 64, chk * 2048 + qi * 128: chk * 2048 + (qi + 1) * 128],
                                   k_fm[off:off + 64, chk * 2048: chk * 2048 + 2048:256],
                                   False, True), inc=c_mmax)
            rdw = [c_mmax.at(i_mm)]
            if h < 2 and qi == 0:
                rdw += [c_qke.now()] if h == 0 else [c_y8.at(NT - 1), c_sx.at(NT - 1)]
            i_rd = op("vector", lambda e, b2=bi, q2=qi, n2=nq, m3=mc: e.reduce_max(
                m3[:, q2:q2 + 1], pb[b2][0:128, 0:n2 + 8], axis=AX),
                waits=rdw, inc=c_mred)
            banks[bi].consumed(c_mred, i_rd)
            nmx += 1
        bw = banks[6].acquire()
        i_tr = op("tensor", lambda e, m3=mc: e.transpose(pb[6][0:16, 0:128], m3, ident_s),
                  waits=[c_mred.last(), c_init.at(0)] + bw, inc=c_trm)
        i_ev = op("scalar", lambda e: e.copy(mtmp[0:16, 0:128], pb[6][0:16, 0:128]),
                  waits=[c_trm.at(i_tr), c_mwr.now(), c_xr.at(NT - 1)], inc=c_mtr)
        banks[6].consumed(c_mtr, i_ev)
        op("scalar", lambda e, h2=h: e.dma_start(
            out=s_m[h2:h2 + 1, :].rearrange("a (q c) -> (a q) c", c=128),
            in_=mtmp[0:16, 0:128]),
           waits=[c_mtr.at(i_ev)], inc=c_mwr)

    m8row = [wstr[32:33, 2048:4096], wstr[64:65, 2048:4096]]
    P_g = wstr[0:8, 5120:7168]
    linv_bc = wstr[:, 4608:5120]
    last_m8 = [[], []]
    last_lrow = [[], []]
    last_pg = []
    pt_last = [None, None]

    for h in range(H):
        par = h % 2
        chk, off = h // 2, 64 * (h % 2)
        row = 64 * (1 - par)
        mw = [c_mwr.at(h)] + last_m8[par]
        if h < 2:
            mw.append(c_mm2a.now())
        op("gpsimd", lambda e, h2=h, p2=par: e.dma_start(out=m8row[p2], in_=s_m[h2:h2 + 1, :]),
           waits=mw, inc=c_m8)
        i_m8 = c_m8.n - 1
        last_m8[par] = []
        for qB in range(4):
            bw = banks[2].acquire()
            op("tensor", MM(pb[2][0:8, 0:512],
                            k_fm[off:off + 64, chk * 2048: chk * 2048 + 2048:256],
                            q_fm[off:off + 64, chk * 2048 + qB * 512: chk * 2048 + (qB + 1) * 512],
                            True, False),
               waits=[c_m8.at(i_m8), c_qke.now()] + bw)
            orow = 32 * (par + 1)
            i_mm = op("tensor", MM(pb[2][0:8, 0:512], ones_b[orow:orow + 1, 0:8],
                                   m8row[par][0:1, qB * 512:(qB + 1) * 512], False, True),
                      inc=c_mmg)
            last_m8[par] = [c_mmg.at(i_mm)]
            pw = [c_mmg.at(i_mm)]
            if qB == 0:
                pw += last_pg
                if h == 0:
                    pw += [c_mm2a.now(), c_mm2b.now(), c_mmvg.now()]
            i_pg = op("scalar", lambda e, q2=qB: e.activation(
                P_g[0:8, q2 * 512:(q2 + 1) * 512], pb[2][0:8, 0:512], AF.Exp, scale=0.125),
                waits=pw, inc=c_pg)
            banks[2].consumed(c_pg, i_pg)
        pg_base = c_pg.n - 4
        if debug and h == 0:
            op("scalar", lambda e: e.dma_start(out=d_pg, in_=P_g),
               waits=[c_pg.at(pg_base + 3)], inc=c_dbg)
            op("scalar", lambda e: e.dma_start(out=d_vg, in_=v_g),
               waits=[c_vg.now()], inc=c_dbg)
            op("scalar", lambda e, p2=par: e.dma_start(out=d_m8, in_=m8row[p2]),
               waits=[c_m8.at(i_m8)], inc=c_dbg)
        last_pg = []
        first_lcopy = True
        for kj in range(NT):
            bi = kj % 2
            qlo, qhi = max(kj - 1, 0), min(kj + 1, NT - 1)
            nq = (qhi - qlo + 1) * 128
            bw = banks[bi].acquire()
            op("tensor", MM(pb[bi][0:128, 0:nq],
                            k_fm[off:off + 64, chk * 2048 + kj * 128: chk * 2048 + (kj + 1) * 128],
                            q_fm[off:off + 64, chk * 2048 + qlo * 128: chk * 2048 + qlo * 128 + nq],
                            True, False),
               waits=[c_m8.at(i_m8)] + bw)
            orow = 32 * (par + 1)
            i_mm = op("tensor", MM(pb[bi][0:128, 0:nq], ones_b[orow:orow + 1, 0:128],
                                   m8row[par][0:1, qlo * 128: qlo * 128 + nq], False, True),
                      inc=c_mmS)
            last_m8[par].append(c_mmS.at(i_mm))
            expw = [c_mmS.at(i_mm)]
            if kj % 2 == 0:
                i_ms = op("vector", lambda e, b2=bi, n2=nq: e.memset(pb[b2][0:1, 0:n2], NEG),
                          waits=[c_mmS.at(i_mm)], inc=c_ms)
                expw.append(c_ms.at(i_ms))
            if pt_last[bi] is not None:
                expw.append(pt_last[bi])
            else:
                expw.append(c_wqw.at(wqw_end[2]))
            i_exp = op("scalar", lambda e, b2=bi, n2=nq: e.activation(
                pt_sb[b2][:, 0:n2], pb[b2][0:128, 0:n2], AF.Exp, scale=0.125),
                waits=expw, inc=c_expS)
            banks[bi].consumed(c_expS, i_exp)
            msk = mask_es if kj % 2 == 0 else mask_os
            mslice = msk[:, 128:384] if kj == 0 else (msk[:, 0:256] if kj == NT - 1 else msk[:, 0:384])
            i_msk = op("gpsimd", lambda e, b2=bi, n2=nq, m3=mslice: e.tensor_tensor(
                pt_sb[b2][:, 0:n2], pt_sb[b2][:, 0:n2], m3, AO.mult),
                waits=[c_expS.at(i_exp), c_mskld.at(0)], inc=c_mask)
            for qi in range(qlo, qhi + 1):
                slot = 3 + qi % 3
                first = (kj == max(qi - 1, 0))
                vbase = kj * 1040 + h * 65
                ptc = pt_sb[bi][:, (qi - qlo) * 128:(qi - qlo + 1) * 128]
                wl = [c_mask.at(i_msk), c_ve.now(), c_mset.at(2)]
                if first:
                    wl += banks[slot].acquire()
                if par == 0:
                    op("tensor", MM(pb[slot][0:65, 0:128], v_tm[:, vbase:vbase + 65], ptc,
                                    first, False), waits=wl, inc=c_pv)
                else:
                    op("tensor", MM(pb[slot][64:128, 0:128], v_tm[:, vbase:vbase + 64], ptc,
                                    first, False), waits=wl)
                    op("tensor", MM(pb[slot][0:1, 128:256], v_tm[:, vbase + 64:vbase + 65], ptc,
                                    first, False), inc=c_pv)
                if first:
                    gb = h * 65
                    pgc = P_g[0:8, qi * 128:(qi + 1) * 128]
                    gw = [c_pg.at(pg_base + qi // 4), c_vg.now()]
                    if par == 0:
                        op("tensor", MM(pb[slot][0:65, 0:128], v_g[0:8, gb:gb + 65], pgc,
                                        False, False), waits=gw, inc=c_pv)
                    else:
                        op("tensor", MM(pb[slot][64:128, 0:128], v_g[0:8, gb:gb + 64], pgc,
                                        False, False), waits=gw)
                        op("tensor", MM(pb[slot][0:1, 128:256], v_g[0:8, gb + 64:gb + 65], pgc,
                                        False, False), inc=c_pv)
                    last_pg = [c_pv.at(c_pv.n - 1)]
            pt_last[bi] = c_pv.at(c_pv.n - 1)
            closed = ([kj - 1] if kj >= 1 else []) + ([NT - 1] if kj == NT - 1 else [])
            for qi in closed:
                slot = 3 + qi % 3
                ocol = chk * 2048 + qi * 128
                ow = [c_pv.at(c_pv.n - 1), c_ve.now(), c_vg.now()]
                if h == 0:
                    ow += [c_mm2b.now(), c_mmvg.now()]
                lw = [c_pv.at(c_pv.n - 1)]
                if first_lcopy:
                    lw += last_lrow[par] + [c_xn.at(NT - 1), c_ssq.at(NT - 1)]
                    first_lcopy = False
                if par == 0:
                    i_oe = op("vector", lambda e, s2=slot, o2=ocol: e.tensor_copy(
                        big2[0:64, o2:o2 + 128], pb[s2][0:64, 0:128]), waits=ow, inc=c_oevac)
                    i_lc = op("scalar", lambda e, s2=slot, q2=qi: e.copy(
                        l_rows[q2 // 8][64:65, (q2 % 8) * 128:(q2 % 8 + 1) * 128],
                        pb[s2][64:65, 0:128]),
                        waits=lw, inc=c_lcopy)
                else:
                    i_oe = op("vector", lambda e, s2=slot, o2=ocol: e.tensor_copy(
                        big2[64:128, o2:o2 + 128], pb[s2][64:128, 0:128]), waits=ow, inc=c_oevac)
                    i_lc = op("scalar", lambda e, s2=slot, q2=qi: e.copy(
                        l_rows[q2 // 8][0:1, (q2 % 8) * 128:(q2 % 8 + 1) * 128],
                        pb[s2][0:1, 128:256]),
                        waits=lw, inc=c_lcopy)
                banks[slot].consumed(c_oevac, i_oe)
                banks[slot].consumed(c_lcopy, i_lc)
        for half2 in range(2):
            op("vector", lambda e, r2=row, h3=half2: e.reciprocal(
                l_rows[h3][r2:r2 + 1, :], l_rows[h3][r2:r2 + 1, :]),
               waits=[c_lcopy.last()], inc=c_recip)
        rc_base = c_recip.n - 2
        last_lrow[par] = []
        for qB in range(4):
            bw = banks[2].acquire()
            i_bc = op("tensor", MM(pb[2][off:off + 64, 0:512], ones_f[row:row + 1, 0:64],
                                   l_rows[qB // 2][row:row + 1, (qB % 2) * 512:(qB % 2 + 1) * 512],
                                   True, True),
                      waits=[c_recip.at(rc_base + qB // 2)] + bw, inc=c_bcmm)
            last_lrow[par].append(c_bcmm.at(i_bc))
            bew = [c_bcmm.at(i_bc)]
            if h < 2 and qB == 0:
                bew += [c_mm2b.now(), c_mmvg.now()]
            i_be = op("scalar", lambda e, o2=off: e.copy(
                linv_bc[o2:o2 + 64, 0:512], pb[2][o2:o2 + 64, 0:512]),
                waits=bew, inc=c_bce)
            banks[2].consumed(c_bce, i_be)
            op("gpsimd", lambda e, o2=off, c2=chk, q2=qB: e.tensor_tensor(
                big2[o2:o2 + 64, c2 * 2048 + q2 * 512: c2 * 2048 + (q2 + 1) * 512],
                big2[o2:o2 + 64, c2 * 2048 + q2 * 512: c2 * 2048 + (q2 + 1) * 512],
                linv_bc[o2:o2 + 64, 0:512], AO.mult),
               waits=[c_bce.at(i_be), c_oevac.last()], inc=c_onorm)

    if debug:
        op("sync", lambda e: e.dma_start(out=d_ofm, in_=big2), waits=[c_onorm.now()], inc=c_out)
    if stage <= 3:
        with nc.Block() as block:
            prog.realize(block, [c_onorm.now(), c_out.now(), c_xnqw.now(), c_wqw.now()])
        es.close()
        return nc

    for kk in range(NKK):
        qd = "sync"
        op(qd, lambda e, k2=kk: e.dma_start(
            out=v_tm[:, k2 * 1024:(k2 + 1) * 1024],
            in_=s_wo[:, k2 * 128:(k2 + 1) * 128], transpose=True),
           waits=[c_wcast.at(31), c_pv.now()], inc=c_wot)
    for half in range(2):
        op("sync", lambda e, hh=half: e.dma_start(
            out=b_bias[hh], in_=b_o[:, hh * 512:(hh + 1) * 512]),
           waits=[c_mmvg.now(), c_mm2b.now()], inc=c_bias)
    for mt in range(NT):
        b = mt % 2
        xw = [c_acce.at((mt - 2) * 2 + 1)] if mt >= 2 else \
            [c_xn.at(NT - 1), c_ssq.at(NT - 1), c_bcmm.now(), c_recip.now()]
        op("sync", lambda e, m3=mt, bb=b: e.dma_start(
            out=x_t[bb], in_=x_h[m3 * 128:(m3 + 1) * 128, :]), waits=xw, inc=c_x2)
        for half in range(2):
            bi = (mt * 2 + half) % 4
            mms = [(pb[bi][0:128, 0:512],
                    big2[:, kk * 2048 + mt * 128: kk * 2048 + (mt + 1) * 128],
                    v_tm[:, kk * 1024 + half * 512: kk * 1024 + (half + 1) * 512])
                   for kk in range(NKK)]
            mms.append((pb[bi][0:128, 0:512], ones_f[0:1, 0:128], b_bias[half][0:1, 0:512]))
            i_mm = mm_group(bi, mms, waits=[c_wot.at(7), c_bias.at(6 + half), c_onorm.now()], inc=c_mm4)
            evw = [c_mm4.at(i_mm), c_x2.at(mt)]
            if mt >= 2:
                evw.append(c_accw.at(mt - 2))
            else:
                evw += [c_t1.at(NT - 1), c_t2.at(NT - 1), c_y8.at(NT - 1), c_dq.at(NT - 1)]
            i_ev = op("vector", lambda e, b2=bi, bb=b, h2=half: e.tensor_tensor(
                xn_t[bb][:, h2 * 512:(h2 + 1) * 512], pb[b2][0:128, 0:512],
                x_t[bb][:, h2 * 512:(h2 + 1) * 512], AO.add),
                waits=evw, inc=c_acce)
            banks[bi].consumed(c_acce, i_ev)
        op("sync", lambda e, m3=mt, bb=b: e.dma_start(
            out=s_acc[m3 * 128:(m3 + 1) * 128, :], in_=xn_t[bb]),
           waits=[c_acce.at(mt * 2 + 1)], inc=c_accw)

    if stage <= 4:
        with nc.Block() as block:
            prog.realize(block, [c_accw.now(), c_xnqw.now(), c_wqw.now()])
        es.close()
        return nc

    for tokB in range(4):
        for kk in range(NKK):
            op("sync", lambda e, tb=tokB, k2=kk: e.dma_start(
                out=big2[:, k2 * 2048 + tb * 512: k2 * 2048 + (tb + 1) * 512],
                in_=s_xnq[tb * 512:(tb + 1) * 512, k2 * 128:(k2 + 1) * 128], transpose=True),
               waits=[c_xnqw.at(15), c_mm4.now()], inc=c_xnqfm)
    op("vector", lambda e: e.tensor_scalar(
        tcol[:, 32:48], tcol[:, 0:16], thr_col[:, 3:4], None, AO.mult),
       waits=[c_deqx.at(NT - 1), c_thr.now()], inc=c_deqg)
    op("vector", lambda e: e.tensor_scalar(
        tcol[:, 48:64], tcol[:, 0:16], thr_col[:, 7:8], None, AO.mult), inc=c_deqg)
    op("vector", lambda e: e.memset(tcol[:, 16:32], 0.0), inc=c_rmax)
    rmax0 = c_rmax.n - 1

    ngu = 0
    for ns in range(8):
        gsl = (ns % 2) * 4096
        for kk in range(NKK):
            gw2 = [c_wqw.at(wqw_end[0])]
            if ns >= 2:
                gw2.append(c_mmg5.at((ns - 2) * 16 + 15))
            op("sync", lambda e, n2=ns, k2=kk, s2=gsl: e.dma_start(
                out=wstr[:, s2 + k2 * 512: s2 + (k2 + 1) * 512],
                in_=s_wg[n2 * 512:(n2 + 1) * 512, k2 * 128:(k2 + 1) * 128], transpose=True),
               waits=gw2 + ([c_mmS.now(), c_mmg.now(), c_pv.now(), c_onorm.now()]
                            if ns < 2 else []),
               inc=c_wgu)
        for kk in range(NKK):
            uw = [c_wqw.at(wqw_end[1])]
            if ns >= 1:
                uw.append(c_mmu5.at((ns - 1) * 16 + 15))
            else:
                uw += [c_mm2b.now(), c_mmvg.now(), c_bce.now()]
            op("sync", lambda e, n2=ns, k2=kk: e.dma_start(
                out=wstr[:, 8192 + k2 * 512: 8192 + (k2 + 1) * 512],
                in_=s_wu[n2 * 512:(n2 + 1) * 512, k2 * 128:(k2 + 1) * 128], transpose=True),
               waits=uw, inc=c_wguu)
        wg_rdy = c_wgu.at(ns * 8 + 7)
        wu_rdy = c_wguu.at(ns * 8 + 7)
        for mt in range(NT):
            big = gu_big[(ngu // 2) % 2]
            bcol = (mt % 2) * 512
            bg, bu = (mt % 2) * 2, (mt % 2) * 2 + 1
            mmsg = [(pb[bg][0:128, 0:512],
                     big2[:, kk * 2048 + mt * 128: kk * 2048 + (mt + 1) * 128],
                     wstr[:, gsl + kk * 512: gsl + (kk + 1) * 512]) for kk in range(NKK)]
            i_mmg = mm_group(bg, mmsg, waits=[wg_rdy, c_xnqfm.at(31)], inc=c_mmg5)
            gev = guev[(mt % 2) * 2]
            gevw = [c_mmg5.at(i_mmg), c_deqg.at(0)]
            gevw += [c_gum.at(c_gum.n - 2)] if c_gum.n >= 2 else \
                [c_xr.at(NT - 1), c_mwr.at(H - 1), c_t2.at(NT - 1)]
            i_ge = op("scalar", lambda e, b2=bg, m3=mt, g2=gev: e.activation(
                g2, pb[b2][0:128, 0:512], AF.Silu, scale=tcol[:, 32 + m3:33 + m3]),
                waits=gevw, inc=c_ge)
            banks[bg].consumed(c_ge, i_ge)
            mmsu = [(pb[bu][0:128, 0:512],
                     big2[:, kk * 2048 + mt * 128: kk * 2048 + (mt + 1) * 128],
                     wstr[:, 8192 + kk * 512: 8192 + (kk + 1) * 512]) for kk in range(NKK)]
            i_mmu = mm_group(bu, mmsu, waits=[wu_rdy, c_xnqfm.at(31)], inc=c_mmu5)
            uev = guev[(mt % 2) * 2 + 1]
            uevw = [c_mmu5.at(i_mmu), c_deqg.at(1)]
            uevw += [c_gum.at(c_gum.n - 2)] if c_gum.n >= 2 else \
                [c_xr.at(NT - 1), c_mwr.at(H - 1), c_t2.at(NT - 1)]
            i_ue = op("scalar", lambda e, b2=bu, m3=mt, u2=uev: e.activation(
                u2, pb[b2][0:128, 0:512], AF.Copy, scale=tcol[:, 48 + m3:49 + m3]),
                waits=uevw, inc=c_ue)
            banks[bu].consumed(c_ue, i_ue)
            guw = [c_guw.at((ngu - 4) // 2), c_rmax.at(rmax0 + ngu - 3)] if ngu >= 4 else \
                ([c_wq.last(), c_wc1.last()] if ngu < 2 else [])
            i_gm = op("gpsimd", lambda e, g2=gev, u2=uev, b3=big, c2=bcol: e.tensor_tensor(
                b3[:, c2:c2 + 512], g2, u2, AO.mult),
                waits=[c_ge.at(i_ge), c_ue.at(i_ue)] + guw, inc=c_gum)
            op("vector", lambda e, b3=big, c2=bcol: e.reduce_max(
                tcol[:, 66:67], b3[:, c2:c2 + 512], axis=AX, apply_absolute_value=True),
               waits=[c_gum.at(i_gm), c_rmax.at(rmax0)])
            op("vector", lambda e, m3=mt: e.tensor_tensor(
                tcol[:, 16 + m3:17 + m3], tcol[:, 16 + m3:17 + m3], tcol[:, 66:67], AO.max),
               inc=c_rmax)
            if mt % 2 == 1:
                op("gpsimd", lambda e, b3=big, n2=ns, m3=mt: e.dma_start(
                    out=s_gu.rearrange("(m p) c -> p m c", p=128)[:, m3 - 1:m3 + 1,
                                                                 n2 * 512:(n2 + 1) * 512],
                    in_=b3.rearrange("p (m c) -> p m c", c=512)),
                   waits=[c_gum.at(c_gum.n - 1)], inc=c_guw)
            ngu += 1

    for mt in range(NT):
        gu_rd = k_fm[:, (mt % 2) * 4096: (mt % 2) * 4096 + 4096]
        rw = [c_guw.at(c_guw.n - 1)]
        if mt >= 2:
            rw.append(c_qt.at((mt - 2) * 4 + 3))
        else:
            rw += [c_mmS.now(), c_pv.now(), c_mmax.now()]
        op("sync", lambda e, m3=mt, g2=gu_rd: e.dma_start(
            out=g2, in_=s_gu[m3 * 128:(m3 + 1) * 128, :]), waits=rw, inc=c_gurd)
        scw = [c_rmax.last()]
        if mt >= 1:
            scw.append(c_qt.at((mt - 1) * 4 + 3))
        op("vector", lambda e, m3=mt: e.tensor_scalar(
            tcol[:, 66:67], tcol[:, 16 + m3:17 + m3], 1e-5, None, AO.max), waits=scw)
        op("vector", lambda e: e.reciprocal(tcol[:, 67:68], tcol[:, 66:67]))
        op("vector", lambda e: e.tensor_scalar(tcol[:, 68:69], tcol[:, 67:68], 127.0, None, AO.mult))
        op("vector", lambda e, m3=mt: e.scalar_tensor_tensor(
            tcol[:, 80 + m3:81 + m3], tcol[:, 66:67], 1.0 / 127.0,
            thr_col[:, 11:12], AO.mult, AO.mult),
           waits=[c_thr.now()], inc=c_sgu)
        for qq in range(4):
            qi4 = mt * 4 + qq
            gslice = gu_rd[:, qq * 1024:(qq + 1) * 1024]
            tw = [c_gurd.at(mt), c_sgu.last()]
            if qi4 >= 1:
                tw += [c_y8b.at(qi4 - 1), c_d2.at(qi4 - 1)]
            else:
                tw += [c_gum.now(), c_guw.at(c_guw.n - 1)]
            i_qt = op("gpsimd", lambda e, g2=gslice: e.tensor_scalar(
                gub2_t, g2, tcol[:, 68:69], None, AO.mult), waits=tw, inc=c_qt)
            yw = [c_qt.at(i_qt)]
            yw += [c_guq.at(qi4 - 1)] if qi4 >= 1 else [c_xnq.at(NT - 1)]
            i_y8 = op("scalar", lambda e: e.activation(y8_t, gub2_t, AF.Copy),
                      waits=yw, inc=c_y8b)
            dw = [c_y8b.at(i_y8)]
            dw += [c_c1b.at(qi4 - 1), c_m2b.at(qi4 - 1)] if qi4 >= 1 else \
                [c_guw.at(c_guw.n - 1)]
            i_d2 = op("vector", lambda e: e.tensor_tensor(wprep_t, gub2_t, y8_t, AO.subtract),
                      waits=dw, inc=c_d2)
            c1w = [c_d2.at(i_d2)]
            c1w += [c_m2b.at(qi4 - 1)] if qi4 >= 1 else [c_wq.last(), c_m2.at(NT - 1)]
            i_c1 = op("gpsimd", lambda e: e.tensor_scalar(
                c1_t[:, 0:1024], wprep_t, 0.5, None, AO.is_ge), waits=c1w, inc=c_c1b)
            m2w = [c_c1b.at(i_c1)]
            m2w += [c_guq.at(qi4 - 1)] if qi4 >= 1 else [c_xnq.at(NT - 1)]
            i_m2 = op("vector", lambda e: e.scalar_tensor_tensor(
                m2_t[:, 0:1024], wprep_t, -0.5, c1_t[:, 0:1024], AO.is_le, AO.subtract),
                waits=m2w, inc=c_m2b)
            guq_w = k_fm[:, 8192 + (qi4 % 2) * 1024: 8192 + (qi4 % 2) * 1024 + 1024]
            gqw = [c_m2b.at(i_m2)]
            gqw += [c_guqw.at(qi4 - 2)] if qi4 >= 2 else [c_mmS.now(), c_pv.now()]
            i_gq = op("vector", lambda e, g3=guq_w: e.scalar_tensor_tensor(
                g3, y8_t, 0.0, m2_t[:, 0:1024], AO.add, AO.subtract), waits=gqw, inc=c_guq)
            op("scalar", lambda e, m3=mt, q2=qq, g3=guq_w: e.dma_start(
                out=s_guq[m3 * 128:(m3 + 1) * 128, q2 * 1024:(q2 + 1) * 1024], in_=g3),
               waits=[c_guq.at(i_gq)], inc=c_guqw)

    for qB in range(4):
        for kk in range(32):
            qd = "sync"
            fw = [c_guqw.at(qB * 16 + 15)]
            if qB >= 1:
                fw.append(c_dnmm.at(qB - 1))
            else:
                fw += [c_mmax.now(), c_mmS.now(), c_pv.now(), c_qke.now()]
            op(qd, lambda e, q2=qB, k2=kk: e.dma_start(
                out=q_fm[:, k2 * 512:(k2 + 1) * 512],
                in_=s_guq[q2 * 512:(q2 + 1) * 512, k2 * 128:(k2 + 1) * 128], transpose=True),
               waits=fw, inc=c_guqfm)
        base_dn = c_dnmm.n
        for kk in range(32):
            wsl = 8192 + (kk % 2) * 1024
            ww = []
            if kk >= 2:
                ww.append(c_dnk.at(qB * 31 + kk - 2))
            elif qB >= 1:
                ww.append(c_dnk.at((qB - 1) * 31 + 30) if kk == 0 else c_dnmm.at(qB - 1))
            else:
                ww += [c_pv.now()]
            ww.append(c_wqw.at(wqw_end[2]))
            i_wd = op("sync",
                      lambda e, k2=kk, w2=wsl: e.dma_start(
                          out=v_tm[:, w2:w2 + 1024],
                          in_=s_wd[:, k2 * 128:(k2 + 1) * 128], transpose=True),
                      waits=ww, inc=c_wdt)
            for mt4 in range(4):
                for hf in range(2):
                    bi = mt4 * 2 + hf
                    bw = banks[bi].acquire() if kk == 0 else []
                    last_of_kk = (mt4 == 3 and hf == 1)
                    op("tensor", MM(pb[bi][0:128, 0:512],
                                    q_fm[:, kk * 512 + mt4 * 128: kk * 512 + (mt4 + 1) * 128],
                                    v_tm[:, wsl + hf * 512: wsl + (hf + 1) * 512],
                                    kk == 0, kk == 31),
                       waits=([c_guqfm.at(qB * 32 + kk), c_wdt.at(i_wd)]
                              if (mt4 == 0 and hf == 0) else []) + bw,
                       inc=(c_dnk if (last_of_kk and kk < 31) else
                            (c_dnmm if (last_of_kk and kk == 31) else None)))
        i_dn_all = c_dnmm.n - 1
        for mt4 in range(4):
            mt = qB * 4 + mt4
            b = mt4 % 2
            aw = [c_accw.at(mt)]
            if mt >= 2:
                aw.append(c_fin.at((mt - 2) * 2 + 1))
            else:
                aw.append(c_acce.at(2 * NT - 1))
                aw.append(c_x2.at(NT - 1))
            op("sync", lambda e, m3=mt, bb=b: e.dma_start(
                out=x_t[bb], in_=s_acc[m3 * 128:(m3 + 1) * 128, :]), waits=aw, inc=c_acr)
            for hf in range(2):
                bi = mt4 * 2 + hf
                fw2 = [c_dnmm.at(i_dn_all), c_acr.at(mt)]
                if mt >= 2:
                    fw2.append(c_out.at(c_out.n - 2))
                else:
                    fw2.append(c_accw.at(NT - 1))
                i_fin = op("vector", lambda e, b2=bi, bb=b, m3=mt, h2=hf: e.scalar_tensor_tensor(
                    xn_t[bb][:, h2 * 512:(h2 + 1) * 512], pb[b2][0:128, 0:512],
                    tcol[:, 80 + m3:81 + m3],
                    x_t[bb][:, h2 * 512:(h2 + 1) * 512], AO.mult, AO.add),
                    waits=fw2, inc=c_fin)
                banks[bi].consumed(c_fin, i_fin)
            op("sync", lambda e, m3=mt, bb=b: e.dma_start(
                out=y_h[m3 * 128:(m3 + 1) * 128, :], in_=xn_t[bb]),
               waits=[c_fin.at(mt * 2 + 1)], inc=c_out)

    if debug:
        op("sync", lambda e: e.dma_start(out=d_tc, in_=tcol), waits=[c_sgu.now()], inc=c_out)
    with nc.Block() as block:
        prog.realize(block, [c_out.now()])
    es.close()
    return nc


def kernel(L0_attn_x, L0_attn_in_proj_w, L0_attn_in_proj_b, L0_attn_out_w,
           L0_attn_out_b, L0_ffn_w_gate, L0_ffn_w_up, L0_ffn_w_down):
    x = np.ascontiguousarray(np.asarray(L0_attn_x, dtype=np.float32))
    ctab, mask_e, mask_o, identm8 = host_consts()
    base = dict(
        w_in=np.ascontiguousarray(np.asarray(L0_attn_in_proj_w, np.float32)),
        b_in=np.ascontiguousarray(np.asarray(L0_attn_in_proj_b, np.float32)).reshape(1, D3),
        w_o=np.ascontiguousarray(np.asarray(L0_attn_out_w, np.float32)),
        b_o=np.ascontiguousarray(np.asarray(L0_attn_out_b, np.float32)).reshape(1, D),
        w_g=np.ascontiguousarray(np.asarray(L0_ffn_w_gate, np.float32)),
        w_u=np.ascontiguousarray(np.asarray(L0_ffn_w_up, np.float32)),
        w_d=np.ascontiguousarray(np.asarray(L0_ffn_w_down, np.float32)),
        ctab=ctab, mask_e=mask_e, mask_o=mask_o, identm8=identm8,
    )
    nc = build("full")
    in_maps = [dict(base, x=np.ascontiguousarray(x[b])) for b in range(8)]
    res = run_bass_kernel_spmd(nc, in_maps, core_ids=list(range(8)))
    out = np.stack([np.asarray(r["y"], np.float32) for r in res.results], axis=0)
    return out


# version 02
# speedup vs baseline: 1.0048x; 1.0048x over previous
import numpy as np
import ml_dtypes
import concourse.bass as bass
import concourse.mybir as mybir
from concourse.bass_utils import run_bass_kernel_spmd
from concourse.alu_op_type import AluOpType as AO
from contextlib import ExitStack

F32 = mybir.dt.float32
BF16 = mybir.dt.bfloat16
I8 = mybir.dt.int8
AF = mybir.ActivationFunctionType
AX = mybir.AxisListType.X

T, D, H, HD, D3, INNER = 2048, 1024, 16, 64, 3072, 4096
NT = T // 128
NKK = D // 128
NEG = -1e9
ENGINES = ("sync", "vector", "scalar", "tensor", "gpsimd")


class Chan:
    __slots__ = ("sem", "unit", "n")

    def __init__(self, sem, unit):
        self.sem, self.unit, self.n = sem, unit, 0

    def at(self, i):
        assert 0 <= i < self.n, f"wait on future push ({i} >= {self.n})"
        return (self.sem, (i + 1) * self.unit)

    def now(self):
        return (self.sem, self.n * self.unit) if self.n else None

    def last(self):
        return self.at(self.n - 1)


class Prog:

    CHAINED = ("vector", "scalar", "gpsimd")

    def __init__(self, nc, es, serial=False):
        self.nc, self.es = nc, es
        self.ops = {e: [] for e in ENGINES}
        self.serial = serial
        self.prev = None
        self.eprev = {e: None for e in ENGINES}
        self.fill = {}
        for e in self.CHAINED:
            sem = self.es.enter_context(nc.semaphore(name=f"fill_{e}"))
            self.fill[e] = Chan(sem, 1)
        if serial:
            self.sfill = Chan(self.es.enter_context(nc.semaphore(name="sfill")), 1)

    def chan(self, unit=1):
        sem = self.es.enter_context(self.nc.semaphore())
        return Chan(sem, unit)

    def op(self, eng, fn, waits=(), inc=None):
        if inc is None and (self.serial or eng in self.CHAINED):
            inc = self.sfill if self.serial else self.fill[eng]
        idx = None
        tgt = None
        if inc is not None:
            idx = inc.n
            inc.n += 1
            tgt = (inc.sem, inc.unit)
        ws = tuple(w for w in waits if w is not None)
        if self.serial:
            if self.prev is not None:
                ws = ws + (self.prev,)
            self.prev = (inc.sem, inc.n * inc.unit)
        elif eng in self.CHAINED:
            if self.eprev[eng] is not None:
                ws = ws + (self.eprev[eng],)
            self.eprev[eng] = (inc.sem, inc.n * inc.unit)
        self.ops[eng].append((fn, ws, tgt))
        return idx

    def realize(self, block, final_waits):
        def mk(name):
            ops = self.ops[name]

            def body(eng):
                guaranteed = {}
                for fn, waits, tgt in ops:
                    for (sem, val) in waits:
                        k = id(sem)
                        if guaranteed.get(k, -1) >= val:
                            continue
                        eng.wait_ge(sem, val)
                        guaranteed[k] = val
                    ins = fn(eng)
                    if tgt is not None:
                        ins.then_inc(tgt[0], tgt[1])
                if name == "sync":
                    for w in final_waits:
                        if w is not None:
                            eng.wait_ge(w[0], w[1])
            return body

        block.sync(mk("sync"))
        block.vector(mk("vector"))
        block.scalar(mk("scalar"))
        block.tensor(mk("tensor"))
        block.gpsimd(mk("gpsimd"))


class PBank:
    def __init__(self, ap):
        self.ap = ap
        self.last = []

    def acquire(self):
        w = self.last
        self.last = []
        return list(w)

    def consumed(self, chan, idx):
        self.last.append(chan.at(idx))


def host_consts():
    j = np.arange(0, HD, 2, dtype=np.float64)
    inv_freq = 1.0 / (10000.0 ** (j / HD))
    c = np.arange(32)
    gidx = (2 * c) % 32
    pos = np.arange(T, dtype=np.float64)
    theta = pos[:, None] * inv_freq[gidx][None, :]
    cos, sin = np.cos(theta), np.sin(theta)
    Cfull = np.zeros((T, D), dtype=np.float64)
    S2full = np.zeros((T, D), dtype=np.float64)
    for h in range(H):
        b = h * HD
        Cfull[:, b + 0:b + HD:2] = cos
        Cfull[:, b + 1:b + HD:2] = cos
        S2full[:, b + 0:b + HD:2] = -sin
        S2full[:, b + 1:b + HD:2] = sin
    ctab = np.concatenate([Cfull, S2full], axis=1).astype(np.float32)

    kp = np.arange(128)[:, None]
    qf = np.arange(128)[None, :]
    m = np.zeros((128, 384), dtype=np.float32)
    m[:, 0:128] = (kp <= qf)
    m[:, 128:256] = 1.0
    m[:, 256:384] = (qf <= kp)
    mask_o = m.astype(ml_dtypes.bfloat16)
    me = m.copy()
    me[0, :] = 0.0
    mask_e = me.astype(ml_dtypes.bfloat16)
    identm8 = np.eye(128).astype(np.float32)
    return ctab, mask_e, mask_o, identm8


def build(upto="full", debug=False, serial=False):
    nc = bass.Bass("TRN2", target_bir_lowering=False)
    es = ExitStack()
    stage = {"p1": 1, "p2": 2, "p3": 3, "p4": 4, "full": 6}[upto]

    dn = lambda n, s, dt, k: nc.dram_tensor(n, s, dt, kind=k).ap()
    x_h = dn("x", [T, D], F32, "ExternalInput")
    w_in = dn("w_in", [D3, D], F32, "ExternalInput")
    b_in = dn("b_in", [1, D3], F32, "ExternalInput")
    w_o = dn("w_o", [D, D], F32, "ExternalInput")
    b_o = dn("b_o", [1, D], F32, "ExternalInput")
    w_g = dn("w_g", [INNER, D], F32, "ExternalInput")
    w_u = dn("w_u", [INNER, D], F32, "ExternalInput")
    w_d = dn("w_d", [D, INNER], F32, "ExternalInput")
    ctab_h = dn("ctab", [T, 2 * D], F32, "ExternalInput")
    mask_e_h = dn("mask_e", [128, 384], BF16, "ExternalInput")
    mask_o_h = dn("mask_o", [128, 384], BF16, "ExternalInput")
    ident_h = dn("identm8", [128, 128], F32, "ExternalInput")
    y_h = dn("y", [T, D], F32, "ExternalOutput")

    sk = "ExternalOutput" if debug else "Internal"
    s_win = dn("s_win", [D3, D], BF16, sk)
    s_wo = dn("s_wo", [D, D], BF16, sk)
    s_wg = dn("s_wg", [INNER, D], BF16, sk)
    s_wu = dn("s_wu", [INNER, D], BF16, sk)
    s_wd = dn("s_wd", [D, INNER], BF16, sk)
    s_xr = dn("s_xr", [T, D], BF16, sk)
    s_xnq = dn("s_xnq", [T, D], BF16, sk)
    s_m = dn("s_m", [H, T], F32, sk)
    s_gu = dn("s_gu", [T, INNER], BF16, sk)
    s_guq = dn("s_guq", [T, INNER], BF16, sk)
    s_acc = dn("s_acc", [T, D], F32, sk)
    d_qfm = dn("d_qfm", [128, 16384], BF16, sk) if debug else None
    d_kfm = dn("d_kfm", [128, 16384], BF16, sk) if debug else None
    d_vtm = dn("d_vtm", [128, 16640], BF16, sk) if debug else None
    d_ofm = dn("d_ofm", [128, 16384], BF16, sk) if debug else None
    d_tc = dn("d_tc", [128, 96], F32, sk) if debug else None
    d_xn = dn("d_xn", [T, D], F32, sk) if debug else None
    d_pg = dn("d_pg", [8, 2048], BF16, sk) if debug else None
    d_vg = dn("d_vg", [8, 1040], BF16, sk) if debug else None
    d_m8 = dn("d_m8", [1, 2048], BF16, sk) if debug else None
    d_g = dn("d_g", [T, INNER], F32, sk) if debug else None
    d_u = dn("d_u", [T, INNER], F32, sk) if debug else None
    d_ws = dn("d_ws", [128, 12288], BF16, sk) if debug else None
    d_sc = dn("d_sc", [128, 32], F32, sk) if debug else None
    d_scall = dn("d_scall", [16, 16], F32, sk) if debug else None

    def sbuf(name, shape, dt):
        return es.enter_context(nc.sbuf_tensor(name, shape, dt)).ap()

    x_t = [sbuf(f"xt{i}", [128, 1024], F32) for i in range(2)]
    xn_t = [sbuf(f"xnt{i}", [128, 1024], F32) for i in range(2)]
    ctab_t = sbuf("ctab_t", [128, 1024], F32)
    d_t = sbuf("d_t", [128, 1024], F32)
    c1_t = sbuf("c1_t", [128, 2048], BF16)
    m2_t = sbuf("m2_t", [128, 2048], BF16)
    wprep_t = sbuf("wprep_t", [128, 1024], F32)
    gub2_t = sbuf("gub2_t", [128, 1024], F32)
    wq_t = sbuf("wq_t", [128, 1024], BF16)
    b_bias = [sbuf(f"bb{i}", [1, 512], F32) for i in range(2)]
    ident_s = sbuf("ident_s", [128, 128], F32)
    ones_f = sbuf("ones_f", [128, 128], F32)
    ones_b = sbuf("ones_b", [128, 128], BF16)
    tcol = sbuf("tcol", [128, 96], F32)
    scols = [sbuf(f"scol{i}", [128, 16], F32) for i in range(2)]
    scal = tcol[0:1, 70:79]
    thr_col = sbuf("thr_col", [128, 12], F32)
    b_cols = sbuf("b_cols", [128, 16], F32)
    mcols = [b_cols, scols[1]]
    y8_t = sbuf("y8_t", [128, 1024], I8)
    q_fm = sbuf("q_fm", [128, 16384], BF16)
    k_fm = sbuf("k_fm", [128, 16384], BF16)
    v_tm = sbuf("v_tm", [128, 16640], BF16)
    big2 = sbuf("big2", [128, 16384], BF16)
    wstr = sbuf("wstr", [128, 12288], BF16)
    b7 = sbuf("b7", [128, 3072], BF16)
    v_g = wstr[0:8, 0:1040]
    mask_os = b7[:, 0:384]
    mask_es = mask_os
    pt_sb = [wq_t[:, 0:384], wq_t[:, 384:768]]
    guev = [ctab_t[:, 0:512], ctab_t[:, 512:1024], d_t[:, 0:512], d_t[:, 512:1024]]
    mtmp = d_t
    l_rows = x_t
    gu_big = [wprep_t, gub2_t]

    pb = [es.enter_context(nc.psum_tensor(f"pb{i}", [128, 512], F32)).ap() for i in range(8)]

    prog = Prog(nc, es, serial=serial)
    op = prog.op
    banks = [PBank(p) for p in pb]

    def MM(o, l, r, s, p):
        return lambda e, o=o, l=l, r=r, s=s, p=p: e.matmul(
            o, l, r, start=s, stop=p, skip_group_check=True)

    def mm_group(bi, mms, waits, inc):
        bw = banks[bi].acquire()
        n = len(mms)
        idx = None
        for i2, (o, l, r) in enumerate(mms):
            idx = op("tensor", MM(o, l, r, i2 == 0, i2 == n - 1),
                     waits=(list(waits) + bw) if i2 == 0 else (),
                     inc=inc if i2 == n - 1 else None)
        return idx

    ch = lambda u=1: prog.chan(u)
    c_init, c_mset, c_mskld = ch(16), ch(), ch(16)
    c_bias, c_bgrp, c_bcol = ch(16), ch(), ch()
    c_x, c_ct = ch(16), ch(16)
    c_v1, c_xsq, c_ssq, c_var, c_rstd, c_xn, c_sq2 = ch(), ch(), ch(), ch(), ch(), ch(), ch()
    c_sx, c_deqx, c_y8, c_dq, c_c1, c_m2, c_xnq = ch(), ch(), ch(), ch(), ch(), ch(), ch()
    c_xnqw, c_t1, c_t2, c_xr, c_xrw, c_xrfm = ch(16), ch(), ch(), ch(), ch(16), ch(16)
    c_wld, c_wcast, c_wabs, c_wsum, c_wmm, c_thr = ch(16), ch(16), ch(), ch(), ch(), ch()
    c_wld2, c_wc1, c_wq, c_wqw = ch(16), ch(), ch(), ch(16)
    c_wqk, c_mm2a, c_qke = ch(16), ch(), ch()
    c_wv, c_mm2b, c_ve, c_mmvg, c_vg = ch(16), ch(), ch(), ch(), ch()
    c_mmax, c_mred, c_trm, c_mtr, c_mwr, c_m8 = ch(), ch(), ch(), ch(), ch(16), ch(16)
    c_mmg, c_pg, c_mmS, c_ms, c_expS, c_mask = ch(), ch(), ch(), ch(), ch(), ch()
    c_pv, c_oevac, c_lcopy, c_recip, c_bcmm, c_bce, c_onorm = ch(), ch(), ch(), ch(), ch(), ch(), ch()
    c_wot, c_x2, c_mm4, c_acce, c_accw = ch(16), ch(16), ch(), ch(), ch(16)
    c_xnqfm, c_wgu, c_wguu, c_mmg5, c_ge, c_mmu5, c_ue = ch(16), ch(16), ch(16), ch(), ch(), ch(), ch()
    c_gum, c_rmax, c_guw = ch(), ch(), ch(16)
    c_gurd, c_sgu, c_qt, c_y8b, c_d2, c_c1b, c_m2b, c_guq, c_guqw = \
        ch(16), ch(), ch(), ch(), ch(), ch(), ch(), ch(), ch(16)
    c_guqfm, c_wdt, c_dnk, c_dnmm, c_acr, c_fin, c_out = ch(16), ch(16), ch(), ch(), ch(16), ch(), ch(16)
    c_deqg = ch()
    c_dbg = ch(16)

    op("sync", lambda e: e.dma_start(out=ident_s, in_=ident_h), inc=c_init)
    op("vector", lambda e: e.memset(ones_f, 1.0), inc=c_mset)
    op("vector", lambda e: e.memset(ones_b, 1.0), inc=c_mset)
    op("vector", lambda e: e.memset(ones_b[32:33, :], -1.0))
    op("vector", lambda e: e.memset(ones_b[64:65, :], -1.0))
    op("vector", lambda e: e.memset(
        v_tm.rearrange("p (t c) -> p t c", c=65)[:, :, 64:65], 1.0), inc=c_mset)
    op("vector", lambda e: e.memset(tcol[:, 69:70], 1e-5), inc=c_mset)

    bw6 = banks[6].acquire()
    for q4 in range(4):
        bb = b_bias[q4 % 2]
        lw = [c_bgrp.at(q4 - 2)] if q4 >= 2 else []
        op("sync", lambda e, b2=bb, qq=q4: e.dma_start(
            out=b2, in_=b_in[:, qq * 512:(qq + 1) * 512]), waits=lw, inc=c_bias)
        for m in range(4):
            op("tensor", MM(pb[6][:, q4 * 4 + m:q4 * 4 + m + 1],
                            bb[0:1, m * 128:(m + 1) * 128],
                            ones_f[0:1, 0:1], q4 == 0 and m == 0, False),
               waits=[c_bias.at(q4), c_mset.at(0)] + (bw6 if (q4 == 0 and m == 0) else []),
               inc=(c_bgrp if m == 3 else None))
    i = op("scalar", lambda e: e.copy(b_cols, pb[6][:, 0:16]), waits=[c_bgrp.at(3)], inc=c_bcol)
    banks[6].consumed(c_bcol, i)

    xr_slot = b7[:, 0:1024]
    xnq_slot = b7[:, 1024:2048]
    t2_slot = b7[:, 2048:3072]
    for i in range(NT):
        b = i % 2
        sc = scols[b]
        xw = [c_xn.at(i - 2), c_ssq.at(i - 2)] if i >= 2 else []
        op("sync", lambda e, ii=i, bb=b: e.dma_start(
            out=x_t[bb], in_=x_h[ii * 128:(ii + 1) * 128, :]), waits=xw, inc=c_x)
        cw = [c_t2.at(i - 1)] if i >= 1 else []
        op("sync", lambda e, ii=i: e.dma_start(
            out=ctab_t, in_=ctab_h[ii * 128:(ii + 1) * 128, 0:1024]), waits=cw, inc=c_ct)
        sqw = [c_xr.at(i - 1)] if i >= 1 else []
        vw = [c_y8.at(i - 2)] if i >= 2 else []
        op("scalar", lambda e, bb=b: e.activation(d_t, x_t[bb], AF.Square),
           waits=[c_x.at(i)] + sqw + vw, inc=c_sq2)
        op("vector", lambda e, bb=b, s=sc: e.reduce_sum(s[:, 0:1], x_t[bb], axis=AX),
           waits=[c_x.at(i)] + vw, inc=c_v1)
        op("vector", lambda e, s=sc: e.reduce_sum(s[:, 1:2], d_t, axis=AX),
           waits=[c_sq2.at(i)], inc=c_ssq)
        op("vector", lambda e, s=sc: e.tensor_scalar(s[:, 2:3], s[:, 0:1], 1.0 / D, None, AO.mult),
           waits=[c_v1.at(i)])
        op("vector", lambda e, s=sc: e.tensor_tensor(s[:, 3:4], s[:, 2:3], s[:, 2:3], AO.mult))
        op("vector", lambda e, s=sc: e.scalar_tensor_tensor(
            s[:, 4:5], s[:, 1:2], 1.0 / D, s[:, 3:4], AO.mult, AO.subtract),
           waits=[c_ssq.at(i)], inc=c_var)
        op("scalar", lambda e, s=sc: e.activation(s[:, 5:6], s[:, 4:5], AF.Ln, bias=tcol[:, 69:70]),
           waits=[c_var.at(i), c_mset.at(3)])
        op("scalar", lambda e, s=sc: e.activation(s[:, 6:7], s[:, 5:6], AF.Exp, scale=-0.5),
           inc=c_rstd)
        xnw = [c_y8.at(i - 2), c_t1.at(i - 2), c_t2.at(i - 2)] if i >= 2 else []
        op("vector", lambda e, bb=b, s=sc: e.tensor_scalar(
            xn_t[bb], x_t[bb], s[:, 2:3], s[:, 6:7], AO.subtract, AO.mult),
           waits=[c_rstd.at(i)] + xnw, inc=c_xn)
        if debug:
            op("scalar", lambda e, ii=i, bb=b: e.dma_start(
                out=d_xn[ii * 128:(ii + 1) * 128, :], in_=xn_t[bb]),
               waits=[c_xn.at(i)], inc=c_dbg)
            op("scalar", lambda e, ii=i, s=sc: e.dma_start(
                out=d_scall[ii:ii + 1, :], in_=s[0:1, 0:16]),
               waits=[c_xn.at(i)], inc=c_dbg)
        op("vector", lambda e, bb=b, s=sc: e.reduce_max(
            s[:, 7:8], xn_t[bb], axis=AX, apply_absolute_value=True), waits=[c_xn.at(i)])
        op("vector", lambda e, s=sc: e.tensor_scalar(s[:, 8:9], s[:, 7:8], 1e-5, None, AO.max))
        op("vector", lambda e, s=sc: e.reciprocal(s[:, 9:10], s[:, 8:9]))
        op("vector", lambda e, s=sc: e.tensor_scalar(s[:, 10:11], s[:, 9:10], 127.0, None, AO.mult),
           inc=c_sx)
        op("vector", lambda e, ii=i, s=sc: e.tensor_scalar(
            tcol[:, ii:ii + 1], s[:, 8:9], 1.0 / 127.0, None, AO.mult), inc=c_deqx)
        y8w = [c_xnq.at(i - 1)] if i >= 1 else []
        op("scalar", lambda e, bb=b, s=sc: e.activation(
            y8_t, xn_t[bb], AF.Copy, scale=s[:, 10:11]),
           waits=[c_sx.at(i)] + y8w, inc=c_y8)
        op("vector", lambda e, bb=b, s=sc: e.scalar_tensor_tensor(
            d_t, xn_t[bb], s[:, 10:11], y8_t, AO.mult, AO.subtract),
           waits=[c_y8.at(i), c_ssq.at(i)], inc=c_dq)
        cg = [c_m2.at(i - 1)] if i >= 1 else []
        op("gpsimd", lambda e: e.tensor_scalar(c1_t[:, 0:1024], d_t, 0.5, None, AO.is_ge),
           waits=[c_dq.at(i)] + cg, inc=c_c1)
        op("vector", lambda e: e.scalar_tensor_tensor(
            m2_t[:, 0:1024], d_t, -0.5, c1_t[:, 0:1024], AO.is_le, AO.subtract),
           waits=[c_c1.at(i)], inc=c_m2)
        xqw = [c_xnqw.at(i - 1)] if i >= 1 else []
        op("vector", lambda e: e.scalar_tensor_tensor(
            xnq_slot, y8_t, 0.0, m2_t[:, 0:1024], AO.add, AO.subtract),
           waits=[c_m2.at(i)] + xqw, inc=c_xnq)
        op("scalar", lambda e, ii=i: e.dma_start(
            out=s_xnq[ii * 128:(ii + 1) * 128, :], in_=xnq_slot),
           waits=[c_xnq.at(i)], inc=c_xnqw)
        op("vector", lambda e, bb=b: e.tensor_tensor(d_t, xn_t[bb], ctab_t, AO.mult),
           waits=[c_ct.at(2 * i), c_m2.at(i)], inc=c_t1)
        op("sync", lambda e, ii=i: e.dma_start(
            out=ctab_t, in_=ctab_h[ii * 128:(ii + 1) * 128, 1024:2048]),
           waits=[c_t1.at(i)], inc=c_ct)
        t2w = [c_xr.at(i - 1)] if i >= 1 else []
        op("gpsimd", lambda e, bb=b: e.tensor_tensor(
            t2_slot.rearrange("p (c two) -> p c two", two=2),
            xn_t[bb].rearrange("p (c two) -> p c two", two=2)[:, :, ::-1],
            ctab_t.rearrange("p (c two) -> p c two", two=2), AO.mult),
           waits=[c_ct.at(2 * i + 1)] + t2w, inc=c_t2)
        xrw = [c_xrw.at(i - 1)] if i >= 1 else []
        op("vector", lambda e: e.tensor_tensor(xr_slot, d_t, t2_slot, AO.add),
           waits=[c_t1.at(i), c_t2.at(i)] + xrw, inc=c_xr)
        op("scalar", lambda e, ii=i: e.dma_start(
            out=s_xr[ii * 128:(ii + 1) * 128, :], in_=xr_slot),
           waits=[c_xr.at(i)], inc=c_xrw)

    wbuf = [wprep_t, gub2_t]
    wbuf_last = [[], []]
    wj = 0
    for j in range(32):
        src_ = w_in[j * 128:(j + 1) * 128, :] if j < 24 else w_o[(j - 24) * 128:(j - 23) * 128, :]
        dst_ = s_win[j * 128:(j + 1) * 128, :] if j < 24 else s_wo[(j - 24) * 128:(j - 23) * 128, :]
        sl = wj % 2
        i_ld = op("sync", lambda e, s2=src_, w=wbuf[sl]: e.dma_start(out=w, in_=s2),
                  waits=wbuf_last[sl], inc=c_wld)
        i_c = op("gpsimd", lambda e, d2=dst_, w=wbuf[sl]: e.dma_start(out=d2, in_=w),
                 waits=[c_wld.at(i_ld)], inc=c_wcast)
        wbuf_last[sl] = [c_wcast.at(i_c)]
        wj += 1

    tern = [(w_g, s_wg, INNER * D), (w_u, s_wu, INNER * D), (w_d, s_wd, D * INNER)]
    wqw_end = []
    for wi, (wsrc, wdst, nelem) in enumerate(tern):
        nchunk = nelem // (128 * 1024)
        ncb = wsrc.shape[1] // 1024
        chunks = [wsrc[(j // ncb) * 128:(j // ncb + 1) * 128,
                       (j % ncb) * 1024:(j % ncb + 1) * 1024] for j in range(nchunk)]
        dchunks = [wdst[(j // ncb) * 128:(j // ncb + 1) * 128,
                        (j % ncb) * 1024:(j % ncb + 1) * 1024] for j in range(nchunk)]
        op("vector", lambda e: e.memset(tcol[:, 64:65], 0.0),
           waits=[c_wsum.now(), c_wmm.now()], inc=c_wsum)
        for j in range(nchunk):
            sl = wj % 2
            i_ld = op("sync", lambda e, f=chunks[j], w=wbuf[sl]: e.dma_start(
                out=w, in_=f), waits=wbuf_last[sl], inc=c_wld)
            i_ab = op("vector", lambda e, w=wbuf[sl]: e.reduce_sum(
                tcol[:, 65:66], w, axis=AX, apply_absolute_value=True),
                waits=[c_wld.at(i_ld), c_wsum.last()], inc=c_wabs)
            wbuf_last[sl] = [c_wabs.at(i_ab)]
            op("vector", lambda e: e.tensor_tensor(
                tcol[:, 64:65], tcol[:, 64:65], tcol[:, 65:66], AO.add),
               waits=[c_wabs.at(i_ab)], inc=c_wsum)
            wj += 1
        bw = banks[7].acquire()
        i_mm = op("tensor", MM(pb[7][0:1, 0:1], tcol[:, 64:65], ones_f[:, 0:1], True, True),
                  waits=[c_wsum.last()] + bw, inc=c_wmm)
        i_t = op("scalar", lambda e, w2=wi: e.copy(scal[0:1, w2:w2 + 1], pb[7][0:1, 0:1]),
                 waits=[c_wmm.at(i_mm)], inc=c_thr)
        banks[7].consumed(c_thr, i_t)
        op("vector", lambda e, w2=wi: e.tensor_scalar(
            scal[0:1, 3 + w2:4 + w2], scal[0:1, w2:w2 + 1], 1.0 / nelem, 1e-5, AO.mult, AO.max),
           waits=[c_thr.at(i_t)], inc=c_thr)
        op("vector", lambda e, w2=wi: e.tensor_scalar(
            scal[0:1, 6 + w2:7 + w2], scal[0:1, 3 + w2:4 + w2], 0.5, None, AO.mult), inc=c_thr)
        bw = banks[7].acquire()
        op("tensor", MM(pb[7][:, 0:1], ones_f[0:1, :], scal[0:1, 6 + wi:7 + wi], True, False),
           waits=[c_thr.last()] + bw)
        i_mm = op("tensor", MM(pb[7][:, 1:2], ones_f[0:1, :], scal[0:1, 3 + wi:4 + wi], False, True),
                  inc=c_wmm)
        i_t = op("scalar", lambda e, w2=wi: e.copy(thr_col[:, 4 * w2:4 * w2 + 2], pb[7][:, 0:2]),
                 waits=[c_wmm.at(i_mm)], inc=c_thr)
        banks[7].consumed(c_thr, i_t)
        op("vector", lambda e, w2=wi: e.tensor_scalar(
            thr_col[:, 4 * w2 + 2:4 * w2 + 3], thr_col[:, 4 * w2:4 * w2 + 1], -1.0, None, AO.mult),
           waits=[c_thr.at(i_t)], inc=c_thr)
        op("vector", lambda e, w2=wi: e.tensor_scalar(
            thr_col[:, 4 * w2 + 3:4 * w2 + 4], thr_col[:, 4 * w2 + 1:4 * w2 + 2], -1.0, None, AO.mult),
           inc=c_thr)
        thr_rdy = c_thr.last()
        for j in range(nchunk):
            sl = wj % 2
            i_ld = op("sync", lambda e, f=chunks[j], w=wbuf[sl]: e.dma_start(
                out=w, in_=f),
                waits=wbuf_last[sl] + [thr_rdy], inc=c_wld2)
            cg = [c_wq.last()] if c_wq.n else []
            i_c1 = op("gpsimd", lambda e, w=wbuf[sl], w2=wi: e.tensor_scalar(
                c1_t[:, 1024:2048], w, thr_col[:, 4 * w2:4 * w2 + 1], None, AO.is_ge),
                waits=[c_wld2.at(i_ld), thr_rdy] + cg +
                      ([c_c1.at(NT - 1)] if (wi == 0 and j == 0) else []),
                inc=c_wc1)
            qw = [c_wqw.at(c_wqw.n - 1)] if c_wqw.n >= 1 else []
            i_wq = op("vector", lambda e, w=wbuf[sl], w2=wi: e.scalar_tensor_tensor(
                wq_t, w, thr_col[:, 4 * w2 + 2:4 * w2 + 3], c1_t[:, 1024:2048],
                AO.is_le, AO.subtract),
                waits=[c_wc1.at(i_c1)] + qw, inc=c_wq)
            wbuf_last[sl] = [c_wq.at(i_wq)]
            op("scalar", lambda e, fd=dchunks[j]: e.dma_start(out=fd, in_=wq_t),
               waits=[c_wq.at(i_wq)], inc=c_wqw)
            wj += 1
        wqw_end.append(c_wqw.n - 1)

    if debug:
        op("scalar", lambda e: e.dma_start(out=d_sc[:, 0:16], in_=scols[0]),
           waits=[c_deqx.at(NT - 2)], inc=c_dbg)
        op("scalar", lambda e: e.dma_start(out=d_sc[:, 16:32], in_=scols[1]),
           waits=[c_deqx.at(NT - 1)], inc=c_dbg)
    op("sync", lambda e: e.dma_start(out=mask_os, in_=mask_o_h),
       waits=[c_xrw.at(NT - 1), c_xnqw.at(NT - 1), c_xr.at(NT - 1)], inc=c_mskld)

    for tokB in range(4):
        for kk in range(NKK):
            qd = "sync"
            op(qd, lambda e, tb=tokB, k2=kk: e.dma_start(
                out=big2[:, k2 * 2048 + tb * 512: k2 * 2048 + (tb + 1) * 512],
                in_=s_xr[tb * 512:(tb + 1) * 512, k2 * 128:(k2 + 1) * 128], transpose=True),
               waits=[c_xrw.at(tokB * 4 + 3)], inc=c_xrfm)

    if stage <= 1:
        with nc.Block() as block:
            prog.realize(block, [c_xnqw.now(), c_xrfm.now(), c_wcast.now(), c_wqw.now(),
                                 c_dbg.now()])
        es.close()
        return nc

    nqk = 0
    for tokB in range(4):
        for mpair in range(8):
            sl = (mpair % 2) * 2048
            grp = tokB * 8 + mpair
            for kk in range(NKK):
                qd = "sync" if kk % 2 == 0 else "scalar"
                jw = [c_wcast.at(11), c_xrfm.at(tokB * 8 + 7)]
                if grp >= 2:
                    jw.append(c_mm2a.at((grp - 2) * 2 + 1))
                op(qd, lambda e, mp=mpair, k2=kk, s2=sl: e.dma_start(
                    out=wstr[:, s2 + k2 * 256: s2 + k2 * 256 + 256],
                    in_=s_win[mp * 256:(mp + 1) * 256, k2 * 128:(k2 + 1) * 128], transpose=True),
                   waits=jw, inc=c_wqk)
            for m2 in range(2):
                m = mpair * 2 + m2
                bi = nqk % 6
                mms = [(pb[bi][0:128, 0:512],
                        wstr[:, sl + kk * 256 + m2 * 128: sl + kk * 256 + (m2 + 1) * 128],
                        big2[:, kk * 2048 + tokB * 512: kk * 2048 + (tokB + 1) * 512])
                       for kk in range(NKK)]
                i_mm = mm_group(bi, mms,
                                waits=[c_wqk.at(grp * 8 + 7), c_xrfm.at(tokB * 8 + 7)],
                                inc=c_mm2a)
                dst = q_fm if m < 8 else k_fm
                coff = (m % 8) * 2048 + tokB * 512
                i_ev = op("vector", lambda e, b2=bi, d2=dst, c2=coff, mm=m: e.tensor_scalar(
                    d2[:, c2:c2 + 512], pb[b2][0:128, 0:512], b_cols[:, mm:mm + 1], None, AO.add),
                    waits=[c_mm2a.at(i_mm), c_bcol.at(0)], inc=c_qke)
                banks[bi].consumed(c_qke, i_ev)
                nqk += 1

    for kk in range(NKK):
        qd = "sync"
        op(qd, lambda e, k2=kk: e.dma_start(
            out=wstr[:, 4096 + k2 * 1024: 4096 + (k2 + 1) * 1024],
            in_=s_win[2048:3072, k2 * 128:(k2 + 1) * 128], transpose=True),
           waits=[c_wcast.at(23), c_mm2a.now()], inc=c_wv)
    for half in range(2):
        op("sync", lambda e, hh=half: e.dma_start(
            out=b_bias[hh], in_=b_in[:, 2048 + hh * 512:2048 + (hh + 1) * 512]),
           waits=[c_bgrp.at(3), c_mm2a.now()], inc=c_bias)
    nv = 0
    vv = v_tm.rearrange("p (t c) -> p t c", c=65)
    vg = v_g.rearrange("p (t c) -> p t c", c=65)
    for mt in range(NT):
        for half in range(2):
            bi = nv % 6
            mms = [(pb[bi][0:128, 0:512],
                    big2[:, kk * 2048 + mt * 128: kk * 2048 + (mt + 1) * 128],
                    wstr[:, 4096 + kk * 1024 + half * 512: 4096 + kk * 1024 + (half + 1) * 512])
                   for kk in range(NKK)]
            mms.append((pb[bi][0:128, 0:512], ones_f[0:1, 0:128], b_bias[half][0:1, 0:512]))
            i_mm = mm_group(bi, mms, waits=[c_wv.at(7), c_bias.at(4 + half), c_xrfm.at(31)], inc=c_mm2b)
            i_ev = op("vector", lambda e, b2=bi, m3=mt, h2=half: e.tensor_copy(
                vv[:, m3 * 16 + h2 * 8:m3 * 16 + (h2 + 1) * 8, 0:64],
                pb[b2][0:128, 0:512].rearrange("p (a b) -> p a b", b=64)),
                waits=[c_mm2b.at(i_mm), c_mset.at(2)], inc=c_ve)
            banks[bi].consumed(c_ve, i_ev)
            nv += 1
    for half in range(2):
        mms = [(pb[6][0:8, 0:512],
                big2[:, kk * 2048: kk * 2048 + 2048:256],
                wstr[:, 4096 + kk * 1024 + half * 512: 4096 + kk * 1024 + (half + 1) * 512])
               for kk in range(NKK)]
        mms.append((pb[6][0:8, 0:512], ones_f[0:1, 0:8], b_bias[half][0:1, 0:512]))
        i_mm = mm_group(6, mms, waits=[c_wv.at(7), c_bias.at(4 + half), c_xrfm.at(31)], inc=c_mmvg)
        i_ev = op("vector", lambda e, h2=half: e.tensor_copy(
            vg[:, h2 * 8:(h2 + 1) * 8, 0:64],
            pb[6][0:8, 0:512].rearrange("p (a b) -> p a b", b=64)),
            waits=[c_mmvg.at(i_mm)], inc=c_vg)
        banks[6].consumed(c_vg, i_ev)
    op("vector", lambda e: e.memset(vg[:, :, 64:65], 1.0),
       waits=[c_wqk.last(), c_vg.at(c_vg.n - 1)], inc=c_vg)

    if debug:
        op("sync", lambda e: e.dma_start(out=d_qfm, in_=q_fm), waits=[c_qke.now()], inc=c_out)
        op("sync", lambda e: e.dma_start(out=d_kfm, in_=k_fm), waits=[c_qke.now()], inc=c_out)
        op("sync", lambda e: e.dma_start(out=d_vtm, in_=v_tm), waits=[c_ve.now(), c_vg.now()], inc=c_out)
    if stage <= 2:
        with nc.Block() as block:
            prog.realize(block, [c_qke.now(), c_ve.now(), c_vg.now(), c_out.now(),
                                 c_xnqw.now(), c_wqw.now()])
        es.close()
        return nc

    nmx = 0
    for h in range(H):
        chk, off = h // 2, 64 * (h % 2)
        mc = mcols[h % 2]
        for qi in range(NT):
            bi = nmx % 2
            qlo, qhi = max(qi - 1, 0), min(qi + 1, NT - 1)
            nq = (qhi - qlo + 1) * 128
            bw = banks[bi].acquire()
            mw = [c_qke.now()] + bw
            if qi == 0 and h >= 2:
                mw.append(c_trm.at(h - 2))
            op("tensor", MM(pb[bi][0:128, 0:nq],
                            q_fm[off:off + 64, chk * 2048 + qi * 128: chk * 2048 + (qi + 1) * 128],
                            k_fm[off:off + 64, chk * 2048 + qlo * 128: chk * 2048 + qlo * 128 + nq],
                            True, False), waits=mw)
            i_mm = op("tensor", MM(pb[bi][0:128, nq:nq + 8],
                                   q_fm[off:off + 64, chk * 2048 + qi * 128: chk * 2048 + (qi + 1) * 128],
                                   k_fm[off:off + 64, chk * 2048: chk * 2048 + 2048:256],
                                   False, True), inc=c_mmax)
            rdw = [c_mmax.at(i_mm)]
            if h < 2 and qi == 0:
                rdw += [c_qke.now()] if h == 0 else [c_y8.at(NT - 1), c_sx.at(NT - 1)]
            i_rd = op("vector", lambda e, b2=bi, q2=qi, n2=nq, m3=mc: e.reduce_max(
                m3[:, q2:q2 + 1], pb[b2][0:128, 0:n2 + 8], axis=AX),
                waits=rdw, inc=c_mred)
            banks[bi].consumed(c_mred, i_rd)
            nmx += 1
        bw = banks[6].acquire()
        i_tr = op("tensor", lambda e, m3=mc: e.transpose(pb[6][0:16, 0:128], m3, ident_s),
                  waits=[c_mred.last(), c_init.at(0)] + bw, inc=c_trm)
        i_ev = op("scalar", lambda e: e.copy(mtmp[0:16, 0:128], pb[6][0:16, 0:128]),
                  waits=[c_trm.at(i_tr), c_mwr.now(), c_xr.at(NT - 1)], inc=c_mtr)
        banks[6].consumed(c_mtr, i_ev)
        op("scalar", lambda e, h2=h: e.dma_start(
            out=s_m[h2:h2 + 1, :].rearrange("a (q c) -> (a q) c", c=128),
            in_=mtmp[0:16, 0:128]),
           waits=[c_mtr.at(i_ev)], inc=c_mwr)

    m8row = [wstr[32:33, 2048:4096], wstr[64:65, 2048:4096]]
    P_g = wstr[0:8, 5120:7168]
    linv_bc = wstr[:, 4608:5120]
    last_m8 = [[], []]
    last_lrow = [[], []]
    last_pg = []
    pt_last = [None, None]

    for h in range(H):
        par = h % 2
        chk, off = h // 2, 64 * (h % 2)
        row = 64 * (1 - par)
        mw = [c_mwr.at(h)] + last_m8[par]
        if h < 2:
            mw.append(c_mm2a.now())
        op("gpsimd", lambda e, h2=h, p2=par: e.dma_start(out=m8row[p2], in_=s_m[h2:h2 + 1, :]),
           waits=mw, inc=c_m8)
        i_m8 = c_m8.n - 1
        last_m8[par] = []
        for qB in range(4):
            bw = banks[2].acquire()
            op("tensor", MM(pb[2][0:8, 0:512],
                            k_fm[off:off + 64, chk * 2048: chk * 2048 + 2048:256],
                            q_fm[off:off + 64, chk * 2048 + qB * 512: chk * 2048 + (qB + 1) * 512],
                            True, False),
               waits=[c_m8.at(i_m8), c_qke.now()] + bw)
            orow = 32 * (par + 1)
            i_mm = op("tensor", MM(pb[2][0:8, 0:512], ones_b[orow:orow + 1, 0:8],
                                   m8row[par][0:1, qB * 512:(qB + 1) * 512], False, True),
                      inc=c_mmg)
            last_m8[par] = [c_mmg.at(i_mm)]
            pw = [c_mmg.at(i_mm)]
            if qB == 0:
                pw += last_pg
                if h == 0:
                    pw += [c_mm2a.now(), c_mm2b.now(), c_mmvg.now()]
            i_pg = op("scalar", lambda e, q2=qB: e.activation(
                P_g[0:8, q2 * 512:(q2 + 1) * 512], pb[2][0:8, 0:512], AF.Exp, scale=0.125),
                waits=pw, inc=c_pg)
            banks[2].consumed(c_pg, i_pg)
        pg_base = c_pg.n - 4
        if debug and h == 0:
            op("scalar", lambda e: e.dma_start(out=d_pg, in_=P_g),
               waits=[c_pg.at(pg_base + 3)], inc=c_dbg)
            op("scalar", lambda e: e.dma_start(out=d_vg, in_=v_g),
               waits=[c_vg.now()], inc=c_dbg)
            op("scalar", lambda e, p2=par: e.dma_start(out=d_m8, in_=m8row[p2]),
               waits=[c_m8.at(i_m8)], inc=c_dbg)
        last_pg = []
        first_lcopy = True
        for kj in range(NT):
            bi = kj % 2
            qlo, qhi = max(kj - 1, 0), min(kj + 1, NT - 1)
            nq = (qhi - qlo + 1) * 128
            bw = banks[bi].acquire()
            op("tensor", MM(pb[bi][0:128, 0:nq],
                            k_fm[off:off + 64, chk * 2048 + kj * 128: chk * 2048 + (kj + 1) * 128],
                            q_fm[off:off + 64, chk * 2048 + qlo * 128: chk * 2048 + qlo * 128 + nq],
                            True, False),
               waits=[c_m8.at(i_m8)] + bw)
            orow = 32 * (par + 1)
            i_mm = op("tensor", MM(pb[bi][0:128, 0:nq], ones_b[orow:orow + 1, 0:128],
                                   m8row[par][0:1, qlo * 128: qlo * 128 + nq], False, True),
                      inc=c_mmS)
            last_m8[par].append(c_mmS.at(i_mm))
            expw = [c_mmS.at(i_mm)]
            if kj % 2 == 0:
                i_ms = op("vector", lambda e, b2=bi, n2=nq: e.memset(pb[b2][0:1, 0:n2], NEG),
                          waits=[c_mmS.at(i_mm)], inc=c_ms)
                expw.append(c_ms.at(i_ms))
            if pt_last[bi] is not None:
                expw.append(pt_last[bi])
            else:
                expw.append(c_wqw.at(wqw_end[2]))
            i_exp = op("scalar", lambda e, b2=bi, n2=nq: e.activation(
                pt_sb[b2][:, 0:n2], pb[b2][0:128, 0:n2], AF.Exp, scale=0.125),
                waits=expw, inc=c_expS)
            banks[bi].consumed(c_expS, i_exp)
            msk = mask_es if kj % 2 == 0 else mask_os
            mslice = msk[:, 128:384] if kj == 0 else (msk[:, 0:256] if kj == NT - 1 else msk[:, 0:384])
            i_msk = op("gpsimd", lambda e, b2=bi, n2=nq, m3=mslice: e.tensor_tensor(
                pt_sb[b2][:, 0:n2], pt_sb[b2][:, 0:n2], m3, AO.mult),
                waits=[c_expS.at(i_exp), c_mskld.at(0)], inc=c_mask)
            for qi in range(qlo, qhi + 1):
                slot = 3 + qi % 3
                first = (kj == max(qi - 1, 0))
                vbase = kj * 1040 + h * 65
                ptc = pt_sb[bi][:, (qi - qlo) * 128:(qi - qlo + 1) * 128]
                wl = [c_mask.at(i_msk), c_ve.now(), c_mset.at(2)]
                if first:
                    wl += banks[slot].acquire()
                if par == 0:
                    op("tensor", MM(pb[slot][0:65, 0:128], v_tm[:, vbase:vbase + 65], ptc,
                                    first, False), waits=wl, inc=c_pv)
                else:
                    op("tensor", MM(pb[slot][64:128, 0:128], v_tm[:, vbase:vbase + 64], ptc,
                                    first, False), waits=wl)
                    op("tensor", MM(pb[slot][0:1, 128:256], v_tm[:, vbase + 64:vbase + 65], ptc,
                                    first, False), inc=c_pv)
                if first:
                    gb = h * 65
                    pgc = P_g[0:8, qi * 128:(qi + 1) * 128]
                    gw = [c_pg.at(pg_base + qi // 4), c_vg.now()]
                    if par == 0:
                        op("tensor", MM(pb[slot][0:65, 0:128], v_g[0:8, gb:gb + 65], pgc,
                                        False, False), waits=gw, inc=c_pv)
                    else:
                        op("tensor", MM(pb[slot][64:128, 0:128], v_g[0:8, gb:gb + 64], pgc,
                                        False, False), waits=gw)
                        op("tensor", MM(pb[slot][0:1, 128:256], v_g[0:8, gb + 64:gb + 65], pgc,
                                        False, False), inc=c_pv)
                    last_pg = [c_pv.at(c_pv.n - 1)]
            pt_last[bi] = c_pv.at(c_pv.n - 1)
            closed = ([kj - 1] if kj >= 1 else []) + ([NT - 1] if kj == NT - 1 else [])
            for qi in closed:
                slot = 3 + qi % 3
                ocol = chk * 2048 + qi * 128
                ow = [c_pv.at(c_pv.n - 1), c_ve.now(), c_vg.now()]
                if h == 0:
                    ow += [c_mm2b.now(), c_mmvg.now()]
                lw = [c_pv.at(c_pv.n - 1)]
                if first_lcopy:
                    lw += last_lrow[par] + [c_xn.at(NT - 1), c_ssq.at(NT - 1)]
                    first_lcopy = False
                if par == 0:
                    i_oe = op("vector", lambda e, s2=slot, o2=ocol: e.tensor_copy(
                        big2[0:64, o2:o2 + 128], pb[s2][0:64, 0:128]), waits=ow, inc=c_oevac)
                    i_lc = op("scalar", lambda e, s2=slot, q2=qi: e.copy(
                        l_rows[q2 // 8][64:65, (q2 % 8) * 128:(q2 % 8 + 1) * 128],
                        pb[s2][64:65, 0:128]),
                        waits=lw, inc=c_lcopy)
                else:
                    i_oe = op("vector", lambda e, s2=slot, o2=ocol: e.tensor_copy(
                        big2[64:128, o2:o2 + 128], pb[s2][64:128, 0:128]), waits=ow, inc=c_oevac)
                    i_lc = op("scalar", lambda e, s2=slot, q2=qi: e.copy(
                        l_rows[q2 // 8][0:1, (q2 % 8) * 128:(q2 % 8 + 1) * 128],
                        pb[s2][0:1, 128:256]),
                        waits=lw, inc=c_lcopy)
                banks[slot].consumed(c_oevac, i_oe)
                banks[slot].consumed(c_lcopy, i_lc)
        for half2 in range(2):
            op("vector", lambda e, r2=row, h3=half2: e.reciprocal(
                l_rows[h3][r2:r2 + 1, :], l_rows[h3][r2:r2 + 1, :]),
               waits=[c_lcopy.last()], inc=c_recip)
        rc_base = c_recip.n - 2
        last_lrow[par] = []
        for qB in range(4):
            bw = banks[2].acquire()
            i_bc = op("tensor", MM(pb[2][off:off + 64, 0:512], ones_f[row:row + 1, 0:64],
                                   l_rows[qB // 2][row:row + 1, (qB % 2) * 512:(qB % 2 + 1) * 512],
                                   True, True),
                      waits=[c_recip.at(rc_base + qB // 2)] + bw, inc=c_bcmm)
            last_lrow[par].append(c_bcmm.at(i_bc))
            bew = [c_bcmm.at(i_bc)]
            if h < 2 and qB == 0:
                bew += [c_mm2b.now(), c_mmvg.now()]
            i_be = op("scalar", lambda e, o2=off: e.copy(
                linv_bc[o2:o2 + 64, 0:512], pb[2][o2:o2 + 64, 0:512]),
                waits=bew, inc=c_bce)
            banks[2].consumed(c_bce, i_be)
            op("gpsimd", lambda e, o2=off, c2=chk, q2=qB: e.tensor_tensor(
                big2[o2:o2 + 64, c2 * 2048 + q2 * 512: c2 * 2048 + (q2 + 1) * 512],
                big2[o2:o2 + 64, c2 * 2048 + q2 * 512: c2 * 2048 + (q2 + 1) * 512],
                linv_bc[o2:o2 + 64, 0:512], AO.mult),
               waits=[c_bce.at(i_be), c_oevac.last()], inc=c_onorm)

    if debug:
        op("sync", lambda e: e.dma_start(out=d_ofm, in_=big2), waits=[c_onorm.now()], inc=c_out)
    if stage <= 3:
        with nc.Block() as block:
            prog.realize(block, [c_onorm.now(), c_out.now(), c_xnqw.now(), c_wqw.now()])
        es.close()
        return nc

    for kk in range(NKK):
        qd = "sync"
        op(qd, lambda e, k2=kk: e.dma_start(
            out=v_tm[:, k2 * 1024:(k2 + 1) * 1024],
            in_=s_wo[:, k2 * 128:(k2 + 1) * 128], transpose=True),
           waits=[c_wcast.at(31), c_pv.now()], inc=c_wot)
    for half in range(2):
        op("sync", lambda e, hh=half: e.dma_start(
            out=b_bias[hh], in_=b_o[:, hh * 512:(hh + 1) * 512]),
           waits=[c_mmvg.now(), c_mm2b.now()], inc=c_bias)
    for mt in range(NT):
        b = mt % 2
        xw = [c_acce.at((mt - 2) * 2 + 1)] if mt >= 2 else \
            [c_xn.at(NT - 1), c_ssq.at(NT - 1), c_bcmm.now(), c_recip.now()]
        op("sync", lambda e, m3=mt, bb=b: e.dma_start(
            out=x_t[bb], in_=x_h[m3 * 128:(m3 + 1) * 128, :]), waits=xw, inc=c_x2)
        for half in range(2):
            bi = (mt * 2 + half) % 4
            mms = [(pb[bi][0:128, 0:512],
                    big2[:, kk * 2048 + mt * 128: kk * 2048 + (mt + 1) * 128],
                    v_tm[:, kk * 1024 + half * 512: kk * 1024 + (half + 1) * 512])
                   for kk in range(NKK)]
            mms.append((pb[bi][0:128, 0:512], ones_f[0:1, 0:128], b_bias[half][0:1, 0:512]))
            i_mm = mm_group(bi, mms, waits=[c_wot.at(7), c_bias.at(6 + half), c_onorm.now()], inc=c_mm4)
            evw = [c_mm4.at(i_mm), c_x2.at(mt)]
            if mt >= 2:
                evw.append(c_accw.at(mt - 2))
            else:
                evw += [c_t1.at(NT - 1), c_t2.at(NT - 1), c_y8.at(NT - 1), c_dq.at(NT - 1)]
            i_ev = op("vector", lambda e, b2=bi, bb=b, h2=half: e.tensor_tensor(
                xn_t[bb][:, h2 * 512:(h2 + 1) * 512], pb[b2][0:128, 0:512],
                x_t[bb][:, h2 * 512:(h2 + 1) * 512], AO.add),
                waits=evw, inc=c_acce)
            banks[bi].consumed(c_acce, i_ev)
        op("sync", lambda e, m3=mt, bb=b: e.dma_start(
            out=s_acc[m3 * 128:(m3 + 1) * 128, :], in_=xn_t[bb]),
           waits=[c_acce.at(mt * 2 + 1)], inc=c_accw)

    if stage <= 4:
        with nc.Block() as block:
            prog.realize(block, [c_accw.now(), c_xnqw.now(), c_wqw.now()])
        es.close()
        return nc

    for tokB in range(4):
        for kk in range(NKK):
            op("sync", lambda e, tb=tokB, k2=kk: e.dma_start(
                out=big2[:, k2 * 2048 + tb * 512: k2 * 2048 + (tb + 1) * 512],
                in_=s_xnq[tb * 512:(tb + 1) * 512, k2 * 128:(k2 + 1) * 128], transpose=True),
               waits=[c_xnqw.at(15), c_mm4.now()], inc=c_xnqfm)
    op("vector", lambda e: e.tensor_scalar(
        tcol[:, 32:48], tcol[:, 0:16], thr_col[:, 3:4], None, AO.mult),
       waits=[c_deqx.at(NT - 1), c_thr.now()], inc=c_deqg)
    op("vector", lambda e: e.tensor_scalar(
        tcol[:, 48:64], tcol[:, 0:16], thr_col[:, 7:8], None, AO.mult), inc=c_deqg)
    op("vector", lambda e: e.memset(tcol[:, 16:32], 0.0), inc=c_rmax)
    rmax0 = c_rmax.n - 1

    ngu = 0
    for ns in range(8):
        gsl = (ns % 2) * 4096
        for kk in range(NKK):
            gw2 = [c_wqw.at(wqw_end[0])]
            if ns >= 2:
                gw2.append(c_mmg5.at((ns - 2) * 16 + 15))
            op("sync", lambda e, n2=ns, k2=kk, s2=gsl: e.dma_start(
                out=wstr[:, s2 + k2 * 512: s2 + (k2 + 1) * 512],
                in_=s_wg[n2 * 512:(n2 + 1) * 512, k2 * 128:(k2 + 1) * 128], transpose=True),
               waits=gw2 + ([c_mmS.now(), c_mmg.now(), c_pv.now(), c_onorm.now()]
                            if ns < 2 else []),
               inc=c_wgu)
        for kk in range(NKK):
            uw = [c_wqw.at(wqw_end[1])]
            if ns >= 1:
                uw.append(c_mmu5.at((ns - 1) * 16 + 15))
            else:
                uw += [c_mm2b.now(), c_mmvg.now(), c_bce.now()]
            op("sync", lambda e, n2=ns, k2=kk: e.dma_start(
                out=wstr[:, 8192 + k2 * 512: 8192 + (k2 + 1) * 512],
                in_=s_wu[n2 * 512:(n2 + 1) * 512, k2 * 128:(k2 + 1) * 128], transpose=True),
               waits=uw, inc=c_wguu)
        wg_rdy = c_wgu.at(ns * 8 + 7)
        wu_rdy = c_wguu.at(ns * 8 + 7)
        for mt in range(NT):
            big = gu_big[(ngu // 2) % 2]
            bcol = (mt % 2) * 512
            bg, bu = (mt % 2) * 2, (mt % 2) * 2 + 1
            mmsg = [(pb[bg][0:128, 0:512],
                     big2[:, kk * 2048 + mt * 128: kk * 2048 + (mt + 1) * 128],
                     wstr[:, gsl + kk * 512: gsl + (kk + 1) * 512]) for kk in range(NKK)]
            i_mmg = mm_group(bg, mmsg, waits=[wg_rdy, c_xnqfm.at(31)], inc=c_mmg5)
            gev = guev[(mt % 2) * 2]
            gevw = [c_mmg5.at(i_mmg), c_deqg.at(0)]
            gevw += [c_gum.at(c_gum.n - 2)] if c_gum.n >= 2 else \
                [c_xr.at(NT - 1), c_mwr.at(H - 1), c_t2.at(NT - 1)]
            i_ge = op("scalar", lambda e, b2=bg, m3=mt, g2=gev: e.activation(
                g2, pb[b2][0:128, 0:512], AF.Silu, scale=tcol[:, 32 + m3:33 + m3]),
                waits=gevw, inc=c_ge)
            banks[bg].consumed(c_ge, i_ge)
            mmsu = [(pb[bu][0:128, 0:512],
                     big2[:, kk * 2048 + mt * 128: kk * 2048 + (mt + 1) * 128],
                     wstr[:, 8192 + kk * 512: 8192 + (kk + 1) * 512]) for kk in range(NKK)]
            i_mmu = mm_group(bu, mmsu, waits=[wu_rdy, c_xnqfm.at(31)], inc=c_mmu5)
            uev = guev[(mt % 2) * 2 + 1]
            uevw = [c_mmu5.at(i_mmu), c_deqg.at(1)]
            uevw += [c_gum.at(c_gum.n - 2)] if c_gum.n >= 2 else \
                [c_xr.at(NT - 1), c_mwr.at(H - 1), c_t2.at(NT - 1)]
            i_ue = op("scalar", lambda e, b2=bu, m3=mt, u2=uev: e.activation(
                u2, pb[b2][0:128, 0:512], AF.Copy, scale=tcol[:, 48 + m3:49 + m3]),
                waits=uevw, inc=c_ue)
            banks[bu].consumed(c_ue, i_ue)
            guw = [c_guw.at((ngu - 4) // 2), c_rmax.at(rmax0 + ngu - 3)] if ngu >= 4 else \
                ([c_wq.last(), c_wc1.last()] if ngu < 2 else [])
            i_gm = op("gpsimd", lambda e, g2=gev, u2=uev, b3=big, c2=bcol: e.tensor_tensor(
                b3[:, c2:c2 + 512], g2, u2, AO.mult),
                waits=[c_ge.at(i_ge), c_ue.at(i_ue)] + guw, inc=c_gum)
            op("vector", lambda e, b3=big, c2=bcol: e.reduce_max(
                tcol[:, 66:67], b3[:, c2:c2 + 512], axis=AX, apply_absolute_value=True),
               waits=[c_gum.at(i_gm), c_rmax.at(rmax0)])
            op("vector", lambda e, m3=mt: e.tensor_tensor(
                tcol[:, 16 + m3:17 + m3], tcol[:, 16 + m3:17 + m3], tcol[:, 66:67], AO.max),
               inc=c_rmax)
            if mt % 2 == 1:
                op("gpsimd", lambda e, b3=big, n2=ns, m3=mt: e.dma_start(
                    out=s_gu.rearrange("(m p) c -> p m c", p=128)[:, m3 - 1:m3 + 1,
                                                                 n2 * 512:(n2 + 1) * 512],
                    in_=b3.rearrange("p (m c) -> p m c", c=512)),
                   waits=[c_gum.at(c_gum.n - 1)], inc=c_guw)
            ngu += 1

    for mt in range(NT):
        gu_rd = k_fm[:, (mt % 2) * 4096: (mt % 2) * 4096 + 4096]
        rw = [c_guw.at(c_guw.n - 1)]
        if mt >= 2:
            rw.append(c_qt.at((mt - 2) * 4 + 3))
        else:
            rw += [c_mmS.now(), c_pv.now(), c_mmax.now()]
        op("sync", lambda e, m3=mt, g2=gu_rd: e.dma_start(
            out=g2, in_=s_gu[m3 * 128:(m3 + 1) * 128, :]), waits=rw, inc=c_gurd)
        scw = [c_rmax.last()]
        if mt >= 1:
            scw.append(c_qt.at((mt - 1) * 4 + 3))
        op("vector", lambda e, m3=mt: e.tensor_scalar(
            tcol[:, 66:67], tcol[:, 16 + m3:17 + m3], 1e-5, None, AO.max), waits=scw)
        op("vector", lambda e: e.reciprocal(tcol[:, 67:68], tcol[:, 66:67]))
        op("vector", lambda e: e.tensor_scalar(tcol[:, 68:69], tcol[:, 67:68], 127.0, None, AO.mult))
        op("vector", lambda e, m3=mt: e.scalar_tensor_tensor(
            tcol[:, 80 + m3:81 + m3], tcol[:, 66:67], 1.0 / 127.0,
            thr_col[:, 11:12], AO.mult, AO.mult),
           waits=[c_thr.now()], inc=c_sgu)
        for qq in range(4):
            qi4 = mt * 4 + qq
            gslice = gu_rd[:, qq * 1024:(qq + 1) * 1024]
            tw = [c_gurd.at(mt), c_sgu.last()]
            if qi4 >= 1:
                tw += [c_y8b.at(qi4 - 1), c_d2.at(qi4 - 1)]
            else:
                tw += [c_gum.now(), c_guw.at(c_guw.n - 1)]
            i_qt = op("gpsimd", lambda e, g2=gslice: e.tensor_scalar(
                gub2_t, g2, tcol[:, 68:69], None, AO.mult), waits=tw, inc=c_qt)
            yw = [c_qt.at(i_qt)]
            yw += [c_guq.at(qi4 - 1)] if qi4 >= 1 else [c_xnq.at(NT - 1)]
            i_y8 = op("scalar", lambda e: e.activation(y8_t, gub2_t, AF.Copy),
                      waits=yw, inc=c_y8b)
            dw = [c_y8b.at(i_y8)]
            dw += [c_c1b.at(qi4 - 1), c_m2b.at(qi4 - 1)] if qi4 >= 1 else \
                [c_guw.at(c_guw.n - 1)]
            i_d2 = op("vector", lambda e: e.tensor_tensor(wprep_t, gub2_t, y8_t, AO.subtract),
                      waits=dw, inc=c_d2)
            c1w = [c_d2.at(i_d2)]
            c1w += [c_m2b.at(qi4 - 1)] if qi4 >= 1 else [c_wq.last(), c_m2.at(NT - 1)]
            i_c1 = op("gpsimd", lambda e: e.tensor_scalar(
                c1_t[:, 0:1024], wprep_t, 0.5, None, AO.is_ge), waits=c1w, inc=c_c1b)
            m2w = [c_c1b.at(i_c1)]
            m2w += [c_guq.at(qi4 - 1)] if qi4 >= 1 else [c_xnq.at(NT - 1)]
            i_m2 = op("vector", lambda e: e.scalar_tensor_tensor(
                m2_t[:, 0:1024], wprep_t, -0.5, c1_t[:, 0:1024], AO.is_le, AO.subtract),
                waits=m2w, inc=c_m2b)
            guq_w = k_fm[:, 8192 + (qi4 % 2) * 1024: 8192 + (qi4 % 2) * 1024 + 1024]
            gqw = [c_m2b.at(i_m2)]
            gqw += [c_guqw.at(qi4 - 2)] if qi4 >= 2 else [c_mmS.now(), c_pv.now()]
            i_gq = op("vector", lambda e, g3=guq_w: e.scalar_tensor_tensor(
                g3, y8_t, 0.0, m2_t[:, 0:1024], AO.add, AO.subtract), waits=gqw, inc=c_guq)
            op("scalar", lambda e, m3=mt, q2=qq, g3=guq_w: e.dma_start(
                out=s_guq[m3 * 128:(m3 + 1) * 128, q2 * 1024:(q2 + 1) * 1024], in_=g3),
               waits=[c_guq.at(i_gq)], inc=c_guqw)

    for qB in range(4):
        for kk in range(32):
            qd = "sync"
            fw = [c_guqw.at(qB * 16 + 15)]
            if qB >= 1:
                fw.append(c_dnmm.at(qB - 1))
            else:
                fw += [c_mmax.now(), c_mmS.now(), c_pv.now(), c_qke.now()]
            op(qd, lambda e, q2=qB, k2=kk: e.dma_start(
                out=q_fm[:, k2 * 512:(k2 + 1) * 512],
                in_=s_guq[q2 * 512:(q2 + 1) * 512, k2 * 128:(k2 + 1) * 128], transpose=True),
               waits=fw, inc=c_guqfm)
        base_dn = c_dnmm.n
        for kk in range(32):
            wsl = 8192 + (kk % 2) * 1024
            ww = []
            if kk >= 2:
                ww.append(c_dnk.at(qB * 31 + kk - 2))
            elif qB >= 1:
                ww.append(c_dnk.at((qB - 1) * 31 + 30) if kk == 0 else c_dnmm.at(qB - 1))
            else:
                ww += [c_pv.now()]
            ww.append(c_wqw.at(wqw_end[2]))
            i_wd = op("sync",
                      lambda e, k2=kk, w2=wsl: e.dma_start(
                          out=v_tm[:, w2:w2 + 1024],
                          in_=s_wd[:, k2 * 128:(k2 + 1) * 128], transpose=True),
                      waits=ww, inc=c_wdt)
            for mt4 in range(4):
                for hf in range(2):
                    bi = mt4 * 2 + hf
                    bw = banks[bi].acquire() if kk == 0 else []
                    last_of_kk = (mt4 == 3 and hf == 1)
                    op("tensor", MM(pb[bi][0:128, 0:512],
                                    q_fm[:, kk * 512 + mt4 * 128: kk * 512 + (mt4 + 1) * 128],
                                    v_tm[:, wsl + hf * 512: wsl + (hf + 1) * 512],
                                    kk == 0, kk == 31),
                       waits=([c_guqfm.at(qB * 32 + kk), c_wdt.at(i_wd)]
                              if (mt4 == 0 and hf == 0) else []) + bw,
                       inc=(c_dnk if (last_of_kk and kk < 31) else
                            (c_dnmm if (last_of_kk and kk == 31) else None)))
        i_dn_all = c_dnmm.n - 1
        for mt4 in range(4):
            mt = qB * 4 + mt4
            b = mt4 % 2
            aw = [c_accw.at(mt)]
            if mt >= 2:
                aw.append(c_fin.at((mt - 2) * 2 + 1))
            else:
                aw.append(c_acce.at(2 * NT - 1))
                aw.append(c_x2.at(NT - 1))
            op("sync", lambda e, m3=mt, bb=b: e.dma_start(
                out=x_t[bb], in_=s_acc[m3 * 128:(m3 + 1) * 128, :]), waits=aw, inc=c_acr)
            for hf in range(2):
                bi = mt4 * 2 + hf
                fw2 = [c_dnmm.at(i_dn_all), c_acr.at(mt)]
                if mt >= 2:
                    fw2.append(c_out.at(c_out.n - 2))
                else:
                    fw2.append(c_accw.at(NT - 1))
                i_fin = op("vector", lambda e, b2=bi, bb=b, m3=mt, h2=hf: e.scalar_tensor_tensor(
                    xn_t[bb][:, h2 * 512:(h2 + 1) * 512], pb[b2][0:128, 0:512],
                    tcol[:, 80 + m3:81 + m3],
                    x_t[bb][:, h2 * 512:(h2 + 1) * 512], AO.mult, AO.add),
                    waits=fw2, inc=c_fin)
                banks[bi].consumed(c_fin, i_fin)
            op("sync", lambda e, m3=mt, bb=b: e.dma_start(
                out=y_h[m3 * 128:(m3 + 1) * 128, :], in_=xn_t[bb]),
               waits=[c_fin.at(mt * 2 + 1)], inc=c_out)

    if debug:
        op("sync", lambda e: e.dma_start(out=d_tc, in_=tcol), waits=[c_sgu.now()], inc=c_out)
    with nc.Block() as block:
        prog.realize(block, [c_out.now()])
    es.close()
    return nc


def kernel(L0_attn_x, L0_attn_in_proj_w, L0_attn_in_proj_b, L0_attn_out_w,
           L0_attn_out_b, L0_ffn_w_gate, L0_ffn_w_up, L0_ffn_w_down):
    x = np.ascontiguousarray(np.asarray(L0_attn_x, dtype=np.float32))
    ctab, mask_e, mask_o, identm8 = host_consts()
    base = dict(
        w_in=np.ascontiguousarray(np.asarray(L0_attn_in_proj_w, np.float32)),
        b_in=np.ascontiguousarray(np.asarray(L0_attn_in_proj_b, np.float32)).reshape(1, D3),
        w_o=np.ascontiguousarray(np.asarray(L0_attn_out_w, np.float32)),
        b_o=np.ascontiguousarray(np.asarray(L0_attn_out_b, np.float32)).reshape(1, D),
        w_g=np.ascontiguousarray(np.asarray(L0_ffn_w_gate, np.float32)),
        w_u=np.ascontiguousarray(np.asarray(L0_ffn_w_up, np.float32)),
        w_d=np.ascontiguousarray(np.asarray(L0_ffn_w_down, np.float32)),
        ctab=ctab, mask_e=mask_e, mask_o=mask_o, identm8=identm8,
    )
    nc = build("full")
    in_maps = [dict(base, x=np.ascontiguousarray(x[b])) for b in range(8)]
    res = run_bass_kernel_spmd(nc, in_maps, core_ids=list(range(8)))
    out = np.stack([np.asarray(r["y"], np.float32) for r in res.results], axis=0)
    return out


# version 03
# speedup vs baseline: 1.4818x; 1.4747x over previous
import numpy as np
import ml_dtypes
import concourse.bass as bass
import concourse.mybir as mybir
from concourse.bass_utils import run_bass_kernel_spmd
from concourse.alu_op_type import AluOpType as AO
from contextlib import ExitStack

F32 = mybir.dt.float32
BF16 = mybir.dt.bfloat16
I8 = mybir.dt.int8
AF = mybir.ActivationFunctionType
AX = mybir.AxisListType.X

T, D, H, HD, D3, INNER = 2048, 1024, 16, 64, 3072, 4096
NT = T // 128
NKK = D // 128
NEG = -1e9
ENGINES = ("sync", "vector", "scalar", "tensor", "gpsimd")


class Chan:
    __slots__ = ("sem", "unit", "n")

    def __init__(self, sem, unit):
        self.sem, self.unit, self.n = sem, unit, 0

    def at(self, i):
        assert 0 <= i < self.n, f"wait on future push ({i} >= {self.n})"
        return (self.sem, (i + 1) * self.unit)

    def now(self):
        return (self.sem, self.n * self.unit) if self.n else None

    def last(self):
        return self.at(self.n - 1)


class Prog:

    CHAINED = ("vector", "scalar", "gpsimd")

    def __init__(self, nc, es, serial=False):
        self.nc, self.es = nc, es
        self.ops = {e: [] for e in ENGINES}
        self.serial = serial
        self.prev = None
        self.eprev = {e: None for e in ENGINES}
        self.fill = {}
        for e in self.CHAINED:
            sem = self.es.enter_context(nc.semaphore(name=f"fill_{e}"))
            self.fill[e] = Chan(sem, 1)
        if serial:
            self.sfill = Chan(self.es.enter_context(nc.semaphore(name="sfill")), 1)

    def chan(self, unit=1):
        sem = self.es.enter_context(self.nc.semaphore())
        return Chan(sem, unit)

    def op(self, eng, fn, waits=(), inc=None):
        if inc is None and (self.serial or eng in self.CHAINED):
            inc = self.sfill if self.serial else self.fill[eng]
        idx = None
        tgt = None
        if inc is not None:
            idx = inc.n
            inc.n += 1
            tgt = (inc.sem, inc.unit)
        ws = tuple(w for w in waits if w is not None)
        if self.serial:
            if self.prev is not None:
                ws = ws + (self.prev,)
            self.prev = (inc.sem, inc.n * inc.unit)
        elif eng in self.CHAINED:
            if self.eprev[eng] is not None:
                ws = ws + (self.eprev[eng],)
            self.eprev[eng] = (inc.sem, inc.n * inc.unit)
        self.ops[eng].append((fn, ws, tgt))
        return idx

    def realize(self, block, final_waits):
        def mk(name):
            ops = self.ops[name]

            def body(eng):
                guaranteed = {}
                for fn, waits, tgt in ops:
                    for (sem, val) in waits:
                        k = id(sem)
                        if guaranteed.get(k, -1) >= val:
                            continue
                        eng.wait_ge(sem, val)
                        guaranteed[k] = val
                    ins = fn(eng)
                    if tgt is not None:
                        ins.then_inc(tgt[0], tgt[1])
                if name == "sync":
                    for w in final_waits:
                        if w is not None:
                            eng.wait_ge(w[0], w[1])
            return body

        block.sync(mk("sync"))
        block.vector(mk("vector"))
        block.scalar(mk("scalar"))
        block.tensor(mk("tensor"))
        block.gpsimd(mk("gpsimd"))


class PBank:
    def __init__(self, ap):
        self.ap = ap
        self.last = []

    def acquire(self):
        w = self.last
        self.last = []
        return list(w)

    def consumed(self, chan, idx):
        self.last.append(chan.at(idx))


def host_consts():
    j = np.arange(0, HD, 2, dtype=np.float64)
    inv_freq = 1.0 / (10000.0 ** (j / HD))
    c = np.arange(32)
    gidx = (2 * c) % 32
    pos = np.arange(T, dtype=np.float64)
    theta = pos[:, None] * inv_freq[gidx][None, :]
    cos, sin = np.cos(theta), np.sin(theta)
    Cfull = np.zeros((T, D), dtype=np.float64)
    S2full = np.zeros((T, D), dtype=np.float64)
    for h in range(H):
        b = h * HD
        Cfull[:, b + 0:b + HD:2] = cos
        Cfull[:, b + 1:b + HD:2] = cos
        S2full[:, b + 0:b + HD:2] = -sin
        S2full[:, b + 1:b + HD:2] = sin
    ctab = np.concatenate([Cfull, S2full], axis=1).astype(np.float32)

    kp = np.arange(128)[:, None]
    qf = np.arange(128)[None, :]
    m = np.zeros((128, 384), dtype=np.float32)
    m[:, 0:128] = (kp <= qf)
    m[:, 128:256] = 1.0
    m[:, 256:384] = (qf <= kp)
    mask_o = m.astype(ml_dtypes.bfloat16)
    me = m.copy()
    me[0, :] = 0.0
    mask_e = me.astype(ml_dtypes.bfloat16)
    identm8 = np.eye(128).astype(np.float32)
    return ctab, mask_e, mask_o, identm8


def build(upto="full", debug=False, serial=False):
    nc = bass.Bass("TRN2", target_bir_lowering=False)
    es = ExitStack()
    stage = {"p1": 1, "p2": 2, "p3": 3, "p4": 4, "full": 6}[upto]

    dn = lambda n, s, dt, k: nc.dram_tensor(n, s, dt, kind=k).ap()
    x_h = dn("x", [T, D], F32, "ExternalInput")
    w_in = dn("w_in", [D3, D], F32, "ExternalInput")
    b_in = dn("b_in", [1, D3], F32, "ExternalInput")
    w_o = dn("w_o", [D, D], F32, "ExternalInput")
    b_o = dn("b_o", [1, D], F32, "ExternalInput")
    w_g = dn("w_g", [INNER, D], F32, "ExternalInput")
    w_u = dn("w_u", [INNER, D], F32, "ExternalInput")
    w_d = dn("w_d", [D, INNER], F32, "ExternalInput")
    ctab_h = dn("ctab", [T, 2 * D], F32, "ExternalInput")
    mask_e_h = dn("mask_e", [128, 384], BF16, "ExternalInput")
    mask_o_h = dn("mask_o", [128, 384], BF16, "ExternalInput")
    ident_h = dn("identm8", [128, 128], F32, "ExternalInput")
    y_h = dn("y", [T, D], F32, "ExternalOutput")

    sk = "ExternalOutput" if debug else "Internal"
    s_win = dn("s_win", [D3, D], BF16, sk)
    s_wo = dn("s_wo", [D, D], BF16, sk)
    s_wg = dn("s_wg", [INNER, D], BF16, sk)
    s_wu = dn("s_wu", [INNER, D], BF16, sk)
    s_wd = dn("s_wd", [D, INNER], BF16, sk)
    s_xr = dn("s_xr", [T, D], BF16, sk)
    s_xnq = dn("s_xnq", [T, D], BF16, sk)
    s_m = dn("s_m", [H, T], F32, sk)
    s_gu = dn("s_gu", [T, INNER], BF16, sk)
    s_guq = dn("s_guq", [T, INNER], BF16, sk)
    s_acc = dn("s_acc", [T, D], F32, sk)
    d_qfm = dn("d_qfm", [128, 16384], BF16, sk) if debug else None
    d_kfm = dn("d_kfm", [128, 16384], BF16, sk) if debug else None
    d_vtm = dn("d_vtm", [128, 16640], BF16, sk) if debug else None
    d_ofm = dn("d_ofm", [128, 16384], BF16, sk) if debug else None
    d_tc = dn("d_tc", [128, 96], F32, sk) if debug else None
    d_xn = dn("d_xn", [T, D], F32, sk) if debug else None
    d_pg = dn("d_pg", [8, 2048], BF16, sk) if debug else None
    d_vg = dn("d_vg", [8, 1040], BF16, sk) if debug else None
    d_m8 = dn("d_m8", [1, 2048], BF16, sk) if debug else None
    d_g = dn("d_g", [T, INNER], F32, sk) if debug else None
    d_u = dn("d_u", [T, INNER], F32, sk) if debug else None
    d_ws = dn("d_ws", [128, 12288], BF16, sk) if debug else None
    d_sc = dn("d_sc", [128, 32], F32, sk) if debug else None
    d_scall = dn("d_scall", [16, 16], F32, sk) if debug else None

    def sbuf(name, shape, dt):
        return es.enter_context(nc.sbuf_tensor(name, shape, dt)).ap()

    x_t = [sbuf(f"xt{i}", [128, 1024], F32) for i in range(2)]
    xn_t = [sbuf(f"xnt{i}", [128, 1024], F32) for i in range(2)]
    ctab_t = sbuf("ctab_t", [128, 1024], F32)
    d_t = sbuf("d_t", [128, 1024], F32)
    c1_t = sbuf("c1_t", [128, 2048], BF16)
    m2_t = sbuf("m2_t", [128, 2048], BF16)
    wprep_t = sbuf("wprep_t", [128, 1024], F32)
    gub2_t = sbuf("gub2_t", [128, 1024], F32)
    wq_t = sbuf("wq_t", [128, 1024], BF16)
    b_bias = [sbuf(f"bb{i}", [1, 512], F32) for i in range(2)]
    ident_s = sbuf("ident_s", [128, 128], F32)
    ones_f = sbuf("ones_f", [128, 128], F32)
    ones_b = sbuf("ones_b", [128, 128], BF16)
    tcol = sbuf("tcol", [128, 96], F32)
    scols = [sbuf(f"scol{i}", [128, 16], F32) for i in range(2)]
    scal = tcol[0:1, 70:79]
    thr_col = sbuf("thr_col", [128, 12], F32)
    b_cols = sbuf("b_cols", [128, 16], F32)
    mcols = [b_cols, scols[1]]
    y8_t = sbuf("y8_t", [128, 1024], I8)
    q_fm = sbuf("q_fm", [128, 16384], BF16)
    k_fm = sbuf("k_fm", [128, 16384], BF16)
    v_tm = sbuf("v_tm", [128, 16640], BF16)
    y8b_t = v_tm[:, 0:1024]
    big2 = sbuf("big2", [128, 16384], BF16)
    wstr = sbuf("wstr", [128, 12288], BF16)
    b7 = sbuf("b7", [128, 3072], BF16)
    v_g = wstr[0:8, 0:1040]
    mask_os = b7[:, 0:384]
    mask_es = mask_os
    pt_sb = [wq_t[:, 0:384], wq_t[:, 384:768]]
    guev = [ctab_t[:, 0:512], ctab_t[:, 512:1024], d_t[:, 0:512], d_t[:, 512:1024]]
    mtmp = d_t
    l_rows = x_t
    gu_big = [wprep_t, gub2_t]

    pb = [es.enter_context(nc.psum_tensor(f"pb{i}", [128, 512], F32)).ap() for i in range(8)]

    prog = Prog(nc, es, serial=serial)
    op = prog.op
    banks = [PBank(p) for p in pb]

    def MM(o, l, r, s, p):
        return lambda e, o=o, l=l, r=r, s=s, p=p: e.matmul(
            o, l, r, start=s, stop=p, skip_group_check=True)

    def mm_group(bi, mms, waits, inc):
        bw = banks[bi].acquire()
        n = len(mms)
        idx = None
        for i2, (o, l, r) in enumerate(mms):
            idx = op("tensor", MM(o, l, r, i2 == 0, i2 == n - 1),
                     waits=(list(waits) + bw) if i2 == 0 else (),
                     inc=inc if i2 == n - 1 else None)
        return idx

    ch = lambda u=1: prog.chan(u)
    c_init, c_mset = ch(16), ch()
    c_bias = ch(16)
    c_x, c_ct = ch(16), ch(16)
    c_v1, c_ssq, c_var, c_rstd, c_xn, c_sq2 = ch(), ch(), ch(), ch(), ch(), ch()
    c_sx, c_deqx, c_y8, c_y8b1, c_dq, c_c1, c_m2, c_xnq = ch(), ch(), ch(), ch(), ch(), ch(), ch(), ch()
    c_xnqw, c_t1, c_t2, c_xr, c_xrw, c_xrfm = ch(16), ch(), ch(), ch(), ch(16), ch(16)
    c_wld, c_wcast, c_wabs, c_wsum, c_wmm, c_thr = ch(16), ch(16), ch(), ch(), ch(), ch()
    c_wld2, c_wc1, c_wq, c_wqw = ch(16), ch(), ch(), ch(16)
    c_wqk, c_mm2a, c_qke = ch(16), ch(), ch()
    c_wv, c_mm2b, c_ve, c_mmvg, c_vg = ch(16), ch(), ch(), ch(), ch()
    c_mmax, c_mred, c_trm, c_mtr, c_mwr, c_m8 = ch(), ch(), ch(), ch(), ch(16), ch(16)
    c_mmg, c_pg, c_mmS, c_ms, c_expS, c_mask = ch(), ch(), ch(), ch(), ch(), ch()
    c_pv, c_oevac, c_lcopy, c_recip, c_bcmm, c_bce, c_onorm = ch(), ch(), ch(), ch(), ch(), ch(), ch()
    c_wot, c_x2, c_mm4, c_acce, c_accw = ch(16), ch(16), ch(), ch(), ch(16)
    c_xnqfm, c_wgu, c_wguu, c_mmg5, c_ge, c_mmu5, c_ue = ch(16), ch(16), ch(16), ch(), ch(), ch(), ch()
    c_gum, c_rmax, c_guw = ch(), ch(), ch(16)
    c_gurd, c_sgu, c_qt, c_y8b, c_y8c, c_d2, c_c1b, c_m2b, c_guq, c_guqw = \
        ch(16), ch(), ch(), ch(), ch(), ch(), ch(), ch(), ch(), ch(16)
    c_guqfm, c_wdt, c_dnk, c_dnmm, c_acr, c_fin, c_out = ch(16), ch(16), ch(), ch(), ch(16), ch(), ch(16)
    c_dbg = ch(16)

    op("sync", lambda e: e.dma_start(out=ident_s, in_=ident_h), inc=c_init)
    op("vector", lambda e: e.memset(ones_f, 1.0), inc=c_mset)
    op("vector", lambda e: e.memset(ones_b, 1.0), inc=c_mset)
    op("vector", lambda e: e.memset(ones_b[32:33, :], -1.0))
    op("vector", lambda e: e.memset(ones_b[64:65, :], -1.0))
    op("vector", lambda e: e.memset(
        v_tm.rearrange("p (t c) -> p t c", c=65)[:, :, 64:65], 1.0), inc=c_mset)
    op("vector", lambda e: e.memset(tcol[:, 69:70], 1e-5), inc=c_mset)

    bw6 = banks[6].acquire()
    for q4 in range(4):
        bb = b_bias[q4 % 2]
        lw = [c_wmm.at(q4 - 2)] if q4 >= 2 else []
        op("sync", lambda e, b2=bb, qq=q4: e.dma_start(
            out=b2, in_=b_in[:, qq * 512:(qq + 1) * 512]), waits=lw, inc=c_bias)
        for m in range(4):
            op("tensor", MM(pb[6][:, q4 * 4 + m:q4 * 4 + m + 1],
                            bb[0:1, m * 128:(m + 1) * 128],
                            ones_f[0:1, 0:1], q4 == 0 and m == 0, False),
               waits=[c_bias.at(q4), c_mset.at(0)] + (bw6 if (q4 == 0 and m == 0) else []),
               inc=(c_wmm if m == 3 else None))
    i = op("scalar", lambda e: e.copy(b_cols, pb[6][:, 0:16]), waits=[c_wmm.at(3)], inc=c_thr)
    i_bcol = i
    banks[6].consumed(c_thr, i)

    xr_slot = b7[:, 0:1024]
    xnq_slot = b7[:, 1024:2048]
    t2_slot = b7[:, 2048:3072]
    for i in range(NT):
        b = i % 2
        sc = scols[b]
        xw = [c_xn.at(i - 2), c_ssq.at(i - 2)] if i >= 2 else []
        op("sync", lambda e, ii=i, bb=b: e.dma_start(
            out=x_t[bb], in_=x_h[ii * 128:(ii + 1) * 128, :]), waits=xw, inc=c_x)
        cw = [c_t2.at(i - 1)] if i >= 1 else []
        op("sync", lambda e, ii=i: e.dma_start(
            out=ctab_t, in_=ctab_h[ii * 128:(ii + 1) * 128, 0:1024]), waits=cw, inc=c_ct)
        sqw = [c_xr.at(i - 1)] if i >= 1 else []
        vw = [c_y8.at(i - 2)] if i >= 2 else []
        op("scalar", lambda e, bb=b: e.activation(d_t, x_t[bb], AF.Square),
           waits=[c_x.at(i)] + sqw + vw, inc=c_sq2)
        op("vector", lambda e, bb=b, s=sc: e.reduce_sum(s[:, 0:1], x_t[bb], axis=AX),
           waits=[c_x.at(i)] + vw, inc=c_v1)
        op("vector", lambda e, s=sc: e.reduce_sum(s[:, 1:2], d_t, axis=AX),
           waits=[c_sq2.at(i)], inc=c_ssq)
        op("vector", lambda e, s=sc: e.tensor_scalar(s[:, 2:3], s[:, 0:1], 1.0 / D, None, AO.mult),
           waits=[c_v1.at(i)])
        op("vector", lambda e, s=sc: e.tensor_tensor(s[:, 3:4], s[:, 2:3], s[:, 2:3], AO.mult))
        op("vector", lambda e, s=sc: e.scalar_tensor_tensor(
            s[:, 4:5], s[:, 1:2], 1.0 / D, s[:, 3:4], AO.mult, AO.subtract),
           waits=[c_ssq.at(i)], inc=c_var)
        op("scalar", lambda e, s=sc: e.activation(s[:, 5:6], s[:, 4:5], AF.Ln, bias=tcol[:, 69:70]),
           waits=[c_var.at(i), c_mset.at(3)])
        op("scalar", lambda e, s=sc: e.activation(s[:, 6:7], s[:, 5:6], AF.Exp, scale=-0.5),
           inc=c_rstd)
        xnw = [c_y8.at(i - 2), c_t1.at(i - 2), c_t2.at(i - 2)] if i >= 2 else []
        op("vector", lambda e, bb=b, s=sc: e.tensor_scalar(
            xn_t[bb], x_t[bb], s[:, 2:3], s[:, 6:7], AO.subtract, AO.mult),
           waits=[c_rstd.at(i)] + xnw, inc=c_xn)
        if debug:
            op("scalar", lambda e, ii=i, bb=b: e.dma_start(
                out=d_xn[ii * 128:(ii + 1) * 128, :], in_=xn_t[bb]),
               waits=[c_xn.at(i)], inc=c_dbg)
            op("scalar", lambda e, ii=i, s=sc: e.dma_start(
                out=d_scall[ii:ii + 1, :], in_=s[0:1, 0:16]),
               waits=[c_xn.at(i)], inc=c_dbg)
        op("vector", lambda e, bb=b, s=sc: e.reduce_max(
            s[:, 7:8], xn_t[bb], axis=AX, apply_absolute_value=True), waits=[c_xn.at(i)])
        op("vector", lambda e, s=sc: e.tensor_scalar(s[:, 8:9], s[:, 7:8], 1e-5, None, AO.max))
        op("vector", lambda e, s=sc: e.reciprocal(s[:, 9:10], s[:, 8:9]))
        op("vector", lambda e, s=sc: e.tensor_scalar(s[:, 10:11], s[:, 9:10], 127.0, None, AO.mult),
           inc=c_sx)
        op("vector", lambda e, ii=i, s=sc: e.tensor_scalar(
            tcol[:, ii:ii + 1], s[:, 8:9], 1.0 / 127.0, None, AO.mult), inc=c_deqx)
        y8w = [c_y8b1.at(i - 1)] if i >= 1 else []
        op("scalar", lambda e, bb=b, s=sc: e.activation(
            y8_t, xn_t[bb], AF.Copy, scale=s[:, 10:11]),
           waits=[c_sx.at(i)] + y8w, inc=c_y8)
        yb = [c_xnq.at(i - 1)] if i >= 1 else []
        op("vector", lambda e: e.tensor_copy(y8b_t, y8_t), waits=[c_y8.at(i)] + yb, inc=c_y8b1)
        op("vector", lambda e, bb=b, s=sc: e.scalar_tensor_tensor(
            d_t, xn_t[bb], s[:, 10:11], y8b_t, AO.mult, AO.subtract),
           waits=[c_ssq.at(i)], inc=c_dq)
        cg = [c_m2.at(i - 1)] if i >= 1 else []
        op("vector", lambda e: e.tensor_scalar(c1_t[:, 0:1024], d_t, 0.5, None, AO.is_ge),
           waits=[c_dq.at(i)] + cg, inc=c_c1)
        op("vector", lambda e: e.scalar_tensor_tensor(
            m2_t[:, 0:1024], d_t, -0.5, c1_t[:, 0:1024], AO.is_le, AO.subtract),
           waits=[c_c1.at(i)], inc=c_m2)
        xqw = [c_xnqw.at(i - 1)] if i >= 1 else []
        op("gpsimd", lambda e: e.tensor_tensor(
            xnq_slot, y8b_t, m2_t[:, 0:1024], AO.subtract),
           waits=[c_m2.at(i), c_y8b1.at(i)] + xqw, inc=c_xnq)
        op("scalar", lambda e, ii=i: e.dma_start(
            out=s_xnq[ii * 128:(ii + 1) * 128, :], in_=xnq_slot),
           waits=[c_xnq.at(i)], inc=c_xnqw)
        op("vector", lambda e, bb=b: e.tensor_tensor(d_t, xn_t[bb], ctab_t, AO.mult),
           waits=[c_ct.at(2 * i), c_m2.at(i)], inc=c_t1)
        op("sync", lambda e, ii=i: e.dma_start(
            out=ctab_t, in_=ctab_h[ii * 128:(ii + 1) * 128, 1024:2048]),
           waits=[c_t1.at(i)], inc=c_ct)
        t2w = [c_xr.at(i - 1)] if i >= 1 else []
        op("gpsimd", lambda e, bb=b: e.tensor_tensor(
            t2_slot.rearrange("p (c two) -> p c two", two=2),
            xn_t[bb].rearrange("p (c two) -> p c two", two=2)[:, :, ::-1],
            ctab_t.rearrange("p (c two) -> p c two", two=2), AO.mult),
           waits=[c_ct.at(2 * i + 1)] + t2w, inc=c_t2)
        xrw = [c_xrw.at(i - 1)] if i >= 1 else []
        op("vector", lambda e: e.tensor_tensor(xr_slot, d_t, t2_slot, AO.add),
           waits=[c_t1.at(i), c_t2.at(i)] + xrw, inc=c_xr)
        op("scalar", lambda e, ii=i: e.dma_start(
            out=s_xr[ii * 128:(ii + 1) * 128, :], in_=xr_slot),
           waits=[c_xr.at(i)], inc=c_xrw)

    wbuf = [wprep_t, gub2_t]
    wbuf_last = [[], []]
    wj = 0
    for j in range(32):
        src_ = w_in[j * 128:(j + 1) * 128, :] if j < 24 else w_o[(j - 24) * 128:(j - 23) * 128, :]
        dst_ = s_win[j * 128:(j + 1) * 128, :] if j < 24 else s_wo[(j - 24) * 128:(j - 23) * 128, :]
        sl = wj % 2
        i_ld = op("sync", lambda e, s2=src_, w=wbuf[sl]: e.dma_start(out=w, in_=s2),
                  waits=wbuf_last[sl], inc=c_wld)
        i_c = op("gpsimd", lambda e, d2=dst_, w=wbuf[sl]: e.dma_start(out=d2, in_=w),
                 waits=[c_wld.at(i_ld)], inc=c_wcast)
        wbuf_last[sl] = [c_wcast.at(i_c)]
        wj += 1

    tern = [(w_g, s_wg, INNER * D), (w_u, s_wu, INNER * D), (w_d, s_wd, D * INNER)]
    wqw_end = []
    for wi, (wsrc, wdst, nelem) in enumerate(tern):
        nchunk = nelem // (128 * 1024)
        ncb = wsrc.shape[1] // 1024
        chunks = [wsrc[(j // ncb) * 128:(j // ncb + 1) * 128,
                       (j % ncb) * 1024:(j % ncb + 1) * 1024] for j in range(nchunk)]
        dchunks = [wdst[(j // ncb) * 128:(j // ncb + 1) * 128,
                        (j % ncb) * 1024:(j % ncb + 1) * 1024] for j in range(nchunk)]
        op("vector", lambda e: e.memset(tcol[:, 64:65], 0.0),
           waits=[c_wsum.now(), c_wmm.now()], inc=c_wsum)
        for j in range(nchunk):
            sl = wj % 2
            i_ld = op("sync", lambda e, f=chunks[j], w=wbuf[sl]: e.dma_start(
                out=w, in_=f), waits=wbuf_last[sl], inc=c_wld)
            i_ab = op("vector", lambda e, w=wbuf[sl]: e.reduce_sum(
                tcol[:, 65:66], w, axis=AX, apply_absolute_value=True),
                waits=[c_wld.at(i_ld), c_wsum.last()], inc=c_wabs)
            wbuf_last[sl] = [c_wabs.at(i_ab)]
            op("vector", lambda e: e.tensor_tensor(
                tcol[:, 64:65], tcol[:, 64:65], tcol[:, 65:66], AO.add),
               waits=[c_wabs.at(i_ab)], inc=c_wsum)
            wj += 1
        bw = banks[7].acquire()
        i_mm = op("tensor", MM(pb[7][0:1, 0:1], tcol[:, 64:65], ones_f[:, 0:1], True, True),
                  waits=[c_wsum.last()] + bw, inc=c_wmm)
        i_t = op("scalar", lambda e, w2=wi: e.copy(scal[0:1, w2:w2 + 1], pb[7][0:1, 0:1]),
                 waits=[c_wmm.at(i_mm)], inc=c_thr)
        banks[7].consumed(c_thr, i_t)
        op("vector", lambda e, w2=wi: e.tensor_scalar(
            scal[0:1, 3 + w2:4 + w2], scal[0:1, w2:w2 + 1], 1.0 / nelem, 1e-5, AO.mult, AO.max),
           waits=[c_thr.at(i_t)], inc=c_thr)
        op("vector", lambda e, w2=wi: e.tensor_scalar(
            scal[0:1, 6 + w2:7 + w2], scal[0:1, 3 + w2:4 + w2], 0.5, None, AO.mult), inc=c_thr)
        bw = banks[7].acquire()
        op("tensor", MM(pb[7][:, 0:1], ones_f[0:1, :], scal[0:1, 6 + wi:7 + wi], True, False),
           waits=[c_thr.last()] + bw)
        i_mm = op("tensor", MM(pb[7][:, 1:2], ones_f[0:1, :], scal[0:1, 3 + wi:4 + wi], False, True),
                  inc=c_wmm)
        i_t = op("scalar", lambda e, w2=wi: e.copy(thr_col[:, 4 * w2:4 * w2 + 2], pb[7][:, 0:2]),
                 waits=[c_wmm.at(i_mm)], inc=c_thr)
        banks[7].consumed(c_thr, i_t)
        op("vector", lambda e, w2=wi: e.tensor_scalar(
            thr_col[:, 4 * w2 + 2:4 * w2 + 3], thr_col[:, 4 * w2:4 * w2 + 1], -1.0, None, AO.mult),
           waits=[c_thr.at(i_t)], inc=c_thr)
        op("vector", lambda e, w2=wi: e.tensor_scalar(
            thr_col[:, 4 * w2 + 3:4 * w2 + 4], thr_col[:, 4 * w2 + 1:4 * w2 + 2], -1.0, None, AO.mult),
           inc=c_thr)
        thr_rdy = c_thr.last()
        for j in range(nchunk):
            sl = wj % 2
            i_ld = op("sync", lambda e, f=chunks[j], w=wbuf[sl]: e.dma_start(
                out=w, in_=f),
                waits=wbuf_last[sl] + [thr_rdy], inc=c_wld2)
            cg = [c_wq.last()] if c_wq.n else []
            i_c1 = op("vector", lambda e, w=wbuf[sl], w2=wi: e.tensor_scalar(
                c1_t[:, 1024:2048], w, thr_col[:, 4 * w2:4 * w2 + 1], None, AO.is_ge),
                waits=[c_wld2.at(i_ld), thr_rdy] + cg +
                      ([c_c1.at(NT - 1)] if (wi == 0 and j == 0) else []),
                inc=c_wc1)
            qw = [c_wqw.at(c_wqw.n - 1)] if c_wqw.n >= 1 else []
            i_wq = op("vector", lambda e, w=wbuf[sl], w2=wi: e.scalar_tensor_tensor(
                wq_t, w, thr_col[:, 4 * w2 + 2:4 * w2 + 3], c1_t[:, 1024:2048],
                AO.is_le, AO.subtract),
                waits=[c_wc1.at(i_c1)] + qw, inc=c_wq)
            wbuf_last[sl] = [c_wq.at(i_wq)]
            op("scalar", lambda e, fd=dchunks[j]: e.dma_start(out=fd, in_=wq_t),
               waits=[c_wq.at(i_wq)], inc=c_wqw)
            wj += 1
        wqw_end.append(c_wqw.n - 1)

    if debug:
        op("scalar", lambda e: e.dma_start(out=d_sc[:, 0:16], in_=scols[0]),
           waits=[c_deqx.at(NT - 2)], inc=c_dbg)
        op("scalar", lambda e: e.dma_start(out=d_sc[:, 16:32], in_=scols[1]),
           waits=[c_deqx.at(NT - 1)], inc=c_dbg)
    op("sync", lambda e: e.dma_start(out=mask_os, in_=mask_o_h),
       waits=[c_xrw.at(NT - 1), c_xnqw.at(NT - 1), c_xr.at(NT - 1)], inc=c_init)

    for tokB in range(4):
        for kk in range(NKK):
            qd = "sync"
            op(qd, lambda e, tb=tokB, k2=kk: e.dma_start(
                out=big2[:, k2 * 2048 + tb * 512: k2 * 2048 + (tb + 1) * 512],
                in_=s_xr[tb * 512:(tb + 1) * 512, k2 * 128:(k2 + 1) * 128], transpose=True),
               waits=[c_xrw.at(tokB * 4 + 3)], inc=c_xrfm)

    if stage <= 1:
        with nc.Block() as block:
            prog.realize(block, [c_xnqw.now(), c_xrfm.now(), c_wcast.now(), c_wqw.now(),
                                 c_dbg.now()])
        es.close()
        return nc

    nqk = 0
    for tokB in range(4):
        for mpair in range(8):
            sl = (mpair % 2) * 2048
            grp = tokB * 8 + mpair
            for kk in range(NKK):
                qd = "sync" if kk % 2 == 0 else "scalar"
                jw = [c_wcast.at(11), c_xrfm.at(tokB * 8 + 7)]
                if grp >= 2:
                    jw.append(c_mm2a.at((grp - 2) * 2 + 1))
                op(qd, lambda e, mp=mpair, k2=kk, s2=sl: e.dma_start(
                    out=wstr[:, s2 + k2 * 256: s2 + k2 * 256 + 256],
                    in_=s_win[mp * 256:(mp + 1) * 256, k2 * 128:(k2 + 1) * 128], transpose=True),
                   waits=jw, inc=c_wqk)
            for m2 in range(2):
                m = mpair * 2 + m2
                bi = nqk % 6
                mms = [(pb[bi][0:128, 0:512],
                        wstr[:, sl + kk * 256 + m2 * 128: sl + kk * 256 + (m2 + 1) * 128],
                        big2[:, kk * 2048 + tokB * 512: kk * 2048 + (tokB + 1) * 512])
                       for kk in range(NKK)]
                i_mm = mm_group(bi, mms,
                                waits=[c_wqk.at(grp * 8 + 7), c_xrfm.at(tokB * 8 + 7)],
                                inc=c_mm2a)
                dst = q_fm if m < 8 else k_fm
                coff = (m % 8) * 2048 + tokB * 512
                i_ev = op("vector", lambda e, b2=bi, d2=dst, c2=coff, mm=m: e.tensor_scalar(
                    d2[:, c2:c2 + 512], pb[b2][0:128, 0:512], b_cols[:, mm:mm + 1], None, AO.add),
                    waits=[c_mm2a.at(i_mm), c_thr.at(i_bcol)], inc=c_qke)
                banks[bi].consumed(c_qke, i_ev)
                nqk += 1

    op("vector", lambda e: e.memset(
        v_tm.rearrange("p (t c) -> p t c", c=65)[:, 0:15, 64:65], 1.0),
       waits=[c_y8b1.at(NT - 1), c_xnq.at(NT - 1)], inc=c_mset)
    for kk in range(NKK):
        qd = "sync"
        op(qd, lambda e, k2=kk: e.dma_start(
            out=wstr[:, 4096 + k2 * 1024: 4096 + (k2 + 1) * 1024],
            in_=s_win[2048:3072, k2 * 128:(k2 + 1) * 128], transpose=True),
           waits=[c_wcast.at(23), c_mm2a.now()], inc=c_wv)
    for half in range(2):
        op("sync", lambda e, hh=half: e.dma_start(
            out=b_bias[hh], in_=b_in[:, 2048 + hh * 512:2048 + (hh + 1) * 512]),
           waits=[c_wmm.at(3), c_mm2a.now()], inc=c_bias)
    nv = 0
    vv = v_tm.rearrange("p (t c) -> p t c", c=65)
    vg = v_g.rearrange("p (t c) -> p t c", c=65)
    for mt in range(NT):
        for half in range(2):
            bi = nv % 6
            mms = [(pb[bi][0:128, 0:512],
                    big2[:, kk * 2048 + mt * 128: kk * 2048 + (mt + 1) * 128],
                    wstr[:, 4096 + kk * 1024 + half * 512: 4096 + kk * 1024 + (half + 1) * 512])
                   for kk in range(NKK)]
            mms.append((pb[bi][0:128, 0:512], ones_f[0:1, 0:128], b_bias[half][0:1, 0:512]))
            i_mm = mm_group(bi, mms, waits=[c_wv.at(7), c_bias.at(4 + half), c_xrfm.at(31)], inc=c_mm2b)
            evw0 = [c_y8b1.at(NT - 1), c_xnq.at(NT - 1)] if nv <= 1 else []
            i_ev = op("vector", lambda e, b2=bi, m3=mt, h2=half: e.tensor_copy(
                vv[:, m3 * 16 + h2 * 8:m3 * 16 + (h2 + 1) * 8, 0:64],
                pb[b2][0:128, 0:512].rearrange("p (a b) -> p a b", b=64)),
                waits=[c_mm2b.at(i_mm), c_mset.at(2)] + evw0, inc=c_ve)
            banks[bi].consumed(c_ve, i_ev)
            nv += 1
    for half in range(2):
        mms = [(pb[6][0:8, 0:512],
                big2[:, kk * 2048: kk * 2048 + 2048:256],
                wstr[:, 4096 + kk * 1024 + half * 512: 4096 + kk * 1024 + (half + 1) * 512])
               for kk in range(NKK)]
        mms.append((pb[6][0:8, 0:512], ones_f[0:1, 0:8], b_bias[half][0:1, 0:512]))
        i_mm = mm_group(6, mms, waits=[c_wv.at(7), c_bias.at(4 + half), c_xrfm.at(31)], inc=c_mmvg)
        i_ev = op("vector", lambda e, h2=half: e.tensor_copy(
            vg[:, h2 * 8:(h2 + 1) * 8, 0:64],
            pb[6][0:8, 0:512].rearrange("p (a b) -> p a b", b=64)),
            waits=[c_mmvg.at(i_mm)], inc=c_vg)
        banks[6].consumed(c_vg, i_ev)
    op("vector", lambda e: e.memset(vg[:, :, 64:65], 1.0),
       waits=[c_wqk.last(), c_vg.at(c_vg.n - 1)], inc=c_vg)

    if debug:
        op("sync", lambda e: e.dma_start(out=d_qfm, in_=q_fm), waits=[c_qke.now()], inc=c_out)
        op("sync", lambda e: e.dma_start(out=d_kfm, in_=k_fm), waits=[c_qke.now()], inc=c_out)
        op("sync", lambda e: e.dma_start(out=d_vtm, in_=v_tm), waits=[c_ve.now(), c_vg.now()], inc=c_out)
    if stage <= 2:
        with nc.Block() as block:
            prog.realize(block, [c_qke.now(), c_ve.now(), c_vg.now(), c_out.now(),
                                 c_xnqw.now(), c_wqw.now()])
        es.close()
        return nc

    nmx = 0
    for h in range(H):
        chk, off = h // 2, 64 * (h % 2)
        mc = mcols[h % 2]
        for qi in range(NT):
            bi = nmx % 2
            qlo, qhi = max(qi - 1, 0), min(qi + 1, NT - 1)
            nq = (qhi - qlo + 1) * 128
            bw = banks[bi].acquire()
            mw = [c_qke.now()] + bw
            if qi == 0 and h >= 2:
                mw.append(c_trm.at(h - 2))
            op("tensor", MM(pb[bi][0:128, 0:nq],
                            q_fm[off:off + 64, chk * 2048 + qi * 128: chk * 2048 + (qi + 1) * 128],
                            k_fm[off:off + 64, chk * 2048 + qlo * 128: chk * 2048 + qlo * 128 + nq],
                            True, False), waits=mw)
            i_mm = op("tensor", MM(pb[bi][0:128, nq:nq + 8],
                                   q_fm[off:off + 64, chk * 2048 + qi * 128: chk * 2048 + (qi + 1) * 128],
                                   k_fm[off:off + 64, chk * 2048: chk * 2048 + 2048:256],
                                   False, True), inc=c_mmax)
            rdw = [c_mmax.at(i_mm)]
            if h < 2 and qi == 0:
                rdw += [c_qke.now()] if h == 0 else [c_y8.at(NT - 1), c_sx.at(NT - 1)]
            i_rd = op("vector", lambda e, b2=bi, q2=qi, n2=nq, m3=mc: e.reduce_max(
                m3[:, q2:q2 + 1], pb[b2][0:128, 0:n2 + 8], axis=AX),
                waits=rdw, inc=c_mred)
            banks[bi].consumed(c_mred, i_rd)
            nmx += 1
        bw = banks[6].acquire()
        i_tr = op("tensor", lambda e, m3=mc: e.transpose(pb[6][0:16, 0:128], m3, ident_s),
                  waits=[c_mred.last(), c_init.at(0)] + bw, inc=c_trm)
        i_ev = op("scalar", lambda e: e.copy(mtmp[0:16, 0:128], pb[6][0:16, 0:128]),
                  waits=[c_trm.at(i_tr), c_mwr.now(), c_xr.at(NT - 1)], inc=c_mtr)
        banks[6].consumed(c_mtr, i_ev)
        op("scalar", lambda e, h2=h: e.dma_start(
            out=s_m[h2:h2 + 1, :].rearrange("a (q c) -> (a q) c", c=128),
            in_=mtmp[0:16, 0:128]),
           waits=[c_mtr.at(i_ev)], inc=c_mwr)

    m8row = [wstr[32:33, 2048:4096], wstr[64:65, 2048:4096]]
    P_g = wstr[0:8, 5120:7168]
    linv_bc = wstr[:, 4608:5120]
    last_m8 = [[], []]
    last_lrow = [[], []]
    last_pg = []
    pt_last = [None, None]

    for h in range(H):
        par = h % 2
        chk, off = h // 2, 64 * (h % 2)
        row = 64 * (1 - par)
        mw = [c_mwr.at(h)] + last_m8[par]
        if h < 2:
            mw.append(c_mm2a.now())
        op("gpsimd", lambda e, h2=h, p2=par: e.dma_start(out=m8row[p2], in_=s_m[h2:h2 + 1, :]),
           waits=mw, inc=c_m8)
        i_m8 = c_m8.n - 1
        last_m8[par] = []
        for qB in range(4):
            bw = banks[2].acquire()
            op("tensor", MM(pb[2][0:8, 0:512],
                            k_fm[off:off + 64, chk * 2048: chk * 2048 + 2048:256],
                            q_fm[off:off + 64, chk * 2048 + qB * 512: chk * 2048 + (qB + 1) * 512],
                            True, False),
               waits=[c_m8.at(i_m8), c_qke.now()] + bw)
            orow = 32 * (par + 1)
            i_mm = op("tensor", MM(pb[2][0:8, 0:512], ones_b[orow:orow + 1, 0:8],
                                   m8row[par][0:1, qB * 512:(qB + 1) * 512], False, True),
                      inc=c_mmg)
            last_m8[par] = [c_mmg.at(i_mm)]
            pw = [c_mmg.at(i_mm)]
            if qB == 0:
                pw += last_pg
                if h == 0:
                    pw += [c_mm2a.now(), c_mm2b.now(), c_mmvg.now()]
            i_pg = op("scalar", lambda e, q2=qB: e.activation(
                P_g[0:8, q2 * 512:(q2 + 1) * 512], pb[2][0:8, 0:512], AF.Exp, scale=0.125),
                waits=pw, inc=c_pg)
            banks[2].consumed(c_pg, i_pg)
        pg_base = c_pg.n - 4
        if debug and h == 0:
            op("scalar", lambda e: e.dma_start(out=d_pg, in_=P_g),
               waits=[c_pg.at(pg_base + 3)], inc=c_dbg)
            op("scalar", lambda e: e.dma_start(out=d_vg, in_=v_g),
               waits=[c_vg.now()], inc=c_dbg)
            op("scalar", lambda e, p2=par: e.dma_start(out=d_m8, in_=m8row[p2]),
               waits=[c_m8.at(i_m8)], inc=c_dbg)
        last_pg = []
        first_lcopy = True
        for kj in range(NT):
            bi = kj % 2
            qlo, qhi = max(kj - 1, 0), min(kj + 1, NT - 1)
            nq = (qhi - qlo + 1) * 128
            bw = banks[bi].acquire()
            op("tensor", MM(pb[bi][0:128, 0:nq],
                            k_fm[off:off + 64, chk * 2048 + kj * 128: chk * 2048 + (kj + 1) * 128],
                            q_fm[off:off + 64, chk * 2048 + qlo * 128: chk * 2048 + qlo * 128 + nq],
                            True, False),
               waits=[c_m8.at(i_m8)] + bw)
            orow = 32 * (par + 1)
            i_mm = op("tensor", MM(pb[bi][0:128, 0:nq], ones_b[orow:orow + 1, 0:128],
                                   m8row[par][0:1, qlo * 128: qlo * 128 + nq], False, True),
                      inc=c_mmS)
            last_m8[par].append(c_mmS.at(i_mm))
            expw = [c_mmS.at(i_mm)]
            if kj % 2 == 0:
                i_ms = op("vector", lambda e, b2=bi, n2=nq: e.memset(pb[b2][0:1, 0:n2], NEG),
                          waits=[c_mmS.at(i_mm)], inc=c_ms)
                expw.append(c_ms.at(i_ms))
            if pt_last[bi] is not None:
                expw.append(pt_last[bi])
            else:
                expw.append(c_wqw.at(wqw_end[2]))
            i_exp = op("scalar", lambda e, b2=bi, n2=nq: e.activation(
                pt_sb[b2][:, 0:n2], pb[b2][0:128, 0:n2], AF.Exp, scale=0.125),
                waits=expw, inc=c_expS)
            banks[bi].consumed(c_expS, i_exp)
            msk = mask_es if kj % 2 == 0 else mask_os
            mslice = msk[:, 128:384] if kj == 0 else (msk[:, 0:256] if kj == NT - 1 else msk[:, 0:384])
            i_msk = op("gpsimd", lambda e, b2=bi, n2=nq, m3=mslice: e.tensor_tensor(
                pt_sb[b2][:, 0:n2], pt_sb[b2][:, 0:n2], m3, AO.mult),
                waits=[c_expS.at(i_exp), c_init.at(1)], inc=c_mask)
            for qi in range(qlo, qhi + 1):
                slot = 3 + qi % 3
                first = (kj == max(qi - 1, 0))
                vbase = kj * 1040 + h * 65
                ptc = pt_sb[bi][:, (qi - qlo) * 128:(qi - qlo + 1) * 128]
                wl = [c_mask.at(i_msk), c_ve.now(), c_mset.at(4)]
                if first:
                    wl += banks[slot].acquire()
                if par == 0:
                    op("tensor", MM(pb[slot][0:65, 0:128], v_tm[:, vbase:vbase + 65], ptc,
                                    first, False), waits=wl, inc=c_pv)
                else:
                    op("tensor", MM(pb[slot][64:128, 0:128], v_tm[:, vbase:vbase + 64], ptc,
                                    first, False), waits=wl)
                    op("tensor", MM(pb[slot][0:1, 128:256], v_tm[:, vbase + 64:vbase + 65], ptc,
                                    first, False), inc=c_pv)
                if first:
                    gb = h * 65
                    pgc = P_g[0:8, qi * 128:(qi + 1) * 128]
                    gw = [c_pg.at(pg_base + qi // 4), c_vg.now()]
                    if par == 0:
                        op("tensor", MM(pb[slot][0:65, 0:128], v_g[0:8, gb:gb + 65], pgc,
                                        False, False), waits=gw, inc=c_pv)
                    else:
                        op("tensor", MM(pb[slot][64:128, 0:128], v_g[0:8, gb:gb + 64], pgc,
                                        False, False), waits=gw)
                        op("tensor", MM(pb[slot][0:1, 128:256], v_g[0:8, gb + 64:gb + 65], pgc,
                                        False, False), inc=c_pv)
                    last_pg = [c_pv.at(c_pv.n - 1)]
            pt_last[bi] = c_pv.at(c_pv.n - 1)
            closed = ([kj - 1] if kj >= 1 else []) + ([NT - 1] if kj == NT - 1 else [])
            for qi in closed:
                slot = 3 + qi % 3
                ocol = chk * 2048 + qi * 128
                ow = [c_pv.at(c_pv.n - 1), c_ve.now(), c_vg.now()]
                if h == 0:
                    ow += [c_mm2b.now(), c_mmvg.now()]
                lw = [c_pv.at(c_pv.n - 1)]
                if first_lcopy:
                    lw += last_lrow[par] + [c_xn.at(NT - 1), c_ssq.at(NT - 1)]
                    first_lcopy = False
                if par == 0:
                    i_oe = op("vector", lambda e, s2=slot, o2=ocol: e.tensor_copy(
                        big2[0:64, o2:o2 + 128], pb[s2][0:64, 0:128]), waits=ow, inc=c_oevac)
                    i_lc = op("scalar", lambda e, s2=slot, q2=qi: e.copy(
                        l_rows[q2 // 8][64:65, (q2 % 8) * 128:(q2 % 8 + 1) * 128],
                        pb[s2][64:65, 0:128]),
                        waits=lw, inc=c_lcopy)
                else:
                    i_oe = op("vector", lambda e, s2=slot, o2=ocol: e.tensor_copy(
                        big2[64:128, o2:o2 + 128], pb[s2][64:128, 0:128]), waits=ow, inc=c_oevac)
                    i_lc = op("scalar", lambda e, s2=slot, q2=qi: e.copy(
                        l_rows[q2 // 8][0:1, (q2 % 8) * 128:(q2 % 8 + 1) * 128],
                        pb[s2][0:1, 128:256]),
                        waits=lw, inc=c_lcopy)
                banks[slot].consumed(c_oevac, i_oe)
                banks[slot].consumed(c_lcopy, i_lc)
        for half2 in range(2):
            op("vector", lambda e, r2=row, h3=half2: e.reciprocal(
                l_rows[h3][r2:r2 + 1, :], l_rows[h3][r2:r2 + 1, :]),
               waits=[c_lcopy.last()], inc=c_recip)
        rc_base = c_recip.n - 2
        last_lrow[par] = []
        for qB in range(4):
            bw = banks[2].acquire()
            i_bc = op("tensor", MM(pb[2][off:off + 64, 0:512], ones_f[row:row + 1, 0:64],
                                   l_rows[qB // 2][row:row + 1, (qB % 2) * 512:(qB % 2 + 1) * 512],
                                   True, True),
                      waits=[c_recip.at(rc_base + qB // 2)] + bw, inc=c_bcmm)
            last_lrow[par].append(c_bcmm.at(i_bc))
            bew = [c_bcmm.at(i_bc)]
            if h < 2 and qB == 0:
                bew += [c_mm2b.now(), c_mmvg.now()]
            i_be = op("scalar", lambda e, o2=off: e.copy(
                linv_bc[o2:o2 + 64, 0:512], pb[2][o2:o2 + 64, 0:512]),
                waits=bew, inc=c_bce)
            banks[2].consumed(c_bce, i_be)
            op("gpsimd", lambda e, o2=off, c2=chk, q2=qB: e.tensor_tensor(
                big2[o2:o2 + 64, c2 * 2048 + q2 * 512: c2 * 2048 + (q2 + 1) * 512],
                big2[o2:o2 + 64, c2 * 2048 + q2 * 512: c2 * 2048 + (q2 + 1) * 512],
                linv_bc[o2:o2 + 64, 0:512], AO.mult),
               waits=[c_bce.at(i_be), c_oevac.last()], inc=c_onorm)

    if debug:
        op("sync", lambda e: e.dma_start(out=d_ofm, in_=big2), waits=[c_onorm.now()], inc=c_out)
    if stage <= 3:
        with nc.Block() as block:
            prog.realize(block, [c_onorm.now(), c_out.now(), c_xnqw.now(), c_wqw.now()])
        es.close()
        return nc

    for kk in range(NKK):
        qd = "sync"
        op(qd, lambda e, k2=kk: e.dma_start(
            out=v_tm[:, k2 * 1024:(k2 + 1) * 1024],
            in_=s_wo[:, k2 * 128:(k2 + 1) * 128], transpose=True),
           waits=[c_wcast.at(31), c_pv.now()], inc=c_wot)
    for half in range(2):
        op("sync", lambda e, hh=half: e.dma_start(
            out=b_bias[hh], in_=b_o[:, hh * 512:(hh + 1) * 512]),
           waits=[c_mmvg.now(), c_mm2b.now()], inc=c_bias)
    for mt in range(NT):
        b = mt % 2
        xw = [c_acce.at((mt - 2) * 2 + 1)] if mt >= 2 else \
            [c_xn.at(NT - 1), c_ssq.at(NT - 1), c_bcmm.now(), c_recip.now()]
        op("sync", lambda e, m3=mt, bb=b: e.dma_start(
            out=x_t[bb], in_=x_h[m3 * 128:(m3 + 1) * 128, :]), waits=xw, inc=c_x2)
        for half in range(2):
            bi = (mt * 2 + half) % 4
            mms = [(pb[bi][0:128, 0:512],
                    big2[:, kk * 2048 + mt * 128: kk * 2048 + (mt + 1) * 128],
                    v_tm[:, kk * 1024 + half * 512: kk * 1024 + (half + 1) * 512])
                   for kk in range(NKK)]
            mms.append((pb[bi][0:128, 0:512], ones_f[0:1, 0:128], b_bias[half][0:1, 0:512]))
            i_mm = mm_group(bi, mms, waits=[c_wot.at(7), c_bias.at(6 + half), c_onorm.now()], inc=c_mm4)
            evw = [c_mm4.at(i_mm), c_x2.at(mt)]
            if mt >= 2:
                evw.append(c_accw.at(mt - 2))
            else:
                evw += [c_t1.at(NT - 1), c_t2.at(NT - 1), c_y8.at(NT - 1), c_dq.at(NT - 1)]
            i_ev = op("vector", lambda e, b2=bi, bb=b, h2=half: e.tensor_tensor(
                xn_t[bb][:, h2 * 512:(h2 + 1) * 512], pb[b2][0:128, 0:512],
                x_t[bb][:, h2 * 512:(h2 + 1) * 512], AO.add),
                waits=evw, inc=c_acce)
            banks[bi].consumed(c_acce, i_ev)
        op("sync", lambda e, m3=mt, bb=b: e.dma_start(
            out=s_acc[m3 * 128:(m3 + 1) * 128, :], in_=xn_t[bb]),
           waits=[c_acce.at(mt * 2 + 1)], inc=c_accw)

    if stage <= 4:
        with nc.Block() as block:
            prog.realize(block, [c_accw.now(), c_xnqw.now(), c_wqw.now()])
        es.close()
        return nc

    for tokB in range(4):
        for kk in range(NKK):
            op("sync", lambda e, tb=tokB, k2=kk: e.dma_start(
                out=big2[:, k2 * 2048 + tb * 512: k2 * 2048 + (tb + 1) * 512],
                in_=s_xnq[tb * 512:(tb + 1) * 512, k2 * 128:(k2 + 1) * 128], transpose=True),
               waits=[c_xnqw.at(15), c_mm4.now()], inc=c_xnqfm)
    op("vector", lambda e: e.tensor_scalar(
        tcol[:, 32:48], tcol[:, 0:16], thr_col[:, 3:4], None, AO.mult),
       waits=[c_deqx.at(NT - 1), c_thr.now()], inc=c_sgu)
    op("vector", lambda e: e.tensor_scalar(
        tcol[:, 48:64], tcol[:, 0:16], thr_col[:, 7:8], None, AO.mult), inc=c_sgu)
    op("vector", lambda e: e.memset(tcol[:, 16:32], 0.0), inc=c_rmax)
    rmax0 = c_rmax.n - 1

    ngu = 0
    for ns in range(8):
        gsl = (ns % 2) * 4096
        for kk in range(NKK):
            gw2 = [c_wqw.at(wqw_end[0])]
            if ns >= 2:
                gw2.append(c_mmg5.at((ns - 2) * 16 + 15))
            op("sync", lambda e, n2=ns, k2=kk, s2=gsl: e.dma_start(
                out=wstr[:, s2 + k2 * 512: s2 + (k2 + 1) * 512],
                in_=s_wg[n2 * 512:(n2 + 1) * 512, k2 * 128:(k2 + 1) * 128], transpose=True),
               waits=gw2 + ([c_mmS.now(), c_mmg.now(), c_pv.now(), c_onorm.now()]
                            if ns < 2 else []),
               inc=c_wgu)
        for kk in range(NKK):
            uw = [c_wqw.at(wqw_end[1])]
            if ns >= 1:
                uw.append(c_mmu5.at((ns - 1) * 16 + 15))
            else:
                uw += [c_mm2b.now(), c_mmvg.now(), c_bce.now()]
            op("sync", lambda e, n2=ns, k2=kk: e.dma_start(
                out=wstr[:, 8192 + k2 * 512: 8192 + (k2 + 1) * 512],
                in_=s_wu[n2 * 512:(n2 + 1) * 512, k2 * 128:(k2 + 1) * 128], transpose=True),
               waits=uw, inc=c_wguu)
        wg_rdy = c_wgu.at(ns * 8 + 7)
        wu_rdy = c_wguu.at(ns * 8 + 7)
        for mt in range(NT):
            big = gu_big[(ngu // 2) % 2]
            bcol = (mt % 2) * 512
            bg, bu = (mt % 2) * 2, (mt % 2) * 2 + 1
            mmsg = [(pb[bg][0:128, 0:512],
                     big2[:, kk * 2048 + mt * 128: kk * 2048 + (mt + 1) * 128],
                     wstr[:, gsl + kk * 512: gsl + (kk + 1) * 512]) for kk in range(NKK)]
            i_mmg = mm_group(bg, mmsg, waits=[wg_rdy, c_xnqfm.at(31)], inc=c_mmg5)
            gev = guev[(mt % 2) * 2]
            gevw = [c_mmg5.at(i_mmg), c_sgu.at(0)]
            gevw += [c_gum.at(c_gum.n - 2)] if c_gum.n >= 2 else \
                [c_xr.at(NT - 1), c_mwr.at(H - 1), c_t2.at(NT - 1)]
            i_ge = op("scalar", lambda e, b2=bg, m3=mt, g2=gev: e.activation(
                g2, pb[b2][0:128, 0:512], AF.Silu, scale=tcol[:, 32 + m3:33 + m3]),
                waits=gevw, inc=c_ge)
            banks[bg].consumed(c_ge, i_ge)
            mmsu = [(pb[bu][0:128, 0:512],
                     big2[:, kk * 2048 + mt * 128: kk * 2048 + (mt + 1) * 128],
                     wstr[:, 8192 + kk * 512: 8192 + (kk + 1) * 512]) for kk in range(NKK)]
            i_mmu = mm_group(bu, mmsu, waits=[wu_rdy, c_xnqfm.at(31)], inc=c_mmu5)
            uev = guev[(mt % 2) * 2 + 1]
            uevw = [c_mmu5.at(i_mmu), c_sgu.at(1)]
            uevw += [c_gum.at(c_gum.n - 2)] if c_gum.n >= 2 else \
                [c_xr.at(NT - 1), c_mwr.at(H - 1), c_t2.at(NT - 1)]
            i_ue = op("scalar", lambda e, b2=bu, m3=mt, u2=uev: e.activation(
                u2, pb[b2][0:128, 0:512], AF.Copy, scale=tcol[:, 48 + m3:49 + m3]),
                waits=uevw, inc=c_ue)
            banks[bu].consumed(c_ue, i_ue)
            guw = [c_guw.at((ngu - 4) // 2), c_rmax.at(rmax0 + ngu - 3)] if ngu >= 4 else \
                ([c_wq.last(), c_wc1.last()] if ngu < 2 else [])
            i_gm = op("gpsimd", lambda e, g2=gev, u2=uev, b3=big, c2=bcol: e.tensor_tensor(
                b3[:, c2:c2 + 512], g2, u2, AO.mult),
                waits=[c_ge.at(i_ge), c_ue.at(i_ue)] + guw, inc=c_gum)
            op("vector", lambda e, b3=big, c2=bcol: e.reduce_max(
                tcol[:, 66:67], b3[:, c2:c2 + 512], axis=AX, apply_absolute_value=True),
               waits=[c_gum.at(i_gm), c_rmax.at(rmax0)])
            op("vector", lambda e, m3=mt: e.tensor_tensor(
                tcol[:, 16 + m3:17 + m3], tcol[:, 16 + m3:17 + m3], tcol[:, 66:67], AO.max),
               inc=c_rmax)
            if mt % 2 == 1:
                op("gpsimd", lambda e, b3=big, n2=ns, m3=mt: e.dma_start(
                    out=s_gu.rearrange("(m p) c -> p m c", p=128)[:, m3 - 1:m3 + 1,
                                                                 n2 * 512:(n2 + 1) * 512],
                    in_=b3.rearrange("p (m c) -> p m c", c=512)),
                   waits=[c_gum.at(c_gum.n - 1)], inc=c_guw)
            ngu += 1

    for mt in range(NT):
        gu_rd = k_fm[:, (mt % 2) * 4096: (mt % 2) * 4096 + 4096]
        rw = [c_guw.at(c_guw.n - 1)]
        if mt >= 2:
            rw.append(c_qt.at((mt - 2) * 4 + 3))
        else:
            rw += [c_mmS.now(), c_pv.now(), c_mmax.now()]
        op("sync", lambda e, m3=mt, g2=gu_rd: e.dma_start(
            out=g2, in_=s_gu[m3 * 128:(m3 + 1) * 128, :]), waits=rw, inc=c_gurd)
        scw = [c_rmax.last()]
        if mt >= 1:
            scw.append(c_qt.at((mt - 1) * 4 + 3))
        op("vector", lambda e, m3=mt: e.tensor_scalar(
            tcol[:, 66:67], tcol[:, 16 + m3:17 + m3], 1e-5, None, AO.max), waits=scw)
        op("vector", lambda e: e.reciprocal(tcol[:, 67:68], tcol[:, 66:67]))
        op("vector", lambda e: e.tensor_scalar(tcol[:, 68:69], tcol[:, 67:68], 127.0, None, AO.mult))
        op("vector", lambda e, m3=mt: e.scalar_tensor_tensor(
            tcol[:, 80 + m3:81 + m3], tcol[:, 66:67], 1.0 / 127.0,
            thr_col[:, 11:12], AO.mult, AO.mult),
           waits=[c_thr.now()], inc=c_sgu)
        for qq in range(4):
            qi4 = mt * 4 + qq
            gslice = gu_rd[:, qq * 1024:(qq + 1) * 1024]
            tw = [c_gurd.at(mt), c_sgu.last()]
            if qi4 >= 1:
                tw.append(c_d2.at(qi4 - 1))
            else:
                tw += [c_gum.now(), c_guw.at(c_guw.n - 1)]
            i_qt = op("scalar", lambda e, g2=gslice: e.activation(
                gub2_t[:, 0:1024], g2, AF.Copy, scale=tcol[:, 68:69]), waits=tw, inc=c_qt)
            yw = [c_qt.at(i_qt)]
            yw += [c_y8c.at(qi4 - 1)] if qi4 >= 1 else [c_y8b1.at(NT - 1)]
            i_y8 = op("scalar", lambda e: e.activation(y8_t, gub2_t[:, 0:1024], AF.Copy),
                      waits=yw, inc=c_y8b)
            cw2 = [c_y8b.at(i_y8)]
            cw2 += [c_guq.at(qi4 - 1)] if qi4 >= 1 else [c_wq.last(), c_wc1.last()]
            i_yc = op("vector", lambda e: e.tensor_copy(c1_t[:, 1024:2048], y8_t),
                      waits=cw2, inc=c_y8c)
            dw = [c_y8c.at(i_yc)]
            if qi4 == 0:
                dw.append(c_guw.at(c_guw.n - 1))
            i_d2 = op("vector", lambda e: e.tensor_tensor(
                wprep_t[:, 0:1024], gub2_t[:, 0:1024], c1_t[:, 1024:2048], AO.subtract),
                waits=dw, inc=c_d2)
            i_c1 = op("vector", lambda e: e.tensor_scalar(
                m2_t[:, 1024:2048], wprep_t[:, 0:1024], 0.5, None, AO.is_ge),
                waits=[c_d2.at(i_d2)], inc=c_c1b)
            m2w = [c_c1b.at(i_c1)]
            m2w += [c_guq.at(qi4 - 1)] if qi4 >= 1 else [c_xnq.at(NT - 1), c_m2.at(NT - 1)]
            i_m2 = op("vector", lambda e: e.scalar_tensor_tensor(
                m2_t[:, 0:1024], wprep_t[:, 0:1024], -0.5, m2_t[:, 1024:2048],
                AO.is_le, AO.subtract),
                waits=m2w, inc=c_m2b)
            guq_w = k_fm[:, 8192 + (qi4 % 2) * 1024: 8192 + (qi4 % 2) * 1024 + 1024]
            gqw = [c_m2b.at(i_m2)]
            gqw += [c_guqw.at(qi4 - 2)] if qi4 >= 2 else [c_mmS.now(), c_pv.now()]
            i_gq = op("gpsimd", lambda e, g3=guq_w: e.tensor_tensor(
                g3, c1_t[:, 1024:2048], m2_t[:, 0:1024], AO.subtract), waits=gqw, inc=c_guq)
            op("scalar", lambda e, m3=mt, q2=qq, g3=guq_w: e.dma_start(
                out=s_guq[m3 * 128:(m3 + 1) * 128, q2 * 1024:(q2 + 1) * 1024], in_=g3),
               waits=[c_guq.at(i_gq)], inc=c_guqw)

    for qB in range(4):
        for kk in range(32):
            qd = "sync"
            fw = [c_guqw.at(qB * 16 + 15)]
            if qB >= 1:
                fw.append(c_dnmm.at(qB - 1))
            else:
                fw += [c_mmax.now(), c_mmS.now(), c_pv.now(), c_qke.now()]
            op(qd, lambda e, q2=qB, k2=kk: e.dma_start(
                out=q_fm[:, k2 * 512:(k2 + 1) * 512],
                in_=s_guq[q2 * 512:(q2 + 1) * 512, k2 * 128:(k2 + 1) * 128], transpose=True),
               waits=fw, inc=c_guqfm)
        base_dn = c_dnmm.n
        for kk in range(32):
            wsl = 8192 + (kk % 2) * 1024
            ww = []
            if kk >= 2:
                ww.append(c_dnk.at(qB * 31 + kk - 2))
            elif qB >= 1:
                ww.append(c_dnk.at((qB - 1) * 31 + 30) if kk == 0 else c_dnmm.at(qB - 1))
            else:
                ww += [c_pv.now()]
            ww.append(c_wqw.at(wqw_end[2]))
            i_wd = op("sync",
                      lambda e, k2=kk, w2=wsl: e.dma_start(
                          out=v_tm[:, w2:w2 + 1024],
                          in_=s_wd[:, k2 * 128:(k2 + 1) * 128], transpose=True),
                      waits=ww, inc=c_wdt)
            for mt4 in range(4):
                for hf in range(2):
                    bi = mt4 * 2 + hf
                    bw = banks[bi].acquire() if kk == 0 else []
                    last_of_kk = (mt4 == 3 and hf == 1)
                    op("tensor", MM(pb[bi][0:128, 0:512],
                                    q_fm[:, kk * 512 + mt4 * 128: kk * 512 + (mt4 + 1) * 128],
                                    v_tm[:, wsl + hf * 512: wsl + (hf + 1) * 512],
                                    kk == 0, kk == 31),
                       waits=([c_guqfm.at(qB * 32 + kk), c_wdt.at(i_wd)]
                              if (mt4 == 0 and hf == 0) else []) + bw,
                       inc=(c_dnk if (last_of_kk and kk < 31) else
                            (c_dnmm if (last_of_kk and kk == 31) else None)))
        i_dn_all = c_dnmm.n - 1
        for mt4 in range(4):
            mt = qB * 4 + mt4
            b = mt4 % 2
            aw = [c_accw.at(mt)]
            if mt >= 2:
                aw.append(c_fin.at((mt - 2) * 2 + 1))
            else:
                aw.append(c_acce.at(2 * NT - 1))
                aw.append(c_x2.at(NT - 1))
            op("sync", lambda e, m3=mt, bb=b: e.dma_start(
                out=x_t[bb], in_=s_acc[m3 * 128:(m3 + 1) * 128, :]), waits=aw, inc=c_acr)
            for hf in range(2):
                bi = mt4 * 2 + hf
                fw2 = [c_dnmm.at(i_dn_all), c_acr.at(mt)]
                if mt >= 2:
                    fw2.append(c_out.at(c_out.n - 2))
                else:
                    fw2.append(c_accw.at(NT - 1))
                i_fin = op("vector", lambda e, b2=bi, bb=b, m3=mt, h2=hf: e.scalar_tensor_tensor(
                    xn_t[bb][:, h2 * 512:(h2 + 1) * 512], pb[b2][0:128, 0:512],
                    tcol[:, 80 + m3:81 + m3],
                    x_t[bb][:, h2 * 512:(h2 + 1) * 512], AO.mult, AO.add),
                    waits=fw2, inc=c_fin)
                banks[bi].consumed(c_fin, i_fin)
            op("sync", lambda e, m3=mt, bb=b: e.dma_start(
                out=y_h[m3 * 128:(m3 + 1) * 128, :], in_=xn_t[bb]),
               waits=[c_fin.at(mt * 2 + 1)], inc=c_out)

    if debug:
        op("sync", lambda e: e.dma_start(out=d_tc, in_=tcol), waits=[c_sgu.now()], inc=c_out)
    with nc.Block() as block:
        prog.realize(block, [c_out.now()])
    es.close()
    return nc


def kernel(L0_attn_x, L0_attn_in_proj_w, L0_attn_in_proj_b, L0_attn_out_w,
           L0_attn_out_b, L0_ffn_w_gate, L0_ffn_w_up, L0_ffn_w_down):
    x = np.ascontiguousarray(np.asarray(L0_attn_x, dtype=np.float32))
    ctab, mask_e, mask_o, identm8 = host_consts()
    base = dict(
        w_in=np.ascontiguousarray(np.asarray(L0_attn_in_proj_w, np.float32)),
        b_in=np.ascontiguousarray(np.asarray(L0_attn_in_proj_b, np.float32)).reshape(1, D3),
        w_o=np.ascontiguousarray(np.asarray(L0_attn_out_w, np.float32)),
        b_o=np.ascontiguousarray(np.asarray(L0_attn_out_b, np.float32)).reshape(1, D),
        w_g=np.ascontiguousarray(np.asarray(L0_ffn_w_gate, np.float32)),
        w_u=np.ascontiguousarray(np.asarray(L0_ffn_w_up, np.float32)),
        w_d=np.ascontiguousarray(np.asarray(L0_ffn_w_down, np.float32)),
        ctab=ctab, mask_e=mask_e, mask_o=mask_o, identm8=identm8,
    )
    nc = build("full")
    in_maps = [dict(base, x=np.ascontiguousarray(x[b])) for b in range(8)]
    res = run_bass_kernel_spmd(nc, in_maps, core_ids=list(range(8)))
    out = np.stack([np.asarray(r["y"], np.float32) for r in res.results], axis=0)
    return out


# version 04 (final)
# speedup vs baseline: 1.5537x; 1.0486x over previous
import numpy as np
import ml_dtypes
import concourse.bass as bass
import concourse.mybir as mybir
from concourse.bass_utils import run_bass_kernel_spmd
from concourse.alu_op_type import AluOpType as AO
from contextlib import ExitStack

F32 = mybir.dt.float32
BF16 = mybir.dt.bfloat16
I8 = mybir.dt.int8
AF = mybir.ActivationFunctionType
AX = mybir.AxisListType.X

T, D, H, HD, D3, INNER = 2048, 1024, 16, 64, 3072, 4096
NT = T // 128
NKK = D // 128
NEG = -1e9
ENGINES = ("sync", "vector", "scalar", "tensor", "gpsimd")


class Chan:
    __slots__ = ("sem", "unit", "n")

    def __init__(self, sem, unit):
        self.sem, self.unit, self.n = sem, unit, 0

    def at(self, i):
        assert 0 <= i < self.n, f"wait on future push ({i} >= {self.n})"
        return (self.sem, (i + 1) * self.unit)

    def now(self):
        return (self.sem, self.n * self.unit) if self.n else None

    def last(self):
        return self.at(self.n - 1)


class Prog:

    CHAINED = ("vector", "scalar", "gpsimd")

    def __init__(self, nc, es, serial=False):
        self.nc, self.es = nc, es
        self.ops = {e: [] for e in ENGINES}
        self.serial = serial
        self.prev = None
        self.eprev = {e: None for e in ENGINES}
        self.fill = {}
        for e in self.CHAINED:
            sem = self.es.enter_context(nc.semaphore(name=f"fill_{e}"))
            self.fill[e] = Chan(sem, 1)
        if serial:
            self.sfill = Chan(self.es.enter_context(nc.semaphore(name="sfill")), 1)

    def chan(self, unit=1):
        sem = self.es.enter_context(self.nc.semaphore())
        return Chan(sem, unit)

    def op(self, eng, fn, waits=(), inc=None):
        if inc is None and (self.serial or eng in self.CHAINED):
            inc = self.sfill if self.serial else self.fill[eng]
        idx = None
        tgt = None
        if inc is not None:
            idx = inc.n
            inc.n += 1
            tgt = (inc.sem, inc.unit)
        ws = tuple(w for w in waits if w is not None)
        if self.serial:
            if self.prev is not None:
                ws = ws + (self.prev,)
            self.prev = (inc.sem, inc.n * inc.unit)
        elif eng in self.CHAINED:
            if self.eprev[eng] is not None:
                ws = ws + (self.eprev[eng],)
            self.eprev[eng] = (inc.sem, inc.n * inc.unit)
        self.ops[eng].append((fn, ws, tgt))
        return idx

    def realize(self, block, final_waits):
        def mk(name):
            ops = self.ops[name]

            def body(eng):
                guaranteed = {}
                for fn, waits, tgt in ops:
                    for (sem, val) in waits:
                        k = id(sem)
                        if guaranteed.get(k, -1) >= val:
                            continue
                        eng.wait_ge(sem, val)
                        guaranteed[k] = val
                    ins = fn(eng)
                    if tgt is not None:
                        ins.then_inc(tgt[0], tgt[1])
                if name == "sync":
                    for w in final_waits:
                        if w is not None:
                            eng.wait_ge(w[0], w[1])
            return body

        block.sync(mk("sync"))
        block.vector(mk("vector"))
        block.scalar(mk("scalar"))
        block.tensor(mk("tensor"))
        block.gpsimd(mk("gpsimd"))


class PBank:
    def __init__(self, ap):
        self.ap = ap
        self.last = []

    def acquire(self):
        w = self.last
        self.last = []
        return list(w)

    def consumed(self, chan, idx):
        self.last.append(chan.at(idx))


def host_consts():
    j = np.arange(0, HD, 2, dtype=np.float64)
    inv_freq = 1.0 / (10000.0 ** (j / HD))
    c = np.arange(32)
    gidx = (2 * c) % 32
    pos = np.arange(T, dtype=np.float64)
    theta = pos[:, None] * inv_freq[gidx][None, :]
    cos, sin = np.cos(theta), np.sin(theta)
    Cfull = np.zeros((T, D), dtype=np.float64)
    S2full = np.zeros((T, D), dtype=np.float64)
    for h in range(H):
        b = h * HD
        Cfull[:, b + 0:b + HD:2] = cos
        Cfull[:, b + 1:b + HD:2] = cos
        S2full[:, b + 0:b + HD:2] = -sin
        S2full[:, b + 1:b + HD:2] = sin
    ctab = np.concatenate([Cfull, S2full], axis=1).astype(np.float32)

    kp = np.arange(128)[:, None]
    qf = np.arange(128)[None, :]
    m = np.zeros((128, 384), dtype=np.float32)
    m[:, 0:128] = (kp <= qf)
    m[:, 128:256] = 1.0
    m[:, 256:384] = (qf <= kp)
    mask_o = m.astype(ml_dtypes.bfloat16)
    me = m.copy()
    me[0, :] = 0.0
    mask_e = me.astype(ml_dtypes.bfloat16)
    identm8 = np.eye(128).astype(np.float32)
    return ctab, mask_e, mask_o, identm8


def build(upto="full", debug=False, serial=False):
    nc = bass.Bass("TRN2", target_bir_lowering=False)
    es = ExitStack()
    stage = {"p1": 1, "p2": 2, "p3": 3, "p4": 4, "full": 6}[upto]

    dn = lambda n, s, dt, k: nc.dram_tensor(n, s, dt, kind=k).ap()
    x_h = dn("x", [T, D], F32, "ExternalInput")
    w_in = dn("w_in", [D3, D], F32, "ExternalInput")
    b_in = dn("b_in", [1, D3], F32, "ExternalInput")
    w_o = dn("w_o", [D, D], F32, "ExternalInput")
    b_o = dn("b_o", [1, D], F32, "ExternalInput")
    w_g = dn("w_g", [INNER, D], F32, "ExternalInput")
    w_u = dn("w_u", [INNER, D], F32, "ExternalInput")
    w_d = dn("w_d", [D, INNER], F32, "ExternalInput")
    ctab_h = dn("ctab", [T, 2 * D], F32, "ExternalInput")
    mask_e_h = dn("mask_e", [128, 384], BF16, "ExternalInput")
    mask_o_h = dn("mask_o", [128, 384], BF16, "ExternalInput")
    ident_h = dn("identm8", [128, 128], F32, "ExternalInput")
    y_h = dn("y", [T, D], F32, "ExternalOutput")

    sk = "ExternalOutput" if debug else "Internal"
    s_win = dn("s_win", [D3, D], BF16, sk)
    s_wo = dn("s_wo", [D, D], BF16, sk)
    s_wg = dn("s_wg", [INNER, D], BF16, sk)
    s_wu = dn("s_wu", [INNER, D], BF16, sk)
    s_wd = dn("s_wd", [D, INNER], BF16, sk)
    s_xr = dn("s_xr", [T, D], BF16, sk)
    s_xnq = dn("s_xnq", [T, D], BF16, sk)
    s_m = dn("s_m", [H, T], F32, sk)
    s_gu = dn("s_gu", [T, INNER], BF16, sk)
    s_guq = dn("s_guq", [T, INNER], BF16, sk)
    s_acc = dn("s_acc", [T, D], F32, sk)
    d_qfm = dn("d_qfm", [128, 16384], BF16, sk) if debug else None
    d_kfm = dn("d_kfm", [128, 16384], BF16, sk) if debug else None
    d_vtm = dn("d_vtm", [128, 16640], BF16, sk) if debug else None
    d_ofm = dn("d_ofm", [128, 16384], BF16, sk) if debug else None
    d_tc = dn("d_tc", [128, 96], F32, sk) if debug else None
    d_xn = dn("d_xn", [T, D], F32, sk) if debug else None
    d_pg = dn("d_pg", [8, 2048], BF16, sk) if debug else None
    d_vg = dn("d_vg", [8, 1040], BF16, sk) if debug else None
    d_m8 = dn("d_m8", [1, 2048], BF16, sk) if debug else None
    d_g = dn("d_g", [T, INNER], F32, sk) if debug else None
    d_u = dn("d_u", [T, INNER], F32, sk) if debug else None
    d_ws = dn("d_ws", [128, 12288], BF16, sk) if debug else None
    d_sc = dn("d_sc", [128, 32], F32, sk) if debug else None
    d_scall = dn("d_scall", [16, 16], F32, sk) if debug else None

    def sbuf(name, shape, dt):
        return es.enter_context(nc.sbuf_tensor(name, shape, dt)).ap()

    x_t = [sbuf(f"xt{i}", [128, 1024], F32) for i in range(2)]
    xn_t = [sbuf(f"xnt{i}", [128, 1024], F32) for i in range(2)]
    ctab_t = sbuf("ctab_t", [128, 1024], F32)
    d_t = sbuf("d_t", [128, 1024], F32)
    c1_t = sbuf("c1_t", [128, 2048], BF16)
    m2_t = sbuf("m2_t", [128, 2048], BF16)
    wprep_t = sbuf("wprep_t", [128, 1024], F32)
    gub2_t = sbuf("gub2_t", [128, 1024], F32)
    wq_t = sbuf("wq_t", [128, 1024], BF16)
    b_bias = [sbuf(f"bb{i}", [1, 512], F32) for i in range(2)]
    ident_s = sbuf("ident_s", [128, 128], F32)
    ones_f = sbuf("ones_f", [128, 128], F32)
    ones_b = sbuf("ones_b", [128, 128], BF16)
    tcol = sbuf("tcol", [128, 96], F32)
    scols = [sbuf(f"scol{i}", [128, 16], F32) for i in range(2)]
    scal = tcol[0:1, 70:79]
    thr_col = sbuf("thr_col", [128, 12], F32)
    b_cols = sbuf("b_cols", [128, 16], F32)
    mcols = [b_cols, scols[1]]
    y8_t = sbuf("y8_t", [128, 1024], I8)
    q_fm = sbuf("q_fm", [128, 16384], BF16)
    k_fm = sbuf("k_fm", [128, 16384], BF16)
    v_tm = sbuf("v_tm", [128, 16640], BF16)
    y8b_t = v_tm[:, 0:1024]
    big2 = sbuf("big2", [128, 16384], BF16)
    wstr = sbuf("wstr", [128, 12288], BF16)
    b7 = sbuf("b7", [128, 3072], BF16)
    v_g = wstr[0:8, 0:1040]
    mask_es = b7[:, 0:384]
    mask_os = b7[:, 384:768]
    pt_sb = [b7[:, 1024:1408], b7[:, 1408:1792]]
    guev = [ctab_t[:, 0:512], ctab_t[:, 512:1024], d_t[:, 0:512], d_t[:, 512:1024]]
    mtmp = d_t
    l_rows = x_t
    gu_big = [wprep_t, gub2_t]

    pb = [es.enter_context(nc.psum_tensor(f"pb{i}", [128, 512], F32)).ap() for i in range(8)]

    prog = Prog(nc, es, serial=serial)
    op = prog.op
    banks = [PBank(p) for p in pb]

    def MM(o, l, r, s, p):
        return lambda e, o=o, l=l, r=r, s=s, p=p: e.matmul(
            o, l, r, start=s, stop=p, skip_group_check=True)

    def mm_group(bi, mms, waits, inc):
        bw = banks[bi].acquire()
        n = len(mms)
        idx = None
        for i2, (o, l, r) in enumerate(mms):
            idx = op("tensor", MM(o, l, r, i2 == 0, i2 == n - 1),
                     waits=(list(waits) + bw) if i2 == 0 else (),
                     inc=inc if i2 == n - 1 else None)
        return idx

    ch = lambda u=1: prog.chan(u)
    c_init, c_mset = ch(16), ch()
    c_bias = ch(16)
    c_x, c_ct = ch(16), ch(16)
    c_v1, c_ssq, c_var, c_rstd, c_xn, c_sq2 = ch(), ch(), ch(), ch(), ch(), ch()
    c_sx, c_deqx, c_y8, c_y8b1, c_dq, c_c1, c_m2, c_xnq = ch(), ch(), ch(), ch(), ch(), ch(), ch(), ch()
    c_xnqw, c_t1, c_t2, c_xr, c_xrw, c_xrfm = ch(16), ch(), ch(), ch(), ch(16), ch(16)
    c_wld, c_wcast, c_wabs, c_wsum, c_wmm, c_thr = ch(16), ch(16), ch(), ch(), ch(), ch()
    c_wld2, c_wc1, c_wq, c_wqw = ch(16), ch(), ch(), ch(16)
    c_wqk, c_mm2a, c_qke = ch(16), ch(), ch()
    c_wv, c_mm2b, c_ve, c_mmvg, c_vg = ch(16), ch(), ch(), ch(), ch()
    c_mmax, c_mred, c_trm, c_mtr, c_mwr, c_m8 = ch(), ch(), ch(), ch(), ch(16), ch(16)
    c_mmg, c_pg, c_mmS, c_ms, c_expS, c_mask = ch(), ch(), ch(), ch(), ch(), ch()
    c_pv, c_oevac, c_lcopy, c_recip, c_bcmm, c_bce, c_onorm = ch(), ch(), ch(), ch(), ch(), ch(), ch()
    c_wot, c_x2, c_mm4, c_acce, c_accw = ch(16), ch(16), ch(), ch(), ch(16)
    c_xnqfm, c_wgu, c_wguu, c_mmg5, c_ge, c_mmu5, c_ue = ch(16), ch(16), ch(16), ch(), ch(), ch(), ch()
    c_gum, c_rmax, c_guw = ch(), ch(), ch(16)
    c_gurd, c_sgu, c_qt, c_y8b, c_y8c, c_d2, c_c1b, c_m2b, c_guq, c_guqw = \
        ch(16), ch(), ch(), ch(), ch(), ch(), ch(), ch(), ch(), ch(16)
    c_guqfm, c_wdt, c_dnk, c_dnmm, c_acr, c_fin, c_out = ch(16), ch(16), ch(), ch(), ch(16), ch(), ch(16)
    c_dbg = ch(16)

    op("sync", lambda e: e.dma_start(out=ident_s, in_=ident_h), inc=c_init)
    op("vector", lambda e: e.memset(ones_f, 1.0), inc=c_mset)
    op("vector", lambda e: e.memset(ones_b, 1.0), inc=c_mset)
    op("vector", lambda e: e.memset(ones_b[32:33, :], -1.0))
    op("vector", lambda e: e.memset(ones_b[64:65, :], -1.0))
    op("vector", lambda e: e.memset(
        v_tm.rearrange("p (t c) -> p t c", c=65)[:, :, 64:65], 1.0), inc=c_mset)
    op("vector", lambda e: e.memset(tcol[:, 69:70], 1e-5), inc=c_mset)

    bw6 = banks[6].acquire()
    for q4 in range(4):
        bb = b_bias[q4 % 2]
        lw = [c_wmm.at(q4 - 2)] if q4 >= 2 else []
        op("sync", lambda e, b2=bb, qq=q4: e.dma_start(
            out=b2, in_=b_in[:, qq * 512:(qq + 1) * 512]), waits=lw, inc=c_bias)
        for m in range(4):
            op("tensor", MM(pb[6][:, q4 * 4 + m:q4 * 4 + m + 1],
                            bb[0:1, m * 128:(m + 1) * 128],
                            ones_f[0:1, 0:1], q4 == 0 and m == 0, False),
               waits=[c_bias.at(q4), c_mset.at(0)] + (bw6 if (q4 == 0 and m == 0) else []),
               inc=(c_wmm if m == 3 else None))
    i = op("scalar", lambda e: e.copy(b_cols, pb[6][:, 0:16]), waits=[c_wmm.at(3)], inc=c_thr)
    i_bcol = i
    banks[6].consumed(c_thr, i)

    xr_slot = b7[:, 0:1024]
    xnq_slot = b7[:, 1024:2048]
    t2_slot = b7[:, 2048:3072]
    for i in range(NT):
        b = i % 2
        sc = scols[b]
        xw = [c_xn.at(i - 2), c_ssq.at(i - 2)] if i >= 2 else []
        op("sync", lambda e, ii=i, bb=b: e.dma_start(
            out=x_t[bb], in_=x_h[ii * 128:(ii + 1) * 128, :]), waits=xw, inc=c_x)
        cw = [c_t2.at(i - 1)] if i >= 1 else []
        op("sync", lambda e, ii=i: e.dma_start(
            out=ctab_t, in_=ctab_h[ii * 128:(ii + 1) * 128, 0:1024]), waits=cw, inc=c_ct)
        sqw = [c_xr.at(i - 1)] if i >= 1 else []
        vw = [c_y8.at(i - 2)] if i >= 2 else []
        op("scalar", lambda e, bb=b: e.activation(d_t, x_t[bb], AF.Square),
           waits=[c_x.at(i)] + sqw + vw, inc=c_sq2)
        op("vector", lambda e, bb=b, s=sc: e.reduce_sum(s[:, 0:1], x_t[bb], axis=AX),
           waits=[c_x.at(i)] + vw, inc=c_v1)
        op("vector", lambda e, s=sc: e.reduce_sum(s[:, 1:2], d_t, axis=AX),
           waits=[c_sq2.at(i)], inc=c_ssq)
        op("vector", lambda e, s=sc: e.tensor_scalar(s[:, 2:3], s[:, 0:1], 1.0 / D, None, AO.mult),
           waits=[c_v1.at(i)])
        op("vector", lambda e, s=sc: e.tensor_tensor(s[:, 3:4], s[:, 2:3], s[:, 2:3], AO.mult))
        op("vector", lambda e, s=sc: e.scalar_tensor_tensor(
            s[:, 4:5], s[:, 1:2], 1.0 / D, s[:, 3:4], AO.mult, AO.subtract),
           waits=[c_ssq.at(i)], inc=c_var)
        op("scalar", lambda e, s=sc: e.activation(s[:, 5:6], s[:, 4:5], AF.Ln, bias=tcol[:, 69:70]),
           waits=[c_var.at(i), c_mset.at(3)])
        op("scalar", lambda e, s=sc: e.activation(s[:, 6:7], s[:, 5:6], AF.Exp, scale=-0.5),
           inc=c_rstd)
        xnw = [c_y8.at(i - 2), c_t1.at(i - 2), c_t2.at(i - 2)] if i >= 2 else []
        op("vector", lambda e, bb=b, s=sc: e.tensor_scalar(
            xn_t[bb], x_t[bb], s[:, 2:3], s[:, 6:7], AO.subtract, AO.mult),
           waits=[c_rstd.at(i)] + xnw, inc=c_xn)
        if debug:
            op("scalar", lambda e, ii=i, bb=b: e.dma_start(
                out=d_xn[ii * 128:(ii + 1) * 128, :], in_=xn_t[bb]),
               waits=[c_xn.at(i)], inc=c_dbg)
            op("scalar", lambda e, ii=i, s=sc: e.dma_start(
                out=d_scall[ii:ii + 1, :], in_=s[0:1, 0:16]),
               waits=[c_xn.at(i)], inc=c_dbg)
        op("vector", lambda e, bb=b, s=sc: e.reduce_max(
            s[:, 7:8], xn_t[bb], axis=AX, apply_absolute_value=True), waits=[c_xn.at(i)])
        op("vector", lambda e, s=sc: e.tensor_scalar(s[:, 8:9], s[:, 7:8], 1e-5, None, AO.max))
        op("vector", lambda e, s=sc: e.reciprocal(s[:, 9:10], s[:, 8:9]))
        op("vector", lambda e, s=sc: e.tensor_scalar(s[:, 10:11], s[:, 9:10], 127.0, None, AO.mult),
           inc=c_sx)
        op("vector", lambda e, ii=i, s=sc: e.tensor_scalar(
            tcol[:, ii:ii + 1], s[:, 8:9], 1.0 / 127.0, None, AO.mult), inc=c_deqx)
        y8w = [c_y8b1.at(i - 1)] if i >= 1 else []
        op("scalar", lambda e, bb=b, s=sc: e.activation(
            y8_t, xn_t[bb], AF.Copy, scale=s[:, 10:11]),
           waits=[c_sx.at(i)] + y8w, inc=c_y8)
        yb = [c_xnq.at(i - 1)] if i >= 1 else []
        op("vector", lambda e: e.tensor_copy(y8b_t, y8_t), waits=[c_y8.at(i)] + yb, inc=c_y8b1)
        op("vector", lambda e, bb=b, s=sc: e.scalar_tensor_tensor(
            d_t, xn_t[bb], s[:, 10:11], y8b_t, AO.mult, AO.subtract),
           waits=[c_ssq.at(i)], inc=c_dq)
        cg = [c_m2.at(i - 1)] if i >= 1 else []
        op("vector", lambda e: e.tensor_scalar(c1_t[:, 0:1024], d_t, 0.5, None, AO.is_ge),
           waits=[c_dq.at(i)] + cg, inc=c_c1)
        op("vector", lambda e: e.scalar_tensor_tensor(
            m2_t[:, 0:1024], d_t, -0.5, c1_t[:, 0:1024], AO.is_le, AO.subtract),
           waits=[c_c1.at(i)], inc=c_m2)
        xqw = [c_xnqw.at(i - 1)] if i >= 1 else []
        op("gpsimd", lambda e: e.tensor_tensor(
            xnq_slot, y8b_t, m2_t[:, 0:1024], AO.subtract),
           waits=[c_m2.at(i), c_y8b1.at(i)] + xqw, inc=c_xnq)
        op("scalar", lambda e, ii=i: e.dma_start(
            out=s_xnq[ii * 128:(ii + 1) * 128, :], in_=xnq_slot),
           waits=[c_xnq.at(i)], inc=c_xnqw)
        op("vector", lambda e, bb=b: e.tensor_tensor(d_t, xn_t[bb], ctab_t, AO.mult),
           waits=[c_ct.at(2 * i), c_m2.at(i)], inc=c_t1)
        op("sync", lambda e, ii=i: e.dma_start(
            out=ctab_t, in_=ctab_h[ii * 128:(ii + 1) * 128, 1024:2048]),
           waits=[c_t1.at(i)], inc=c_ct)
        t2w = [c_xr.at(i - 1)] if i >= 1 else []
        op("gpsimd", lambda e, bb=b: e.tensor_tensor(
            t2_slot.rearrange("p (c two) -> p c two", two=2),
            xn_t[bb].rearrange("p (c two) -> p c two", two=2)[:, :, ::-1],
            ctab_t.rearrange("p (c two) -> p c two", two=2), AO.mult),
           waits=[c_ct.at(2 * i + 1)] + t2w, inc=c_t2)
        xrw = [c_xrw.at(i - 1)] if i >= 1 else []
        op("vector", lambda e: e.tensor_tensor(xr_slot, d_t, t2_slot, AO.add),
           waits=[c_t1.at(i), c_t2.at(i)] + xrw, inc=c_xr)
        op("scalar", lambda e, ii=i: e.dma_start(
            out=s_xr[ii * 128:(ii + 1) * 128, :], in_=xr_slot),
           waits=[c_xr.at(i)], inc=c_xrw)

    wbuf = [wprep_t, gub2_t]
    wbuf_last = [[], []]
    wj = 0
    for j in range(32):
        src_ = w_in[j * 128:(j + 1) * 128, :] if j < 24 else w_o[(j - 24) * 128:(j - 23) * 128, :]
        dst_ = s_win[j * 128:(j + 1) * 128, :] if j < 24 else s_wo[(j - 24) * 128:(j - 23) * 128, :]
        sl = wj % 2
        i_ld = op("sync", lambda e, s2=src_, w=wbuf[sl]: e.dma_start(out=w, in_=s2),
                  waits=wbuf_last[sl], inc=c_wld)
        i_c = op("gpsimd", lambda e, d2=dst_, w=wbuf[sl]: e.dma_start(out=d2, in_=w),
                 waits=[c_wld.at(i_ld)], inc=c_wcast)
        wbuf_last[sl] = [c_wcast.at(i_c)]
        wj += 1

    for tokB in range(4):
        for kk in range(NKK):
            qd = "sync"
            op(qd, lambda e, tb=tokB, k2=kk: e.dma_start(
                out=big2[:, k2 * 2048 + tb * 512: k2 * 2048 + (tb + 1) * 512],
                in_=s_xr[tb * 512:(tb + 1) * 512, k2 * 128:(k2 + 1) * 128], transpose=True),
               waits=[c_xrw.at(tokB * 4 + 3)], inc=c_xrfm)

    if stage <= 1:
        with nc.Block() as block:
            prog.realize(block, [c_xnqw.now(), c_xrfm.now(), c_wcast.now(), c_wqw.now(),
                                 c_dbg.now()])
        es.close()
        return nc

    nqk = 0
    for tokB in range(4):
        for mpair in range(8):
            sl = (mpair % 2) * 2048
            grp = tokB * 8 + mpair
            for kk in range(NKK):
                qd = "sync" if kk % 2 == 0 else "scalar"
                jw = [c_wcast.at(11), c_xrfm.at(tokB * 8 + 7)]
                if grp >= 2:
                    jw.append(c_mm2a.at((grp - 2) * 2 + 1))
                op(qd, lambda e, mp=mpair, k2=kk, s2=sl: e.dma_start(
                    out=wstr[:, s2 + k2 * 256: s2 + k2 * 256 + 256],
                    in_=s_win[mp * 256:(mp + 1) * 256, k2 * 128:(k2 + 1) * 128], transpose=True),
                   waits=jw, inc=c_wqk)
            for m2 in range(2):
                m = mpair * 2 + m2
                bi = nqk % 6
                mms = [(pb[bi][0:128, 0:512],
                        wstr[:, sl + kk * 256 + m2 * 128: sl + kk * 256 + (m2 + 1) * 128],
                        big2[:, kk * 2048 + tokB * 512: kk * 2048 + (tokB + 1) * 512])
                       for kk in range(NKK)]
                i_mm = mm_group(bi, mms,
                                waits=[c_wqk.at(grp * 8 + 7), c_xrfm.at(tokB * 8 + 7)],
                                inc=c_mm2a)
                dst = q_fm if m < 8 else k_fm
                coff = (m % 8) * 2048 + tokB * 512
                i_ev = op("vector", lambda e, b2=bi, d2=dst, c2=coff, mm=m: e.tensor_scalar(
                    d2[:, c2:c2 + 512], pb[b2][0:128, 0:512], b_cols[:, mm:mm + 1], None, AO.add),
                    waits=[c_mm2a.at(i_mm), c_thr.at(i_bcol)], inc=c_qke)
                banks[bi].consumed(c_qke, i_ev)
                nqk += 1

    op("vector", lambda e: e.memset(
        v_tm.rearrange("p (t c) -> p t c", c=65)[:, 0:15, 64:65], 1.0),
       waits=[c_y8b1.at(NT - 1), c_xnq.at(NT - 1)], inc=c_mset)
    for kk in range(NKK):
        qd = "sync"
        op(qd, lambda e, k2=kk: e.dma_start(
            out=wstr[:, 4096 + k2 * 1024: 4096 + (k2 + 1) * 1024],
            in_=s_win[2048:3072, k2 * 128:(k2 + 1) * 128], transpose=True),
           waits=[c_wcast.at(23), c_mm2a.now()], inc=c_wv)
    for half in range(2):
        op("sync", lambda e, hh=half: e.dma_start(
            out=b_bias[hh], in_=b_in[:, 2048 + hh * 512:2048 + (hh + 1) * 512]),
           waits=[c_wmm.at(3), c_mm2a.now()], inc=c_bias)
    nv = 0
    vv = v_tm.rearrange("p (t c) -> p t c", c=65)
    vg = v_g.rearrange("p (t c) -> p t c", c=65)
    for mt in range(NT):
        for half in range(2):
            bi = nv % 6
            mms = [(pb[bi][0:128, 0:512],
                    big2[:, kk * 2048 + mt * 128: kk * 2048 + (mt + 1) * 128],
                    wstr[:, 4096 + kk * 1024 + half * 512: 4096 + kk * 1024 + (half + 1) * 512])
                   for kk in range(NKK)]
            mms.append((pb[bi][0:128, 0:512], ones_f[0:1, 0:128], b_bias[half][0:1, 0:512]))
            i_mm = mm_group(bi, mms, waits=[c_wv.at(7), c_bias.at(4 + half), c_xrfm.at(31)], inc=c_mm2b)
            evw0 = [c_y8b1.at(NT - 1), c_xnq.at(NT - 1)] if nv <= 1 else []
            i_ev = op("vector", lambda e, b2=bi, m3=mt, h2=half: e.tensor_copy(
                vv[:, m3 * 16 + h2 * 8:m3 * 16 + (h2 + 1) * 8, 0:64],
                pb[b2][0:128, 0:512].rearrange("p (a b) -> p a b", b=64)),
                waits=[c_mm2b.at(i_mm), c_mset.at(2)] + evw0, inc=c_ve)
            banks[bi].consumed(c_ve, i_ev)
            nv += 1
    for half in range(2):
        mms = [(pb[6][0:8, 0:512],
                big2[:, kk * 2048: kk * 2048 + 2048:256],
                wstr[:, 4096 + kk * 1024 + half * 512: 4096 + kk * 1024 + (half + 1) * 512])
               for kk in range(NKK)]
        mms.append((pb[6][0:8, 0:512], ones_f[0:1, 0:8], b_bias[half][0:1, 0:512]))
        i_mm = mm_group(6, mms, waits=[c_wv.at(7), c_bias.at(4 + half), c_xrfm.at(31)], inc=c_mmvg)
        i_ev = op("vector", lambda e, h2=half: e.tensor_copy(
            vg[:, h2 * 8:(h2 + 1) * 8, 0:64],
            pb[6][0:8, 0:512].rearrange("p (a b) -> p a b", b=64)),
            waits=[c_mmvg.at(i_mm)], inc=c_vg)
        banks[6].consumed(c_vg, i_ev)
    op("vector", lambda e: e.memset(vg[:, :, 64:65], 1.0),
       waits=[c_wqk.last(), c_vg.at(c_vg.n - 1)], inc=c_vg)

    if debug:
        op("sync", lambda e: e.dma_start(out=d_qfm, in_=q_fm), waits=[c_qke.now()], inc=c_out)
        op("sync", lambda e: e.dma_start(out=d_kfm, in_=k_fm), waits=[c_qke.now()], inc=c_out)
        op("sync", lambda e: e.dma_start(out=d_vtm, in_=v_tm), waits=[c_ve.now(), c_vg.now()], inc=c_out)
    if stage <= 2:
        with nc.Block() as block:
            prog.realize(block, [c_qke.now(), c_ve.now(), c_vg.now(), c_out.now(),
                                 c_xnqw.now(), c_wqw.now()])
        es.close()
        return nc

    nmx = 0
    for h in range(H):
        chk, off = h // 2, 64 * (h % 2)
        mc = mcols[h % 2]
        for qi in range(NT):
            bi = nmx % 2
            qlo, qhi = max(qi - 1, 0), min(qi + 1, NT - 1)
            nq = (qhi - qlo + 1) * 128
            bw = banks[bi].acquire()
            mw = [c_qke.now()] + bw
            if qi == 0 and h >= 2:
                mw.append(c_trm.at(h - 2))
            op("tensor", MM(pb[bi][0:128, 0:nq],
                            q_fm[off:off + 64, chk * 2048 + qi * 128: chk * 2048 + (qi + 1) * 128],
                            k_fm[off:off + 64, chk * 2048 + qlo * 128: chk * 2048 + qlo * 128 + nq],
                            True, False), waits=mw)
            i_mm = op("tensor", MM(pb[bi][0:128, nq:nq + 8],
                                   q_fm[off:off + 64, chk * 2048 + qi * 128: chk * 2048 + (qi + 1) * 128],
                                   k_fm[off:off + 64, chk * 2048: chk * 2048 + 2048:256],
                                   False, True), inc=c_mmax)
            rdw = [c_mmax.at(i_mm)]
            if h < 2 and qi == 0:
                rdw += [c_qke.now()] if h == 0 else [c_y8.at(NT - 1), c_sx.at(NT - 1)]
            i_rd = op("vector", lambda e, b2=bi, q2=qi, n2=nq, m3=mc: e.reduce_max(
                m3[:, q2:q2 + 1], pb[b2][0:128, 0:n2 + 8], axis=AX),
                waits=rdw, inc=c_mred)
            banks[bi].consumed(c_mred, i_rd)
            nmx += 1
        bw = banks[6].acquire()
        i_tr = op("tensor", lambda e, m3=mc: e.transpose(pb[6][0:16, 0:128], m3, ident_s),
                  waits=[c_mred.last(), c_init.at(0)] + bw, inc=c_trm)
        i_ev = op("scalar", lambda e: e.copy(mtmp[0:16, 0:128], pb[6][0:16, 0:128]),
                  waits=[c_trm.at(i_tr), c_mwr.now(), c_xr.at(NT - 1)], inc=c_mtr)
        banks[6].consumed(c_mtr, i_ev)
        op("scalar", lambda e, h2=h: e.dma_start(
            out=s_m[h2:h2 + 1, :].rearrange("a (q c) -> (a q) c", c=128),
            in_=mtmp[0:16, 0:128]),
           waits=[c_mtr.at(i_ev)], inc=c_mwr)

    tern = [(w_g, s_wg, INNER * D), (w_u, s_wu, INNER * D), (w_d, s_wd, D * INNER)]
    wqw_end = []
    for wi, (wsrc, wdst, nelem) in enumerate(tern):
        nchunk = nelem // (128 * 1024)
        ncb = wsrc.shape[1] // 1024
        chunks = [wsrc[(j // ncb) * 128:(j // ncb + 1) * 128,
                       (j % ncb) * 1024:(j % ncb + 1) * 1024] for j in range(nchunk)]
        dchunks = [wdst[(j // ncb) * 128:(j // ncb + 1) * 128,
                        (j % ncb) * 1024:(j % ncb + 1) * 1024] for j in range(nchunk)]
        op("vector", lambda e: e.memset(tcol[:, 64:65], 0.0),
           waits=[c_wsum.now(), c_wmm.now()], inc=c_wsum)
        for j in range(nchunk):
            sl = wj % 2
            i_ld = op("sync", lambda e, f=chunks[j], w=wbuf[sl]: e.dma_start(
                out=w, in_=f), waits=wbuf_last[sl], inc=c_wld)
            i_ab = op("vector", lambda e, w=wbuf[sl]: e.reduce_sum(
                tcol[:, 65:66], w, axis=AX, apply_absolute_value=True),
                waits=[c_wld.at(i_ld), c_wsum.last()], inc=c_wabs)
            wbuf_last[sl] = [c_wabs.at(i_ab)]
            op("vector", lambda e: e.tensor_tensor(
                tcol[:, 64:65], tcol[:, 64:65], tcol[:, 65:66], AO.add),
               waits=[c_wabs.at(i_ab)], inc=c_wsum)
            wj += 1
        bw = banks[7].acquire()
        i_mm = op("tensor", MM(pb[7][0:1, 0:1], tcol[:, 64:65], ones_f[:, 0:1], True, True),
                  waits=[c_wsum.last()] + bw, inc=c_wmm)
        i_t = op("scalar", lambda e, w2=wi: e.copy(scal[0:1, w2:w2 + 1], pb[7][0:1, 0:1]),
                 waits=[c_wmm.at(i_mm)], inc=c_thr)
        banks[7].consumed(c_thr, i_t)
        op("vector", lambda e, w2=wi: e.tensor_scalar(
            scal[0:1, 3 + w2:4 + w2], scal[0:1, w2:w2 + 1], 1.0 / nelem, 1e-5, AO.mult, AO.max),
           waits=[c_thr.at(i_t)], inc=c_thr)
        op("vector", lambda e, w2=wi: e.tensor_scalar(
            scal[0:1, 6 + w2:7 + w2], scal[0:1, 3 + w2:4 + w2], 0.5, None, AO.mult), inc=c_thr)
        bw = banks[7].acquire()
        op("tensor", MM(pb[7][:, 0:1], ones_f[0:1, :], scal[0:1, 6 + wi:7 + wi], True, False),
           waits=[c_thr.last()] + bw)
        i_mm = op("tensor", MM(pb[7][:, 1:2], ones_f[0:1, :], scal[0:1, 3 + wi:4 + wi], False, True),
                  inc=c_wmm)
        i_t = op("scalar", lambda e, w2=wi: e.copy(thr_col[:, 4 * w2:4 * w2 + 2], pb[7][:, 0:2]),
                 waits=[c_wmm.at(i_mm)], inc=c_thr)
        banks[7].consumed(c_thr, i_t)
        op("vector", lambda e, w2=wi: e.tensor_scalar(
            thr_col[:, 4 * w2 + 2:4 * w2 + 3], thr_col[:, 4 * w2:4 * w2 + 1], -1.0, None, AO.mult),
           waits=[c_thr.at(i_t)], inc=c_thr)
        op("vector", lambda e, w2=wi: e.tensor_scalar(
            thr_col[:, 4 * w2 + 3:4 * w2 + 4], thr_col[:, 4 * w2 + 1:4 * w2 + 2], -1.0, None, AO.mult),
           inc=c_thr)
        thr_rdy = c_thr.last()
        for j in range(nchunk):
            sl = wj % 2
            i_ld = op("sync", lambda e, f=chunks[j], w=wbuf[sl]: e.dma_start(
                out=w, in_=f),
                waits=wbuf_last[sl] + [thr_rdy], inc=c_wld2)
            cg = [c_wq.last()] if c_wq.n else []
            i_c1 = op("vector", lambda e, w=wbuf[sl], w2=wi: e.tensor_scalar(
                c1_t[:, 1024:2048], w, thr_col[:, 4 * w2:4 * w2 + 1], None, AO.is_ge),
                waits=[c_wld2.at(i_ld), thr_rdy] + cg +
                      ([c_c1.at(NT - 1)] if (wi == 0 and j == 0) else []),
                inc=c_wc1)
            qw = [c_wqw.at(c_wqw.n - 1)] if c_wqw.n >= 1 else []
            i_wq = op("vector", lambda e, w=wbuf[sl], w2=wi: e.scalar_tensor_tensor(
                wq_t, w, thr_col[:, 4 * w2 + 2:4 * w2 + 3], c1_t[:, 1024:2048],
                AO.is_le, AO.subtract),
                waits=[c_wc1.at(i_c1)] + qw, inc=c_wq)
            wbuf_last[sl] = [c_wq.at(i_wq)]
            op("scalar", lambda e, fd=dchunks[j]: e.dma_start(out=fd, in_=wq_t),
               waits=[c_wq.at(i_wq)], inc=c_wqw)
            wj += 1
        wqw_end.append(c_wqw.n - 1)

    if debug:
        op("scalar", lambda e: e.dma_start(out=d_sc[:, 0:16], in_=scols[0]),
           waits=[c_deqx.at(NT - 2)], inc=c_dbg)
        op("scalar", lambda e: e.dma_start(out=d_sc[:, 16:32], in_=scols[1]),
           waits=[c_deqx.at(NT - 1)], inc=c_dbg)
    op("sync", lambda e: e.dma_start(out=mask_es, in_=mask_e_h),
       waits=[c_xrw.at(NT - 1), c_xnqw.at(NT - 1), c_xr.at(NT - 1)], inc=c_init)
    op("sync", lambda e: e.dma_start(out=mask_os, in_=mask_o_h), inc=c_init)


    m8row = [wstr[32:33, 2048:4096], wstr[64:65, 2048:4096]]
    P_g = wstr[0:8, 5120:7168]
    linv_bc = wstr[:, 4608:5120]
    last_m8 = [[], []]
    last_lrow = [[], []]
    last_pg = []
    pt_last = [None, None]

    for h in range(H):
        par = h % 2
        chk, off = h // 2, 64 * (h % 2)
        row = 64 * (1 - par)
        mw = [c_mwr.at(h)] + last_m8[par]
        if h < 2:
            mw.append(c_mm2a.now())
        op("gpsimd", lambda e, h2=h, p2=par: e.dma_start(out=m8row[p2], in_=s_m[h2:h2 + 1, :]),
           waits=mw, inc=c_m8)
        i_m8 = c_m8.n - 1
        last_m8[par] = []
        for qB in range(4):
            bw = banks[2].acquire()
            op("tensor", MM(pb[2][0:8, 0:512],
                            k_fm[off:off + 64, chk * 2048: chk * 2048 + 2048:256],
                            q_fm[off:off + 64, chk * 2048 + qB * 512: chk * 2048 + (qB + 1) * 512],
                            True, False),
               waits=[c_m8.at(i_m8), c_qke.now()] + bw)
            orow = 32 * (par + 1)
            i_mm = op("tensor", MM(pb[2][0:8, 0:512], ones_b[orow:orow + 1, 0:8],
                                   m8row[par][0:1, qB * 512:(qB + 1) * 512], False, True),
                      inc=c_mmg)
            last_m8[par] = [c_mmg.at(i_mm)]
            pw = [c_mmg.at(i_mm)]
            if qB == 0:
                pw += last_pg
                if h == 0:
                    pw += [c_mm2a.now(), c_mm2b.now(), c_mmvg.now()]
            i_pg = op("scalar", lambda e, q2=qB: e.activation(
                P_g[0:8, q2 * 512:(q2 + 1) * 512], pb[2][0:8, 0:512], AF.Exp, scale=0.125),
                waits=pw, inc=c_pg)
            banks[2].consumed(c_pg, i_pg)
        pg_base = c_pg.n - 4
        if debug and h == 0:
            op("scalar", lambda e: e.dma_start(out=d_pg, in_=P_g),
               waits=[c_pg.at(pg_base + 3)], inc=c_dbg)
            op("scalar", lambda e: e.dma_start(out=d_vg, in_=v_g),
               waits=[c_vg.now()], inc=c_dbg)
            op("scalar", lambda e, p2=par: e.dma_start(out=d_m8, in_=m8row[p2]),
               waits=[c_m8.at(i_m8)], inc=c_dbg)
        last_pg = []
        first_lcopy = True
        for kj in range(NT):
            bi = kj % 2
            qlo, qhi = max(kj - 1, 0), min(kj + 1, NT - 1)
            nq = (qhi - qlo + 1) * 128
            bw = banks[bi].acquire()
            op("tensor", MM(pb[bi][0:128, 0:nq],
                            k_fm[off:off + 64, chk * 2048 + kj * 128: chk * 2048 + (kj + 1) * 128],
                            q_fm[off:off + 64, chk * 2048 + qlo * 128: chk * 2048 + qlo * 128 + nq],
                            True, False),
               waits=[c_m8.at(i_m8)] + bw)
            orow = 32 * (par + 1)
            i_mm = op("tensor", MM(pb[bi][0:128, 0:nq], ones_b[orow:orow + 1, 0:128],
                                   m8row[par][0:1, qlo * 128: qlo * 128 + nq], False, True),
                      inc=c_mmS)
            last_m8[par].append(c_mmS.at(i_mm))
            expw = [c_mmS.at(i_mm)]
            if pt_last[bi] is not None:
                expw.append(pt_last[bi])
            else:
                expw.append(c_xnqw.at(NT - 1))
            i_exp = op("scalar", lambda e, b2=bi, n2=nq: e.activation(
                pt_sb[b2][:, 0:n2], pb[b2][0:128, 0:n2], AF.Exp, scale=0.125),
                waits=expw, inc=c_expS)
            banks[bi].consumed(c_expS, i_exp)
            msk = mask_es if kj % 2 == 0 else mask_os
            mslice = msk[:, 128:384] if kj == 0 else (msk[:, 0:256] if kj == NT - 1 else msk[:, 0:384])
            i_msk = op("vector", lambda e, b2=bi, n2=nq, m3=mslice: e.tensor_tensor(
                pt_sb[b2][:, 0:n2], pt_sb[b2][:, 0:n2], m3, AO.mult),
                waits=[c_expS.at(i_exp), c_init.at(2)], inc=c_mask)
            for qi in range(qlo, qhi + 1):
                slot = 3 + qi % 3
                first = (kj == max(qi - 1, 0))
                vbase = kj * 1040 + h * 65
                ptc = pt_sb[bi][:, (qi - qlo) * 128:(qi - qlo + 1) * 128]
                wl = [c_mask.at(i_msk), c_ve.now(), c_mset.at(4)]
                if first:
                    wl += banks[slot].acquire()
                if par == 0:
                    op("tensor", MM(pb[slot][0:65, 0:128], v_tm[:, vbase:vbase + 65], ptc,
                                    first, False), waits=wl, inc=c_pv)
                else:
                    op("tensor", MM(pb[slot][64:128, 0:128], v_tm[:, vbase:vbase + 64], ptc,
                                    first, False), waits=wl)
                    op("tensor", MM(pb[slot][0:1, 128:256], v_tm[:, vbase + 64:vbase + 65], ptc,
                                    first, False), inc=c_pv)
                if first:
                    gb = h * 65
                    pgc = P_g[0:8, qi * 128:(qi + 1) * 128]
                    gw = [c_pg.at(pg_base + qi // 4), c_vg.now()]
                    if par == 0:
                        op("tensor", MM(pb[slot][0:65, 0:128], v_g[0:8, gb:gb + 65], pgc,
                                        False, False), waits=gw, inc=c_pv)
                    else:
                        op("tensor", MM(pb[slot][64:128, 0:128], v_g[0:8, gb:gb + 64], pgc,
                                        False, False), waits=gw)
                        op("tensor", MM(pb[slot][0:1, 128:256], v_g[0:8, gb + 64:gb + 65], pgc,
                                        False, False), inc=c_pv)
                    last_pg = [c_pv.at(c_pv.n - 1)]
            pt_last[bi] = c_pv.at(c_pv.n - 1)
            closed = ([kj - 1] if kj >= 1 else []) + ([NT - 1] if kj == NT - 1 else [])
            for qi in closed:
                slot = 3 + qi % 3
                ocol = chk * 2048 + qi * 128
                ow = [c_pv.at(c_pv.n - 1), c_ve.now(), c_vg.now()]
                if h == 0:
                    ow += [c_mm2b.now(), c_mmvg.now()]
                lw = [c_pv.at(c_pv.n - 1)]
                if first_lcopy:
                    lw += last_lrow[par] + [c_xn.at(NT - 1), c_ssq.at(NT - 1)]
                    first_lcopy = False
                if par == 0:
                    i_oe = op("vector", lambda e, s2=slot, o2=ocol: e.tensor_copy(
                        big2[0:64, o2:o2 + 128], pb[s2][0:64, 0:128]), waits=ow, inc=c_oevac)
                    i_lc = op("scalar", lambda e, s2=slot, q2=qi: e.copy(
                        l_rows[q2 // 8][64:65, (q2 % 8) * 128:(q2 % 8 + 1) * 128],
                        pb[s2][64:65, 0:128]),
                        waits=lw, inc=c_lcopy)
                else:
                    i_oe = op("vector", lambda e, s2=slot, o2=ocol: e.tensor_copy(
                        big2[64:128, o2:o2 + 128], pb[s2][64:128, 0:128]), waits=ow, inc=c_oevac)
                    i_lc = op("scalar", lambda e, s2=slot, q2=qi: e.copy(
                        l_rows[q2 // 8][0:1, (q2 % 8) * 128:(q2 % 8 + 1) * 128],
                        pb[s2][0:1, 128:256]),
                        waits=lw, inc=c_lcopy)
                banks[slot].consumed(c_oevac, i_oe)
                banks[slot].consumed(c_lcopy, i_lc)
        for half2 in range(2):
            op("vector", lambda e, r2=row, h3=half2: e.reciprocal(
                l_rows[h3][r2:r2 + 1, :], l_rows[h3][r2:r2 + 1, :]),
               waits=[c_lcopy.last()], inc=c_recip)
        rc_base = c_recip.n - 2
        last_lrow[par] = []
        for qB in range(4):
            bw = banks[2].acquire()
            i_bc = op("tensor", MM(pb[2][off:off + 64, 0:512], ones_f[row:row + 1, 0:64],
                                   l_rows[qB // 2][row:row + 1, (qB % 2) * 512:(qB % 2 + 1) * 512],
                                   True, True),
                      waits=[c_recip.at(rc_base + qB // 2)] + bw, inc=c_bcmm)
            last_lrow[par].append(c_bcmm.at(i_bc))
            bew = [c_bcmm.at(i_bc)]
            if h < 2 and qB == 0:
                bew += [c_mm2b.now(), c_mmvg.now()]
            i_be = op("scalar", lambda e, o2=off: e.copy(
                linv_bc[o2:o2 + 64, 0:512], pb[2][o2:o2 + 64, 0:512]),
                waits=bew, inc=c_bce)
            banks[2].consumed(c_bce, i_be)
            op("gpsimd", lambda e, o2=off, c2=chk, q2=qB: e.tensor_tensor(
                big2[o2:o2 + 64, c2 * 2048 + q2 * 512: c2 * 2048 + (q2 + 1) * 512],
                big2[o2:o2 + 64, c2 * 2048 + q2 * 512: c2 * 2048 + (q2 + 1) * 512],
                linv_bc[o2:o2 + 64, 0:512], AO.mult),
               waits=[c_bce.at(i_be), c_oevac.last()], inc=c_onorm)

    if debug:
        op("sync", lambda e: e.dma_start(out=d_ofm, in_=big2), waits=[c_onorm.now()], inc=c_out)
    if stage <= 3:
        with nc.Block() as block:
            prog.realize(block, [c_onorm.now(), c_out.now(), c_xnqw.now(), c_wqw.now()])
        es.close()
        return nc

    for kk in range(NKK):
        qd = "sync"
        op(qd, lambda e, k2=kk: e.dma_start(
            out=v_tm[:, k2 * 1024:(k2 + 1) * 1024],
            in_=s_wo[:, k2 * 128:(k2 + 1) * 128], transpose=True),
           waits=[c_wcast.at(31), c_pv.now()], inc=c_wot)
    for half in range(2):
        op("sync", lambda e, hh=half: e.dma_start(
            out=b_bias[hh], in_=b_o[:, hh * 512:(hh + 1) * 512]),
           waits=[c_mmvg.now(), c_mm2b.now()], inc=c_bias)
    for mt in range(NT):
        b = mt % 2
        xw = [c_acce.at((mt - 2) * 2 + 1)] if mt >= 2 else \
            [c_xn.at(NT - 1), c_ssq.at(NT - 1), c_bcmm.now(), c_recip.now()]
        op("sync", lambda e, m3=mt, bb=b: e.dma_start(
            out=x_t[bb], in_=x_h[m3 * 128:(m3 + 1) * 128, :]), waits=xw, inc=c_x2)
        for half in range(2):
            bi = (mt * 2 + half) % 4
            mms = [(pb[bi][0:128, 0:512],
                    big2[:, kk * 2048 + mt * 128: kk * 2048 + (mt + 1) * 128],
                    v_tm[:, kk * 1024 + half * 512: kk * 1024 + (half + 1) * 512])
                   for kk in range(NKK)]
            mms.append((pb[bi][0:128, 0:512], ones_f[0:1, 0:128], b_bias[half][0:1, 0:512]))
            i_mm = mm_group(bi, mms, waits=[c_wot.at(7), c_bias.at(6 + half), c_onorm.now()], inc=c_mm4)
            evw = [c_mm4.at(i_mm), c_x2.at(mt)]
            if mt >= 2:
                evw.append(c_accw.at(mt - 2))
            else:
                evw += [c_t1.at(NT - 1), c_t2.at(NT - 1), c_y8.at(NT - 1), c_dq.at(NT - 1)]
            i_ev = op("vector", lambda e, b2=bi, bb=b, h2=half: e.tensor_tensor(
                xn_t[bb][:, h2 * 512:(h2 + 1) * 512], pb[b2][0:128, 0:512],
                x_t[bb][:, h2 * 512:(h2 + 1) * 512], AO.add),
                waits=evw, inc=c_acce)
            banks[bi].consumed(c_acce, i_ev)
        op("sync", lambda e, m3=mt, bb=b: e.dma_start(
            out=s_acc[m3 * 128:(m3 + 1) * 128, :], in_=xn_t[bb]),
           waits=[c_acce.at(mt * 2 + 1)], inc=c_accw)

    if stage <= 4:
        with nc.Block() as block:
            prog.realize(block, [c_accw.now(), c_xnqw.now(), c_wqw.now()])
        es.close()
        return nc

    for tokB in range(4):
        for kk in range(NKK):
            op("sync", lambda e, tb=tokB, k2=kk: e.dma_start(
                out=big2[:, k2 * 2048 + tb * 512: k2 * 2048 + (tb + 1) * 512],
                in_=s_xnq[tb * 512:(tb + 1) * 512, k2 * 128:(k2 + 1) * 128], transpose=True),
               waits=[c_xnqw.at(15), c_mm4.now()], inc=c_xnqfm)
    op("vector", lambda e: e.tensor_scalar(
        tcol[:, 32:48], tcol[:, 0:16], thr_col[:, 3:4], None, AO.mult),
       waits=[c_deqx.at(NT - 1), c_thr.now()], inc=c_sgu)
    op("vector", lambda e: e.tensor_scalar(
        tcol[:, 48:64], tcol[:, 0:16], thr_col[:, 7:8], None, AO.mult), inc=c_sgu)
    op("vector", lambda e: e.memset(tcol[:, 16:32], 0.0), inc=c_rmax)
    rmax0 = c_rmax.n - 1

    ngu = 0
    for ns in range(8):
        gsl = (ns % 2) * 4096
        for kk in range(NKK):
            gw2 = [c_wqw.at(wqw_end[0])]
            if ns >= 2:
                gw2.append(c_mmg5.at((ns - 2) * 16 + 15))
            op("sync", lambda e, n2=ns, k2=kk, s2=gsl: e.dma_start(
                out=wstr[:, s2 + k2 * 512: s2 + (k2 + 1) * 512],
                in_=s_wg[n2 * 512:(n2 + 1) * 512, k2 * 128:(k2 + 1) * 128], transpose=True),
               waits=gw2 + ([c_mmS.now(), c_mmg.now(), c_pv.now(), c_onorm.now()]
                            if ns < 2 else []),
               inc=c_wgu)
        for kk in range(NKK):
            uw = [c_wqw.at(wqw_end[1])]
            if ns >= 1:
                uw.append(c_mmu5.at((ns - 1) * 16 + 15))
            else:
                uw += [c_mm2b.now(), c_mmvg.now(), c_bce.now()]
            op("sync", lambda e, n2=ns, k2=kk: e.dma_start(
                out=wstr[:, 8192 + k2 * 512: 8192 + (k2 + 1) * 512],
                in_=s_wu[n2 * 512:(n2 + 1) * 512, k2 * 128:(k2 + 1) * 128], transpose=True),
               waits=uw, inc=c_wguu)
        wg_rdy = c_wgu.at(ns * 8 + 7)
        wu_rdy = c_wguu.at(ns * 8 + 7)
        for mt in range(NT):
            big = gu_big[(ngu // 2) % 2]
            bcol = (mt % 2) * 512
            bg, bu = (mt % 2) * 2, (mt % 2) * 2 + 1
            mmsg = [(pb[bg][0:128, 0:512],
                     big2[:, kk * 2048 + mt * 128: kk * 2048 + (mt + 1) * 128],
                     wstr[:, gsl + kk * 512: gsl + (kk + 1) * 512]) for kk in range(NKK)]
            i_mmg = mm_group(bg, mmsg, waits=[wg_rdy, c_xnqfm.at(31)], inc=c_mmg5)
            gev = guev[(mt % 2) * 2]
            gevw = [c_mmg5.at(i_mmg), c_sgu.at(0)]
            gevw += [c_gum.at(c_gum.n - 2)] if c_gum.n >= 2 else \
                [c_xr.at(NT - 1), c_mwr.at(H - 1), c_t2.at(NT - 1)]
            i_ge = op("scalar", lambda e, b2=bg, m3=mt, g2=gev: e.activation(
                g2, pb[b2][0:128, 0:512], AF.Silu, scale=tcol[:, 32 + m3:33 + m3]),
                waits=gevw, inc=c_ge)
            banks[bg].consumed(c_ge, i_ge)
            mmsu = [(pb[bu][0:128, 0:512],
                     big2[:, kk * 2048 + mt * 128: kk * 2048 + (mt + 1) * 128],
                     wstr[:, 8192 + kk * 512: 8192 + (kk + 1) * 512]) for kk in range(NKK)]
            i_mmu = mm_group(bu, mmsu, waits=[wu_rdy, c_xnqfm.at(31)], inc=c_mmu5)
            uev = guev[(mt % 2) * 2 + 1]
            uevw = [c_mmu5.at(i_mmu), c_sgu.at(1)]
            uevw += [c_gum.at(c_gum.n - 2)] if c_gum.n >= 2 else \
                [c_xr.at(NT - 1), c_mwr.at(H - 1), c_t2.at(NT - 1)]
            i_ue = op("scalar", lambda e, b2=bu, m3=mt, u2=uev: e.activation(
                u2, pb[b2][0:128, 0:512], AF.Copy, scale=tcol[:, 48 + m3:49 + m3]),
                waits=uevw, inc=c_ue)
            banks[bu].consumed(c_ue, i_ue)
            guw = [c_guw.at((ngu - 4) // 2), c_rmax.at(rmax0 + ngu - 3)] if ngu >= 4 else \
                ([c_wq.last(), c_wc1.last()] if ngu < 2 else [])
            i_gm = op("gpsimd", lambda e, g2=gev, u2=uev, b3=big, c2=bcol: e.tensor_tensor(
                b3[:, c2:c2 + 512], g2, u2, AO.mult),
                waits=[c_ge.at(i_ge), c_ue.at(i_ue)] + guw, inc=c_gum)
            op("vector", lambda e, b3=big, c2=bcol: e.reduce_max(
                tcol[:, 66:67], b3[:, c2:c2 + 512], axis=AX, apply_absolute_value=True),
               waits=[c_gum.at(i_gm), c_rmax.at(rmax0)])
            op("vector", lambda e, m3=mt: e.tensor_tensor(
                tcol[:, 16 + m3:17 + m3], tcol[:, 16 + m3:17 + m3], tcol[:, 66:67], AO.max),
               inc=c_rmax)
            if mt % 2 == 1:
                op("gpsimd", lambda e, b3=big, n2=ns, m3=mt: e.dma_start(
                    out=s_gu.rearrange("(m p) c -> p m c", p=128)[:, m3 - 1:m3 + 1,
                                                                 n2 * 512:(n2 + 1) * 512],
                    in_=b3.rearrange("p (m c) -> p m c", c=512)),
                   waits=[c_gum.at(c_gum.n - 1)], inc=c_guw)
            ngu += 1

    for mt in range(NT):
        gu_rd = k_fm[:, (mt % 2) * 4096: (mt % 2) * 4096 + 4096]
        rw = [c_guw.at(c_guw.n - 1)]
        if mt >= 2:
            rw.append(c_qt.at((mt - 2) * 4 + 3))
        else:
            rw += [c_mmS.now(), c_pv.now(), c_mmax.now()]
        op("sync", lambda e, m3=mt, g2=gu_rd: e.dma_start(
            out=g2, in_=s_gu[m3 * 128:(m3 + 1) * 128, :]), waits=rw, inc=c_gurd)
        scw = [c_rmax.last()]
        if mt >= 1:
            scw.append(c_qt.at((mt - 1) * 4 + 3))
        op("vector", lambda e, m3=mt: e.tensor_scalar(
            tcol[:, 66:67], tcol[:, 16 + m3:17 + m3], 1e-5, None, AO.max), waits=scw)
        op("vector", lambda e: e.reciprocal(tcol[:, 67:68], tcol[:, 66:67]))
        op("vector", lambda e: e.tensor_scalar(tcol[:, 68:69], tcol[:, 67:68], 127.0, None, AO.mult))
        op("vector", lambda e, m3=mt: e.scalar_tensor_tensor(
            tcol[:, 80 + m3:81 + m3], tcol[:, 66:67], 1.0 / 127.0,
            thr_col[:, 11:12], AO.mult, AO.mult),
           waits=[c_thr.now()], inc=c_sgu)
        for qq in range(4):
            qi4 = mt * 4 + qq
            gslice = gu_rd[:, qq * 1024:(qq + 1) * 1024]
            tw = [c_gurd.at(mt), c_sgu.last()]
            if qi4 >= 1:
                tw.append(c_d2.at(qi4 - 1))
            else:
                tw += [c_gum.now(), c_guw.at(c_guw.n - 1)]
            i_qt = op("scalar", lambda e, g2=gslice: e.activation(
                gub2_t[:, 0:1024], g2, AF.Copy, scale=tcol[:, 68:69]), waits=tw, inc=c_qt)
            yw = [c_qt.at(i_qt)]
            yw += [c_y8c.at(qi4 - 1)] if qi4 >= 1 else [c_y8b1.at(NT - 1)]
            i_y8 = op("scalar", lambda e: e.activation(y8_t, gub2_t[:, 0:1024], AF.Copy),
                      waits=yw, inc=c_y8b)
            cw2 = [c_y8b.at(i_y8)]
            cw2 += [c_guq.at(qi4 - 1)] if qi4 >= 1 else [c_wq.last(), c_wc1.last()]
            i_yc = op("vector", lambda e: e.tensor_copy(c1_t[:, 1024:2048], y8_t),
                      waits=cw2, inc=c_y8c)
            dw = [c_y8c.at(i_yc)]
            if qi4 == 0:
                dw.append(c_guw.at(c_guw.n - 1))
            i_d2 = op("vector", lambda e: e.tensor_tensor(
                wprep_t[:, 0:1024], gub2_t[:, 0:1024], c1_t[:, 1024:2048], AO.subtract),
                waits=dw, inc=c_d2)
            i_c1 = op("vector", lambda e: e.tensor_scalar(
                m2_t[:, 1024:2048], wprep_t[:, 0:1024], 0.5, None, AO.is_ge),
                waits=[c_d2.at(i_d2)], inc=c_c1b)
            m2w = [c_c1b.at(i_c1)]
            m2w += [c_guq.at(qi4 - 1)] if qi4 >= 1 else [c_xnq.at(NT - 1), c_m2.at(NT - 1)]
            i_m2 = op("vector", lambda e: e.scalar_tensor_tensor(
                m2_t[:, 0:1024], wprep_t[:, 0:1024], -0.5, m2_t[:, 1024:2048],
                AO.is_le, AO.subtract),
                waits=m2w, inc=c_m2b)
            guq_w = k_fm[:, 8192 + (qi4 % 2) * 1024: 8192 + (qi4 % 2) * 1024 + 1024]
            gqw = [c_m2b.at(i_m2)]
            gqw += [c_guqw.at(qi4 - 2)] if qi4 >= 2 else [c_mmS.now(), c_pv.now()]
            i_gq = op("gpsimd", lambda e, g3=guq_w: e.tensor_tensor(
                g3, c1_t[:, 1024:2048], m2_t[:, 0:1024], AO.subtract), waits=gqw, inc=c_guq)
            op("scalar", lambda e, m3=mt, q2=qq, g3=guq_w: e.dma_start(
                out=s_guq[m3 * 128:(m3 + 1) * 128, q2 * 1024:(q2 + 1) * 1024], in_=g3),
               waits=[c_guq.at(i_gq)], inc=c_guqw)

    for qB in range(4):
        for kk in range(32):
            qd = "sync"
            fw = [c_guqw.at(qB * 16 + 15)]
            if qB >= 1:
                fw.append(c_dnmm.at(qB - 1))
            else:
                fw += [c_mmax.now(), c_mmS.now(), c_pv.now(), c_qke.now()]
            op(qd, lambda e, q2=qB, k2=kk: e.dma_start(
                out=q_fm[:, k2 * 512:(k2 + 1) * 512],
                in_=s_guq[q2 * 512:(q2 + 1) * 512, k2 * 128:(k2 + 1) * 128], transpose=True),
               waits=fw, inc=c_guqfm)
        base_dn = c_dnmm.n
        for kk in range(32):
            wsl = 8192 + (kk % 2) * 1024
            ww = []
            if kk >= 2:
                ww.append(c_dnk.at(qB * 31 + kk - 2))
            elif qB >= 1:
                ww.append(c_dnk.at((qB - 1) * 31 + 30) if kk == 0 else c_dnmm.at(qB - 1))
            else:
                ww += [c_pv.now()]
            ww.append(c_wqw.at(wqw_end[2]))
            i_wd = op("sync",
                      lambda e, k2=kk, w2=wsl: e.dma_start(
                          out=v_tm[:, w2:w2 + 1024],
                          in_=s_wd[:, k2 * 128:(k2 + 1) * 128], transpose=True),
                      waits=ww, inc=c_wdt)
            for mt4 in range(4):
                for hf in range(2):
                    bi = mt4 * 2 + hf
                    bw = banks[bi].acquire() if kk == 0 else []
                    last_of_kk = (mt4 == 3 and hf == 1)
                    op("tensor", MM(pb[bi][0:128, 0:512],
                                    q_fm[:, kk * 512 + mt4 * 128: kk * 512 + (mt4 + 1) * 128],
                                    v_tm[:, wsl + hf * 512: wsl + (hf + 1) * 512],
                                    kk == 0, kk == 31),
                       waits=([c_guqfm.at(qB * 32 + kk), c_wdt.at(i_wd)]
                              if (mt4 == 0 and hf == 0) else []) + bw,
                       inc=(c_dnk if (last_of_kk and kk < 31) else
                            (c_dnmm if (last_of_kk and kk == 31) else None)))
        i_dn_all = c_dnmm.n - 1
        for mt4 in range(4):
            mt = qB * 4 + mt4
            b = mt4 % 2
            aw = [c_accw.at(mt)]
            if mt >= 2:
                aw.append(c_fin.at((mt - 2) * 2 + 1))
            else:
                aw.append(c_acce.at(2 * NT - 1))
                aw.append(c_x2.at(NT - 1))
            op("sync", lambda e, m3=mt, bb=b: e.dma_start(
                out=x_t[bb], in_=s_acc[m3 * 128:(m3 + 1) * 128, :]), waits=aw, inc=c_acr)
            for hf in range(2):
                bi = mt4 * 2 + hf
                fw2 = [c_dnmm.at(i_dn_all), c_acr.at(mt)]
                if mt >= 2:
                    fw2.append(c_out.at(c_out.n - 2))
                else:
                    fw2.append(c_accw.at(NT - 1))
                i_fin = op("vector", lambda e, b2=bi, bb=b, m3=mt, h2=hf: e.scalar_tensor_tensor(
                    xn_t[bb][:, h2 * 512:(h2 + 1) * 512], pb[b2][0:128, 0:512],
                    tcol[:, 80 + m3:81 + m3],
                    x_t[bb][:, h2 * 512:(h2 + 1) * 512], AO.mult, AO.add),
                    waits=fw2, inc=c_fin)
                banks[bi].consumed(c_fin, i_fin)
            op("sync", lambda e, m3=mt, bb=b: e.dma_start(
                out=y_h[m3 * 128:(m3 + 1) * 128, :], in_=xn_t[bb]),
               waits=[c_fin.at(mt * 2 + 1)], inc=c_out)

    if debug:
        op("sync", lambda e: e.dma_start(out=d_tc, in_=tcol), waits=[c_sgu.now()], inc=c_out)
    with nc.Block() as block:
        prog.realize(block, [c_out.now()])
    es.close()
    return nc


def kernel(L0_attn_x, L0_attn_in_proj_w, L0_attn_in_proj_b, L0_attn_out_w,
           L0_attn_out_b, L0_ffn_w_gate, L0_ffn_w_up, L0_ffn_w_down):
    x = np.ascontiguousarray(np.asarray(L0_attn_x, dtype=np.float32))
    ctab, mask_e, mask_o, identm8 = host_consts()
    base = dict(
        w_in=np.ascontiguousarray(np.asarray(L0_attn_in_proj_w, np.float32)),
        b_in=np.ascontiguousarray(np.asarray(L0_attn_in_proj_b, np.float32)).reshape(1, D3),
        w_o=np.ascontiguousarray(np.asarray(L0_attn_out_w, np.float32)),
        b_o=np.ascontiguousarray(np.asarray(L0_attn_out_b, np.float32)).reshape(1, D),
        w_g=np.ascontiguousarray(np.asarray(L0_ffn_w_gate, np.float32)),
        w_u=np.ascontiguousarray(np.asarray(L0_ffn_w_up, np.float32)),
        w_d=np.ascontiguousarray(np.asarray(L0_ffn_w_down, np.float32)),
        ctab=ctab, mask_e=mask_e, mask_o=mask_o, identm8=identm8,
    )
    nc = build("full")
    in_maps = [dict(base, x=np.ascontiguousarray(x[b])) for b in range(8)]
    res = run_bass_kernel_spmd(nc, in_maps, core_ids=list(range(8)))
    out = np.stack([np.asarray(r["y"], np.float32) for r in res.results], axis=0)
    return out
